# version 03
# baseline (speedup 1.0000x reference)
import numpy as np
import ml_dtypes
from contextlib import ExitStack

import concourse.bass as bass
import concourse.mybir as mybir
from concourse.bass_utils import run_bass_kernel_spmd

F32 = mybir.dt.float32
BF16 = mybir.dt.bfloat16
U8 = mybir.dt.uint8
AF = mybir.ActivationFunctionType
ALU = mybir.AluOpType
AX = mybir.AxisListType
BF = ml_dtypes.bfloat16

NCORES = 8
B, S, D = 4, 4096, 4096
T = 2048
NTT = T // 128
HC = D // 128


def rope_tm_tables(theta, positions, dhead=128, rep=1):
    half = dhead // 2
    inv = 1.0 / theta ** (np.arange(0, half, dtype=np.float64) * 2.0 / dhead)
    fr = np.asarray(positions, dtype=np.float64)[:, None] * inv[None, :]
    c = np.cos(fr).astype(np.float32)
    s = np.sin(fr).astype(np.float32)
    return (np.ascontiguousarray(np.tile(c, (1, rep))),
            np.ascontiguousarray(np.tile(s, (1, rep))))

def l0_allowed_kb(qb):
    return [kb for kb in range(qb + 1) if qb - kb < 16 or (kb + 1) % 8 == 0]


NS_G = [14, 15, 16, 17]
NS_OFF = [0, 14, 29, 45]
NS_TOT = 62

def l0_tile_plan():
    plans = {}
    for g in range(4):
        for h in range(2):
            q0 = h * 32 + g * 8
            allowed = set()
            for qb in range(q0, q0 + 8):
                allowed.update(l0_allowed_kb(qb))
            slots = []
            for i in range(4):
                slots.append(dict(j=q0 // 2 + i, w0=1.0, w1=1.0, v0=0.0, v1=0.0))
            e0 = q0 - 16
            for i in range(4):
                if e0 >= 0:
                    j = e0 // 2 + i
                    kb0, kb1 = 2 * j, 2 * j + 1
                    slots.append(dict(j=j, w0=1.0, w1=1.0,
                                      v0=float((kb0 + 1) % 8 == 0),
                                      v1=float((kb1 + 1) % 8 == 0)))
                else:
                    slots.append(dict(j=0, w0=0.0, w1=0.0, v0=0.0, v1=0.0, dead=True))
            covered = {s["j"] for s in slots if not s.get("dead")}
            clean = sorted({kb // 2 for kb in allowed} - covered)
            for j in clean:
                slots.append(dict(j=j, w0=float(2 * j in allowed),
                                  w1=float(2 * j + 1 in allowed), v0=0.0, v1=0.0))
            while len(slots) < NS_G[g]:
                slots.append(dict(j=0, w0=0.0, w1=0.0, v0=0.0, v1=0.0, dead=True))
            assert len(slots) == NS_G[g], (h, g, len(slots))
            plans[(h, g)] = slots
    return plans

L0_PLANS = l0_tile_plan()


def new_nc():
    return bass.Bass("TRN2", target_bir_lowering=False, debug=False)

def build_l0_qkv(stage="full"):
    nc = new_nc()
    x = nc.dram_tensor("x", [T, D], F32, kind="ExternalInput").ap()
    wt = nc.dram_tensor("wt", [D, 6144], F32, kind="ExternalInput").ap()
    brow_in = nc.dram_tensor("brow", [1, 6144], F32, kind="ExternalInput").ap()
    cos_in = nc.dram_tensor("cos", [T, 64], F32, kind="ExternalInput").ap()
    sin_in = nc.dram_tensor("sin", [T, 64], F32, kind="ExternalInput").ap()
    qr = nc.dram_tensor("qr", [T, 4096], BF16, kind="ExternalOutput").ap()
    kr = nc.dram_tensor("kr", [T, 1024], BF16, kind="ExternalOutput").ap()
    vr = nc.dram_tensor("vr", [T, 1024], BF16, kind="ExternalOutput").ap()
    xn_out = nc.dram_tensor("xn", [T, D], BF16, kind="ExternalOutput").ap()
    wt_bf = nc.dram_tensor("wt_bf", [D, 6144], BF16, kind="Internal").ap()
    ones_h = nc.inline_tensor(np.ones((1, 128), dtype=BF), name="ones_row")

    units = []
    for j in range(2):
        for uc in range(12):
            for ti in range(8):
                units.append(dict(j=j, uc=uc, ti=ti, g=j * 12 + uc,
                                  kind="q" if uc < 8 else "kv",
                                  trow=j * 1024 + ti * 128))
    NU = len(units)

    with ExitStack() as ctx:
        e = ctx.enter_context
        xt = [e(nc.sbuf_tensor(f"xt{i}", [128, D], F32)) for i in range(2)]
        bns = e(nc.sbuf_tensor("bns", [128, 48], F32))
        mv = [e(nc.sbuf_tensor(f"mv{i}", [128, 2], F32)) for i in range(2)]
        sc = e(nc.sbuf_tensor("sc", [128, 4], F32))
        consts = e(nc.sbuf_tensor("consts", [128, 3], F32))
        xn_t = e(nc.sbuf_tensor("xnt", [128, D], BF16))
        xnf = e(nc.sbuf_tensor("xnf", [128, HC * 1024], BF16))
        wsl = [e(nc.sbuf_tensor(f"wsl{i}", [128, HC * 512], BF16)) for i in range(2)]
        browb = e(nc.sbuf_tensor("browb", [1, 6144], BF16))
        onesr = e(nc.sbuf_tensor("onesr", [1, 128], BF16))
        cs = e(nc.sbuf_tensor("cs", [128, NTT * 64], F32))
        sn = e(nc.sbuf_tensor("sn", [128, NTT * 64], F32))
        pb = [e(nc.sbuf_tensor(f"pb{i}", [128, 512], F32)) for i in range(2)]
        ro = [e(nc.sbuf_tensor(f"ro{i}", [128, 512], BF16)) for i in range(2)]
        vb = [e(nc.sbuf_tensor(f"vb{i}", [128, 256], BF16)) for i in range(2)]
        tmA = e(nc.sbuf_tensor("tmA", [128, 512], F32))
        tmB = e(nc.sbuf_tensor("tmB", [128, 512], F32))
        ps = [e(nc.psum_tensor(f"ps{i}", [128, 512], F32)) for i in range(4)]

        ldc = e(nc.semaphore("ldc"))
        ldx = [e(nc.semaphore(f"ldx{i}")) for i in range(2)]
        wc = e(nc.semaphore("wc"))
        wcb = e(nc.semaphore("wcb"))
        wl = [e(nc.semaphore(f"wl{i}")) for i in range(2)]
        xf = e(nc.semaphore("xf"))
        vst = e(nc.semaphore("vst"))
        sst = e(nc.semaphore("sst"))
        mm = e(nc.semaphore("mm"))
        evs = e(nc.semaphore("evs"))
        rpv = e(nc.semaphore("rpv"))
        rpp = e(nc.semaphore("rpp"))
        sxn = [e(nc.semaphore(f"sxn{i}")) for i in range(2)]
        sop = [e(nc.semaphore(f"sop{i}")) for i in range(2)]

        WLP = {}
        _a = [0, 0]
        for _g in range(24):
            _uc = _g % 12
            _a[_g % 2] += 16 if _uc < 8 else 32
            WLP[_g] = _a[_g % 2]
        SOP = {}
        _b = [0, 0]
        for _i, _u in enumerate(units):
            _b[_i % 2] += 16 if _u["kind"] == "q" else 32
            SOP[_i] = _b[_i % 2]
        ev_per_unit = [2 if u["kind"] == "kv" else 1 for u in units]
        ev_pfx = [0]
        for k in ev_per_unit:
            ev_pfx.append(ev_pfx[-1] + k)

        def emit_rope(eng, pbv, rov, tm, nh, cslc, snlc, half, sem):
            w = 64 * nh
            eng.wait_ge(sem[0], sem[1])
            lo = pbv[:, :, 0:64]; hi = pbv[:, :, 64:128]
            t1 = tm.ap()[:, 0:w].rearrange("p (h d) -> p h d", d=64)
            t2 = tm.ap()[:, w:2 * w].rearrange("p (h d) -> p h d", d=64)
            first, second = (lo, hi) if half == "lo" else (hi, lo)
            aa, cc = bass.broadcast_tensor_aps(first, cslc)
            eng.tensor_tensor(t1, aa, cc, ALU.mult)
            bb, ss = bass.broadcast_tensor_aps(second, snlc)
            i1 = eng.tensor_tensor(t2, bb, ss, ALU.mult)
            i1.then_inc(sem[0], 1); sem[1] += 1
            eng.wait_ge(sem[0], sem[1])
            out = rov[:, :, 0:64] if half == "lo" else rov[:, :, 64:128]
            i2 = eng.tensor_tensor(out, t1, t2, ALU.subtract if half == "lo" else ALU.add)
            i2.then_inc(sem[0], 1); sem[1] += 1

        with nc.Block() as blk:
            @blk.gpsimd
            def _(g):
                if stage != "lnnocast":
                    g.dma_start(out=browb[0:1, :], in_=brow_in[0:1, :]).then_inc(wcb, 16)
                    for c in range(8):
                        g.dma_start(out=wt_bf[c * 512:(c + 1) * 512, :],
                                    in_=wt[c * 512:(c + 1) * 512, :]).then_inc(wc, 16)
                if stage != "full":
                    return
                g.wait_ge(ldc, 48)
                semp = [rpp, 0]
                for i, u in enumerate(units):
                    par = i % 2
                    tcos = u["j"] * 8 + u["ti"]
                    cslc = cs.ap()[:, tcos * 64:(tcos + 1) * 64].rearrange("p (r d) -> p r d", r=1)
                    snlc = sn.ap()[:, tcos * 64:(tcos + 1) * 64].rearrange("p (r d) -> p r d", r=1)
                    g.wait_ge(evs, ev_pfx[i] + 1)
                    if i >= 2:
                        g.wait_ge(sop[par], SOP[i - 2])
                    if u["kind"] == "q":
                        pbv = pb[par].ap()[:, 0:512].rearrange("p (h d) -> p h d", d=128)
                        rov = ro[par].ap()[:, 0:512].rearrange("p (h d) -> p h d", d=128)
                        emit_rope(g, pbv, rov, tmB, 4, cslc, snlc, "hi", semp)
                    else:
                        pbv = pb[par].ap()[:, 0:512].rearrange("p (h d) -> p h d", d=256)[:, :, 0:128]
                        rov = ro[par].ap()[:, 0:256].rearrange("p (h d) -> p h d", d=128)
                        emit_rope(g, pbv, rov, tmB, 2, cslc, snlc, "hi", semp)

            @blk.vector
            def _(v):
                v.memset(consts[:, 0:1], 1e-5)
                v.memset(consts[:, 1:2], -0.5)
                v.memset(consts[:, 2:3], -1.0).then_inc(vst, 1)
                for i in range(NTT):
                    v.wait_ge(ldx[i % 2], 16 * (i // 2 + 1))
                    if i >= 1:
                        v.wait_ge(vst, 1 + 2 * i)
                    if i >= 2:
                        v.wait_ge(sst, 5 * (i - 2) + 3)
                    for c in range(8):
                        ii = v.bn_stats(bns[:, 6 * c:6 * c + 6], xt[i % 2][:, 512 * c:512 * (c + 1)])
                    ii.then_inc(vst, 1)
                    v.wait_ge(vst, 1 + 2 * i + 1)
                    v.bn_aggr(mv[i % 2][:, :], bns[:, :]).then_inc(vst, 1)
                if stage != "full":
                    return
                v.wait_ge(ldc, 48)
                semv = [rpv, 0]
                for i, u in enumerate(units):
                    par = i % 2
                    tcos = u["j"] * 8 + u["ti"]
                    cslc = cs.ap()[:, tcos * 64:(tcos + 1) * 64].rearrange("p (r d) -> p r d", r=1)
                    snlc = sn.ap()[:, tcos * 64:(tcos + 1) * 64].rearrange("p (r d) -> p r d", r=1)
                    v.wait_ge(evs, ev_pfx[i] + 1)
                    if i >= 2:
                        v.wait_ge(sop[par], SOP[i - 2])
                    if u["kind"] == "q":
                        pbv = pb[par].ap()[:, 0:512].rearrange("p (h d) -> p h d", d=128)
                        rov = ro[par].ap()[:, 0:512].rearrange("p (h d) -> p h d", d=128)
                        emit_rope(v, pbv, rov, tmA, 4, cslc, snlc, "lo", semv)
                    else:
                        pbv = pb[par].ap()[:, 0:512].rearrange("p (h d) -> p h d", d=256)[:, :, 0:128]
                        rov = ro[par].ap()[:, 0:256].rearrange("p (h d) -> p h d", d=128)
                        emit_rope(v, pbv, rov, tmA, 2, cslc, snlc, "lo", semv)

            @blk.scalar
            def _(s):
                s.wait_ge(vst, 1)
                for i in range(NTT):
                    s.wait_ge(vst, 1 + 2 * (i + 1))
                    if i >= 1:
                        jj, kk = divmod(i - 1, 8)
                        s.wait_ge(sxn[jj], 16 * (kk + 1))
                    s.activation(sc[:, 0:1], mv[i % 2][:, 1:2], AF.Ln, bias=consts[:, 0:1]).then_inc(sst, 1)
                    s.wait_ge(sst, 5 * i + 1)
                    s.activation(sc[:, 1:2], sc[:, 0:1], AF.Exp, scale=consts[:, 1:2]).then_inc(sst, 1)
                    s.wait_ge(sst, 5 * i + 2)
                    s.activation(sc[:, 2:3], mv[i % 2][:, 0:1], AF.Identity, scale=sc[:, 1:2]).then_inc(sst, 1)
                    s.wait_ge(sst, 5 * i + 3)
                    s.activation(sc[:, 3:4], sc[:, 2:3], AF.Identity, scale=consts[:, 2:3]).then_inc(sst, 1)
                    s.wait_ge(sst, 5 * i + 4)
                    s.activation(xn_t[:, :], xt[i % 2][:, :], AF.Identity,
                                 scale=sc[:, 1:2], bias=sc[:, 3:4]).then_inc(sst, 1)
                if stage != "full":
                    return
                s.wait_ge(sxn[0], 16 * 8)
                for c in range(HC):
                    s.dma_start_transpose(xnf[:, c * 1024:(c + 1) * 1024],
                                          xn_out[0:1024, c * 128:(c + 1) * 128]).then_inc(xf, 16)
                for i, u in enumerate(units):
                    if i == 96:
                        s.wait_ge(sxn[1], 16 * 8)
                        s.wait_ge(mm, 96)
                        for c in range(HC):
                            s.dma_start_transpose(xnf[:, c * 1024:(c + 1) * 1024],
                                                  xn_out[1024:2048, c * 128:(c + 1) * 128]).then_inc(xf, 16)
                    s.wait_ge(mm, i + 1)
                    bank = i % 4
                    par = i % 2
                    if i >= 2:
                        s.wait_ge(rpv, 2 * (i - 2) + 1)
                        s.wait_ge(rpp, 2 * (i - 2) + 1)
                    s.activation(pb[par][:, :], ps[bank][:, :], AF.Identity).then_inc(evs, 1)
                    if u["kind"] == "kv":
                        if i >= 2:
                            s.wait_ge(sop[par], SOP[i - 2])
                        s.activation(vb[par].ap()[:, 0:256].rearrange("p (h d) -> p h d", d=128),
                                     ps[bank].ap()[:, 0:512].rearrange("p (h d) -> p h d", d=256)[:, :, 128:256],
                                     AF.Identity).then_inc(evs, 1)

            @blk.tensor
            def _(t):
                if stage != "full":
                    return
                t.wait_ge(ldc, 48)
                t.wait_ge(wcb, 16)
                for i, u in enumerate(units):
                    j = u["j"]; g = u["g"]
                    t.wait_ge(wl[g % 2], WLP[g])
                    t.wait_ge(xf, 16 * HC * (j + 1))
                    if i >= 4:
                        t.wait_ge(evs, ev_pfx[i - 3])
                    bank = i % 4
                    for c in range(HC):
                        t.matmul(ps[bank][:, :],
                                 xnf[:, c * 1024 + u["ti"] * 128: c * 1024 + u["ti"] * 128 + 128],
                                 wsl[g % 2][:, c * 512:(c + 1) * 512],
                                 start=(c == 0), stop=False)
                    if u["uc"] < 8:
                        bslice = browb[0:1, u["uc"] * 768:u["uc"] * 768 + 512]
                    else:
                        m = u["uc"] - 8
                        bslice = browb.ap().rearrange("p (kv o) -> p kv o", o=768)[:, 2 * m:2 * m + 2, 512:768]
                    t.matmul(ps[bank][:, :], onesr[0:1, :], bslice,
                             start=False, stop=True).then_inc(mm, 1)

            @blk.sync
            def _(s2):
                s2.dma_start(out=onesr[0:1, :], in_=ones_h.ap()[0:1, :]).then_inc(ldc, 16)
                s2.dma_start(out=cs.ap().rearrange("p (n d) -> p n d", d=64),
                             in_=cos_in.rearrange("(n p) d -> p n d", p=128)).then_inc(ldc, 16)
                s2.dma_start(out=sn.ap().rearrange("p (n d) -> p n d", d=64),
                             in_=sin_in.rearrange("(n p) d -> p n d", p=128)).then_inc(ldc, 16)
                for i in range(NTT):
                    if i >= 2:
                        s2.wait_ge(sst, 5 * (i - 1))
                    s2.dma_start(out=xt[i % 2][:, :],
                                 in_=x[i * 128:(i + 1) * 128, :]).then_inc(ldx[i % 2], 16)
                    if i >= 1:
                        s2.wait_ge(sst, 5 * i)
                        s2.dma_start(out=xn_out[(i - 1) * 128:i * 128, :],
                                     in_=xn_t[:, :]).then_inc(sxn[(i - 1) // 8], 16)
                s2.wait_ge(sst, 5 * NTT)
                s2.dma_start(out=xn_out[(NTT - 1) * 128:NTT * 128, :],
                             in_=xn_t[:, :]).then_inc(sxn[1], 16)
                if stage != "full":
                    s2.wait_ge(sxn[0], 128)
                    s2.wait_ge(sxn[1], 128)
                    return
                s2.wait_ge(wc, 128)

                def slab_load(g):
                    uc2 = g % 12
                    if uc2 < 8:
                        s2.dma_start(out=wsl[g % 2].ap().rearrange("p (c o) -> p c o", o=512),
                                     in_=wt_bf[:, uc2 * 768:uc2 * 768 + 512]
                                     .rearrange("(c p) o -> p c o", p=128)).then_inc(wl[g % 2], 16)
                    else:
                        m2 = uc2 - 8
                        s2.dma_start(out=wsl[g % 2].ap().rearrange("p (c g o) -> p c g o", c=HC, g=2)[:, :, 0, :],
                                     in_=wt_bf[:, 1536 * m2 + 512:1536 * m2 + 512 + 256]
                                     .rearrange("(c p) o -> p c o", p=128)).then_inc(wl[g % 2], 16)
                        s2.dma_start(out=wsl[g % 2].ap().rearrange("p (c g o) -> p c g o", c=HC, g=2)[:, :, 1, :],
                                     in_=wt_bf[:, 1536 * m2 + 1280:1536 * m2 + 1280 + 256]
                                     .rearrange("(c p) o -> p c o", p=128)).then_inc(wl[g % 2], 16)

                slab_load(0)
                slab_load(1)
                for i, u in enumerate(units):
                    s2.wait_ge(rpv, 2 * (i + 1))
                    s2.wait_ge(rpp, 2 * (i + 1))
                    tr = u["trow"]
                    if u["kind"] == "q":
                        s2.dma_start(out=qr[tr:tr + 128, u["uc"] * 512:(u["uc"] + 1) * 512],
                                     in_=ro[i % 2][:, :]).then_inc(sop[i % 2], 16)
                    else:
                        s2.wait_ge(evs, ev_pfx[i + 1])
                        m = u["uc"] - 8
                        s2.dma_start(out=kr[tr:tr + 128, m * 256:(m + 1) * 256],
                                     in_=ro[i % 2][:, 0:256]).then_inc(sop[i % 2], 16)
                        s2.dma_start(out=vr[tr:tr + 128, m * 256:(m + 1) * 256],
                                     in_=vb[i % 2][:, 0:256]).then_inc(sop[i % 2], 16)
                    if i % 8 == 7 and i // 8 + 2 < 24:
                        s2.wait_ge(mm, 8 * (i // 8 + 1))
                        slab_load(i // 8 + 2)
                s2.wait_ge(sop[0], SOP[190])
                s2.wait_ge(sop[1], SOP[191])
                s2.wait_ge(sxn[0], 128)
                s2.wait_ge(sxn[1], 128)
    return nc


def build_l0_attn():
    nc = new_nc()
    qf = nc.dram_tensor("qf", [T, 4096], BF16, kind="ExternalInput").ap()
    kg_in = nc.dram_tensor("kg", [NS_TOT, 8, 128, 128], BF16, kind="ExternalInput").ap()
    vg_in = nc.dram_tensor("vg", [NS_TOT, 8, 128, 128], BF16, kind="ExternalInput").ap()
    biasc_in = nc.dram_tensor("biasc", [128, NS_TOT], F32, kind="ExternalInput").ap()
    emask_in = nc.dram_tensor("emaski", [128, 16 * 512], BF16, kind="ExternalInput").ap()
    x_in = nc.dram_tensor("x", [T, D], F32, kind="ExternalInput").ap()
    wdt = nc.dram_tensor("wdt", [D, D], F32, kind="ExternalInput").ap()
    bdr = nc.dram_tensor("bdr", [1, D], F32, kind="ExternalInput").ap()
    x1_out = nc.dram_tensor("x1", [T, D], F32, kind="ExternalOutput").ap()
    wd_bf = nc.dram_tensor("wd_bf", [D, D], BF16, kind="Internal").ap()
    ones_h = nc.inline_tensor(np.ones((1, 128), dtype=BF), name="ones_row")
    ones_c = nc.inline_tensor(np.ones((128, 1), dtype=BF), name="ones_col")
    dm = np.zeros((128, 4 * 512), dtype=BF)
    for i in range(4):
        tk = np.arange(128)[:, None] + 128 * i
        tq = np.arange(512)[None, :]
        dm[:, i * 512:(i + 1) * 512] = (tk <= tq).astype(BF)
    dmask_h = nc.inline_tensor(dm, name="dmask4")


    with ExitStack() as ctx:
        e = ctx.enter_context
        kg = [e(nc.sbuf_tensor(f"kg{i}", [128, 17 * 128], BF16)) for i in range(2)]
        vg = [e(nc.sbuf_tensor(f"vg{i}", [128, 17 * 128], BF16)) for i in range(2)]
        qsl = [e(nc.sbuf_tensor(f"qsl{i}", [128, 512], BF16)) for i in range(2)]
        expT = [e(nc.sbuf_tensor(f"expT{i}", [128, 512], BF16)) for i in range(4)]
        dmask = e(nc.sbuf_tensor("dmask", [128, 4 * 512], BF16))
        emask = e(nc.sbuf_tensor("emask", [128, 16 * 512], BF16))
        biasc = e(nc.sbuf_tensor("biasc_sb", [128, NS_TOT], F32))
        sm_sc = e(nc.sbuf_tensor("sm_sc", [128, 1], F32))
        onesr = e(nc.sbuf_tensor("onesr", [1, 128], BF16))
        onesc = e(nc.sbuf_tensor("onesc", [128, 1], BF16))
        rcp = e(nc.sbuf_tensor("rcp", [1, 512], F32))
        rcpb = e(nc.sbuf_tensor("rcpb", [1, 512], BF16))
        rb_sb = e(nc.sbuf_tensor("rb_sb", [128, 512], BF16))
        ctxe = [e(nc.sbuf_tensor(f"ctxe{i}", [128, 512], BF16)) for i in range(2)]
        ctx_sb = e(nc.sbuf_tensor("ctx_sb", [128, 32 * 512], BF16))
        wdsl = e(nc.sbuf_tensor("wdsl", [128, HC * 512], BF16))
        bd_b = e(nc.sbuf_tensor("bd_b", [1, D], BF16))
        ao = [e(nc.sbuf_tensor(f"ao{i}", [128, D], BF16)) for i in range(4)]
        xt = [e(nc.sbuf_tensor("xt0", [128, D], F32))]
        tmpc = e(nc.sbuf_tensor("tmpc", [128, D], F32))
        bns = e(nc.sbuf_tensor("bns", [128, 48], F32))
        mv2 = [e(nc.sbuf_tensor(f"mv2{i}", [128, 2], F32)) for i in range(2)]
        sc2 = e(nc.sbuf_tensor("sc2", [128, 4], F32))
        consts = e(nc.sbuf_tensor("consts", [128, 3], F32))
        scb = [e(nc.psum_tensor(f"scb{i}", [128, 512], F32)) for i in range(4)]
        cps = [e(nc.psum_tensor(f"cps{i}", [128, 512], F32)) for i in range(2)]
        su = e(nc.psum_tensor("su", [33, 512], F32))
        rbp = e(nc.psum_tensor("rbp", [128, 512], F32))

        ldc = e(nc.semaphore("ldc"))
        wdc = e(nc.semaphore("wdc"))
        wdcb = e(nc.semaphore("wdcb"))
        kgl = [e(nc.semaphore(f"kgl{i}")) for i in range(2)]
        qls = [e(nc.semaphore(f"qls{i}")) for i in range(2)]
        msc = e(nc.semaphore("msc"))
        mpv = e(nc.semaphore("mpv"))
        msu = e(nc.semaphore("msu"))
        mrb = e(nc.semaphore("mrb"))
        exps = e(nc.semaphore("exps"))
        dmk = e(nc.semaphore("dmk"))
        vnr = e(nc.semaphore("vnr"))
        cts = e(nc.semaphore("cts"))
        wdl = e(nc.semaphore("wdl"))
        mdd = e(nc.semaphore("mdd"))
        dev = e(nc.semaphore("dev"))
        vst2 = e(nc.semaphore("vst2"))
        sst2 = e(nc.semaphore("sst2"))
        cmb = e(nc.semaphore("cmb"))
        cmb2 = e(nc.semaphore("cmb2"))
        xld = e(nc.semaphore("xld"))
        sx1 = e(nc.semaphore("sx1"))

        NHG = 128
        SLOT_PFX = [0]
        for _hg in range(NHG):
            SLOT_PFX.append(SLOT_PFX[-1] + NS_G[_hg // 32])

        with nc.Block() as blk:
            @blk.gpsimd
            def _(g):
                g.dma_start(out=bd_b[0:1, :], in_=bdr[0:1, :]).then_inc(wdcb, 16)
                for c in range(4):
                    g.dma_start(out=wd_bf[c * 1024:(c + 1) * 1024, :],
                                in_=wdt[c * 1024:(c + 1) * 1024, :]).then_inc(wdc, 16)
                for gt in range(16):
                    g.wait_ge(cmb, gt + 1)
                    g.wait_ge(xld, 16 * (gt + 1))
                    if gt >= 1:
                        g.wait_ge(sx1, 16 * gt)
                    g.tensor_tensor(tmpc[:, :], xt[0][:, :], tmpc[:, :], ALU.add).then_inc(cmb2, 1)

            @blk.vector
            def _(v):
                v.memset(consts[:, 0:1], 1e-5)
                v.memset(consts[:, 1:2], -0.5)
                v.memset(consts[:, 2:3], -1.0)
                v.memset(sm_sc[:, 0:1], 1.0 / 128.0).then_inc(vst2, 1)
                v.wait_ge(ldc, 80)
                for hg in range(NHG):
                    grp = hg // 32
                    head = hg % 32
                    for s in range(8):
                        v.wait_ge(exps, SLOT_PFX[hg] + s + 1)
                        msl = (dmask[:, s * 512:(s + 1) * 512] if s < 4 else
                               emask[:, (grp * 4 + s - 4) * 512:(grp * 4 + s - 3) * 512])
                        eb = (SLOT_PFX[hg] + s) % 4
                        v.tensor_tensor(expT[eb][:, :], expT[eb][:, :],
                                        msl, ALU.mult).then_inc(dmk, 1)
                    v.wait_ge(msu, hg + 1)
                    v.wait_ge(vnr, 4 * hg)
                    v.reciprocal(rcp[0:1, :], su[32 * (hg % 2):32 * (hg % 2) + 1, :]).then_inc(vnr, 1)
                    v.wait_ge(vnr, 4 * hg + 1)
                    if hg >= 1:
                        v.wait_ge(mrb, hg)
                    v.tensor_copy(rcpb[0:1, :], rcp[0:1, :]).then_inc(vnr, 1)
                    v.wait_ge(mrb, hg + 1)
                    v.tensor_copy(rb_sb[:, :], rbp[:, :]).then_inc(vnr, 1)
                    v.wait_ge(cts, hg + 1)
                    v.wait_ge(vnr, 4 * hg + 3)
                    if grp >= 1:
                        v.wait_ge(mdd, 32 * grp)
                    v.tensor_tensor(ctx_sb[:, head * 512:(head + 1) * 512],
                                    ctxe[hg % 2][:, :], rb_sb[:, :], ALU.mult).then_inc(vnr, 1)
                    if hg % 32 == 31:
                        gg = grp
                        for tt in range(4):
                            gt = gg * 4 + tt
                            v.wait_ge(dev, 32 * gg + 28 + tt + 1)
                            if gt >= 2:
                                v.wait_ge(sst2, 5 * (gt - 2) + 3)
                            if gt >= 1:
                                v.wait_ge(vst2, 1 + 2 * gt)
                            for c in range(8):
                                ii = v.bn_stats(bns[:, 6 * c:6 * c + 6], ao[tt][:, 512 * c:512 * (c + 1)])
                            ii.then_inc(vst2, 1)
                            v.wait_ge(vst2, 1 + 2 * gt + 1)
                            v.bn_aggr(mv2[gt % 2][:, :], bns[:, :]).then_inc(vst2, 1)
                            v.wait_ge(sst2, 5 * gt + 4)
                            if gt >= 1:
                                v.wait_ge(sx1, 16 * gt)
                            v.tensor_scalar(tmpc[:, :], ao[tt][:, :], mv2[gt % 2][:, 0:1],
                                            sc2[:, 1:2], ALU.subtract, ALU.mult).then_inc(cmb, 1)

            @blk.scalar
            def _(s):
                s.wait_ge(ldc, 80)
                s.wait_ge(vst2, 1)
                for hg in range(NHG):
                    grp = hg // 32
                    head = hg % 32
                    if hg >= 2:
                        s.wait_ge(msc, SLOT_PFX[hg - 1])
                    s.dma_start_transpose(qsl[hg % 2][:, :],
                                          qf[grp * 512:(grp + 1) * 512,
                                             head * 128:(head + 1) * 128]).then_inc(qls[hg % 2], 16)
                    for seq in range(NS_G[grp]):
                        gs = SLOT_PFX[hg] + seq
                        s.wait_ge(msc, gs + 1)
                        if gs >= 4:
                            s.wait_ge(mpv, gs - 3)
                        s.activation(expT[gs % 4][:, :], scb[gs % 4][:, :], AF.Exp,
                                     scale=sm_sc[:, 0:1],
                                     bias=biasc[:, NS_OFF[grp] + seq:NS_OFF[grp] + seq + 1]).then_inc(exps, 1)
                    s.wait_ge(mpv, SLOT_PFX[hg + 1])
                    s.activation(ctxe[hg % 2][:, :], cps[hg % 2][:, :], AF.Identity).then_inc(cts, 1)
                    if hg % 32 == 31:
                        gg = grp
                        for oc in range(8):
                            for tt in range(4):
                                k = 32 * gg + 4 * oc + tt
                                s.wait_ge(mdd, k + 1)
                                if gg >= 1 or oc >= 1:
                                    if oc == 0:
                                        s.wait_ge(cmb, 4 * (gg - 1) + tt + 1)
                                s.activation(ao[tt][:, oc * 512:(oc + 1) * 512],
                                             scb[k % 4][:, :], AF.Identity).then_inc(dev, 1)
                        for tt in range(4):
                            gt = gg * 4 + tt
                            s.wait_ge(vst2, 1 + 2 * (gt + 1))
                            s.activation(sc2[:, 0:1], mv2[gt % 2][:, 1:2], AF.Ln, bias=consts[:, 0:1]).then_inc(sst2, 1)
                            s.wait_ge(sst2, 5 * gt + 1)
                            s.activation(sc2[:, 1:2], sc2[:, 0:1], AF.Exp, scale=consts[:, 1:2]).then_inc(sst2, 1)
                            s.wait_ge(sst2, 5 * gt + 2)
                            s.activation(sc2[:, 2:3], sc2[:, 1:2], AF.Identity).then_inc(sst2, 1)
                            s.wait_ge(sst2, 5 * gt + 3)
                            s.activation(sc2[:, 3:4], sc2[:, 2:3], AF.Identity).then_inc(sst2, 1)
                            s.wait_ge(sst2, 5 * gt + 4)
                            s.activation(sc2[:, 3:4], sc2[:, 3:4], AF.Identity).then_inc(sst2, 1)

            @blk.tensor
            def _(t):
                t.wait_ge(ldc, 80)
                t.wait_ge(wdcb, 16)
                for hg in range(NHG):
                    grp = hg // 32
                    kvi = (hg % 32) // 4
                    K = grp * 8 + kvi
                    t.wait_ge(kgl[K % 2], 32 * (K // 2 + 1))
                    t.wait_ge(qls[hg % 2], 16 * (hg // 2 + 1))
                    if grp >= 1 and hg % 32 == 0:
                        t.wait_ge(dev, 32 * grp)
                    nsg = NS_G[grp]

                    def score_mm(seq):
                        gs = SLOT_PFX[hg] + seq
                        if gs >= 4:
                            t.wait_ge(exps, gs - 3)
                        t.matmul(scb[gs % 4][:, :], kg[K % 2][:, seq * 128:(seq + 1) * 128],
                                 qsl[hg % 2][:, :], start=True, stop=True).then_inc(msc, 1)

                    def pv_mm(seq):
                        gs = SLOT_PFX[hg] + seq
                        t.wait_ge(exps, gs + 1)
                        if seq < 8:
                            t.wait_ge(dmk, 8 * hg + seq + 1)
                        if seq == 0:
                            if hg >= 2:
                                t.wait_ge(vnr, 4 * (hg - 2) + 1)
                                t.wait_ge(cts, hg - 1)
                        jj = t.matmul(su[32 * (hg % 2):32 * (hg % 2) + 1, :], onesc[:, 0:1], expT[gs % 4][:, :],
                                      start=(seq == 0), stop=(seq == nsg - 1))
                        if seq == nsg - 1:
                            jj.then_inc(msu, 1)
                        t.matmul(cps[hg % 2][:, :], vg[K % 2][:, seq * 128:(seq + 1) * 128],
                                 expT[gs % 4][:, :], start=(seq == 0), stop=(seq == nsg - 1)).then_inc(mpv, 1)

                    def rb_mm(u):
                        t.wait_ge(vnr, 4 * u + 2)
                        if u >= 1:
                            t.wait_ge(vnr, 4 * (u - 1) + 3)
                        t.matmul(rbp[:, :], onesr[0:1, :], rcpb[0:1, :], start=True, stop=True).then_inc(mrb, 1)

                    score_mm(0)
                    score_mm(1)
                    score_mm(2)
                    if hg >= 1 and hg % 32 != 0:
                        rb_mm(hg - 1)
                    for seq in range(3, nsg):
                        score_mm(seq)
                        pv_mm(seq - 3)
                    pv_mm(nsg - 3)
                    pv_mm(nsg - 2)
                    pv_mm(nsg - 1)
                    if hg % 32 == 31:
                        rb_mm(hg)
                        gg = grp
                        t.wait_ge(vnr, 4 * 32 * (gg + 1))
                        for oc in range(8):
                            t.wait_ge(wdl, 16 * (8 * gg + oc + 1))
                            for tt in range(4):
                                k = 32 * gg + 4 * oc + tt
                                bank = k % 4
                                if k >= 4:
                                    t.wait_ge(dev, k - 3)
                                for hh in range(32):
                                    t.matmul(scb[bank][:, :],
                                             ctx_sb[:, hh * 512 + tt * 128: hh * 512 + tt * 128 + 128],
                                             wdsl[:, hh * 512:(hh + 1) * 512],
                                             start=(hh == 0), stop=False)
                                t.matmul(scb[bank][:, :], onesr[0:1, :],
                                         bd_b[0:1, oc * 512:(oc + 1) * 512],
                                         start=False, stop=True).then_inc(mdd, 1)

            @blk.sync
            def _(s2):
                s2.dma_start(out=onesr[0:1, :], in_=ones_h.ap()[0:1, :]).then_inc(ldc, 16)
                s2.dma_start(out=onesc[:, 0:1], in_=ones_c.ap()[:, 0:1]).then_inc(ldc, 16)
                s2.dma_start(out=dmask[:, :], in_=dmask_h.ap()[:, :]).then_inc(ldc, 16)
                s2.dma_start(out=biasc[:, :], in_=biasc_in[:, :]).then_inc(ldc, 16)
                s2.dma_start(out=emask[:, :], in_=emask_in[:, :]).then_inc(ldc, 16)
                for grp in range(4):
                    for kvi in range(8):
                        K = grp * 8 + kvi
                        nsg = NS_G[grp]
                        if K >= 2:
                            s2.wait_ge(mpv, SLOT_PFX[4 * (K - 1)])
                        s2.dma_start(out=kg[K % 2].ap().rearrange("p (s c) -> p s c", c=128)[:, 0:nsg, :],
                                     in_=kg_in[NS_OFF[grp]:NS_OFF[grp] + nsg, kvi, :, :].rearrange("s p c -> p s c")).then_inc(kgl[K % 2], 16)
                        s2.dma_start(out=vg[K % 2].ap().rearrange("p (s c) -> p s c", c=128)[:, 0:nsg, :],
                                     in_=vg_in[NS_OFF[grp]:NS_OFF[grp] + nsg, kvi, :, :].rearrange("s p c -> p s c")).then_inc(kgl[K % 2], 16)
                    for oc in range(8):
                        G = grp * 8 + oc
                        if G >= 1:
                            s2.wait_ge(mdd, 4 * G)
                        else:
                            s2.wait_ge(wdc, 64)
                        s2.dma_start(out=wdsl.ap().rearrange("p (c o) -> p c o", o=512),
                                     in_=wd_bf[:, oc * 512:(oc + 1) * 512]
                                     .rearrange("(c p) o -> p c o", p=128)).then_inc(wdl, 16)
                    for tt in range(4):
                        gt = grp * 4 + tt
                        if gt >= 1:
                            s2.wait_ge(cmb2, gt)
                        s2.dma_start(out=xt[0][:, :],
                                     in_=x_in[gt * 128:(gt + 1) * 128, :]).then_inc(xld, 16)
                        s2.wait_ge(cmb2, gt + 1)
                        s2.dma_start(out=x1_out[gt * 128:(gt + 1) * 128, :],
                                     in_=tmpc[:, :]).then_inc(sx1, 16)
                s2.wait_ge(sx1, 16 * 16)
    return nc

def build_l0_ffn():
    nc = new_nc()
    x1_in = nc.dram_tensor("x1", [T, D], F32, kind="ExternalInput").ap()
    wbt = nc.dram_tensor("wbt", [D, D], F32, kind="ExternalInput").ap()
    bbr = nc.dram_tensor("bbr", [1, D], F32, kind="ExternalInput").ap()
    enc_in = nc.dram_tensor("encc", [D, 40], F32, kind="ExternalInput").ap()
    c0p = nc.dram_tensor("c0p", [D, 96], F32, kind="ExternalInput").ap()
    m0p = nc.dram_tensor("m0p", [D, 96], U8, kind="ExternalInput").ap()
    c1p = nc.dram_tensor("c1p", [D, 24], F32, kind="ExternalInput").ap()
    m1p = nc.dram_tensor("m1p", [D, 24], U8, kind="ExternalInput").ap()
    mtt = nc.dram_tensor("mtt", [D, D], F32, kind="ExternalInput").ap()
    x2_out = nc.dram_tensor("x2", [T, D], F32, kind="ExternalOutput").ap()
    xn_hbm = nc.dram_tensor("xnh", [T, D], BF16, kind="Internal").ap()
    wb_bf = nc.dram_tensor("wb_bf", [D, D], BF16, kind="Internal").ap()
    mt_bf = nc.dram_tensor("mt_bf", [D, D], BF16, kind="Internal").ap()
    ones_h = nc.inline_tensor(np.ones((1, 128), dtype=BF), name="ones_row")
    ident_h = nc.inline_tensor(np.eye(128, dtype=BF), name="ident128")

    with ExitStack() as ctx:
        e = ctx.enter_context
        xt = [e(nc.sbuf_tensor("xt0", [128, D], F32))]
        bns = e(nc.sbuf_tensor("bns", [128, 48], F32))
        mv = [e(nc.sbuf_tensor(f"mv{i}", [128, 2], F32)) for i in range(2)]
        sc = e(nc.sbuf_tensor("sc", [128, 4], F32))
        consts = e(nc.sbuf_tensor("consts", [128, 4], F32))
        xn_t = e(nc.sbuf_tensor("xnt", [128, D], BF16))
        xnf = e(nc.sbuf_tensor("xnf", [128, HC * 512], BF16))
        wsl = e(nc.sbuf_tensor("wsl", [128, HC * 512], BF16))
        mtsl = [e(nc.sbuf_tensor("mtsl0", [128, HC * 128], BF16))]
        encb = e(nc.sbuf_tensor("encb", [128, HC * 40], BF16))
        bb_b = e(nc.sbuf_tensor("bb_b", [1, D], BF16))
        onesr = e(nc.sbuf_tensor("onesr", [1, 128], BF16))
        ident = e(nc.sbuf_tensor("ident", [128, 128], BF16))
        c0sb = e(nc.sbuf_tensor("c0sb", [128, 96], F32))
        m0sb = e(nc.sbuf_tensor("m0sb", [128, 96], U8))
        c1sb = e(nc.sbuf_tensor("c1sb", [128, 24], F32))
        m1sb = e(nc.sbuf_tensor("m1sb", [128, 24], U8))
        mf0 = e(nc.sbuf_tensor("mf0", [128, 96], F32))
        mf1 = e(nc.sbuf_tensor("mf1", [128, 24], F32))
        tsum = e(nc.sbuf_tensor("tsum", [128, 40], F32))
        cbf = e(nc.sbuf_tensor("cbf", [128, HC * 40], BF16))
        dsb = [e(nc.sbuf_tensor(f"dsb{i}", [128, 40], BF16)) for i in range(2)]
        esb = [e(nc.sbuf_tensor(f"esb{i}", [128, 40], BF16)) for i in range(2)]
        dT = e(nc.sbuf_tensor("dT", [40, D], BF16))
        e_fm = e(nc.sbuf_tensor("e_fm", [40, NTT * 128], BF16))
        ao = [e(nc.sbuf_tensor(f"ao{i}", [128, D], BF16)) for i in range(4)]
        tmpc = e(nc.sbuf_tensor("tmpc", [128, D], F32))
        xr = e(nc.sbuf_tensor("xr", [128, D], F32))
        ps = [e(nc.psum_tensor(f"ps{i}", [128, 512], F32)) for i in range(4)]
        pd = [e(nc.psum_tensor(f"pd{i}", [128, 40], F32)) for i in range(2)]
        pt2 = [e(nc.psum_tensor(f"pt2{i}", [40, 128], BF16)) for i in range(2)]

        ldc = e(nc.semaphore("ldc"))
        ldx = e(nc.semaphore("ldx"))
        gc = e(nc.semaphore("gc"))
        cfl = [e(nc.semaphore(f"cfl{i}")) for i in range(2)]
        mtl = e(nc.semaphore("mtl"))
        wbl = e(nc.semaphore("wbl"))
        xfl = e(nc.semaphore("xfl"))
        vst = e(nc.semaphore("vst"))
        sst = e(nc.semaphore("sst"))
        vcc = e(nc.semaphore("vcc"))
        mdc = e(nc.semaphore("mdc"))
        dse = e(nc.semaphore("dse"))
        mdt = e(nc.semaphore("mdt"))
        vdt = e(nc.semaphore("vdt"))
        mec = e(nc.semaphore("mec"))
        ese = e(nc.semaphore("ese"))
        met = e(nc.semaphore("met"))
        vet = e(nc.semaphore("vet"))
        mdd = e(nc.semaphore("mdd"))
        mev = e(nc.semaphore("mev"))
        vst2 = e(nc.semaphore("vst2"))
        sst2 = e(nc.semaphore("sst2"))
        cmb = e(nc.semaphore("cmb"))
        cmb2 = e(nc.semaphore("cmb2"))
        xld = e(nc.semaphore("xld"))
        sxn = [e(nc.semaphore(f"sxn{i}")) for i in range(2)]
        sx2 = e(nc.semaphore("sx2"))

        with nc.Block() as blk:
            @blk.gpsimd
            def _(g):
                g.dma_start(out=bb_b[0:1, :], in_=bbr[0:1, :]).then_inc(gc, 16)
                g.dma_start(out=encb.ap().rearrange("p (c r) -> p c r", r=40),
                            in_=enc_in.rearrange("(c p) r -> p c r", p=128)).then_inc(gc, 16)
                for c in range(4):
                    g.dma_start(out=wb_bf[c * 1024:(c + 1) * 1024, :],
                                in_=wbt[c * 1024:(c + 1) * 1024, :]).then_inc(gc, 16)
                for c in range(4):
                    g.dma_start(out=mt_bf[c * 1024:(c + 1) * 1024, :],
                                in_=mtt[c * 1024:(c + 1) * 1024, :]).then_inc(gc, 16)
                for gt in range(16):
                    g.wait_ge(cmb, gt + 1)
                    g.wait_ge(xld, 16 * (gt + 1))
                    if gt >= 1:
                        g.wait_ge(sx2, 16 * gt)
                    g.tensor_tensor(tmpc[:, :], xr[:, :], tmpc[:, :], ALU.add).then_inc(cmb2, 1)

            @blk.vector
            def _(v):
                v.memset(consts[:, 0:1], 1e-5)
                v.memset(consts[:, 1:2], -0.5)
                v.memset(consts[:, 2:3], -1.0)
                v.memset(consts[:, 3:4], 0.5).then_inc(vst, 1)
                for i in range(NTT):
                    v.wait_ge(ldx, 16 * (i + 1))
                    if i >= 1:
                        v.wait_ge(vst, 1 + 2 * i)
                    if i >= 2:
                        v.wait_ge(sst, 5 * (i - 2) + 3)
                    for c in range(8):
                        ii = v.bn_stats(bns[:, 6 * c:6 * c + 6], xt[0][:, 512 * c:512 * (c + 1)])
                    ii.then_inc(vst, 1)
                    v.wait_ge(vst, 1 + 2 * i + 1)
                    v.bn_aggr(mv[i % 2][:, :], bns[:, :]).then_inc(vst, 1)
                nvc = 0
                for oc in range(HC):
                    v.wait_ge(cfl[oc % 2], 64 * (oc // 2 + 1))
                    v.tensor_copy(mf0[:, :], m0sb[:, :])
                    v.tensor_copy(mf1[:, :], m1sb[:, :]).then_inc(vcc, 1); nvc += 1
                    v.wait_ge(vcc, nvc)
                    v.tensor_tensor(mf0[:, :], mf0[:, :], c0sb[:, :], ALU.mult)
                    v.tensor_tensor(mf1[:, :], mf1[:, :], c1sb[:, :], ALU.mult).then_inc(vcc, 1); nvc += 1
                    v.wait_ge(vcc, nvc)
                    v.tensor_tensor(tsum[:, 0:32], mf0[:, 0:32], mf0[:, 32:64], ALU.add)
                    v.tensor_tensor(tsum[:, 32:40], mf1[:, 0:8], mf1[:, 8:16], ALU.add).then_inc(vcc, 1); nvc += 1
                    v.wait_ge(vcc, nvc)
                    v.tensor_tensor(cbf[:, oc * 40:oc * 40 + 32], tsum[:, 0:32], mf0[:, 64:96], ALU.add)
                    v.tensor_tensor(cbf[:, oc * 40 + 32:oc * 40 + 40], tsum[:, 32:40], mf1[:, 16:24], ALU.add).then_inc(vcc, 1); nvc += 1
                for u in range(HC):
                    v.wait_ge(mdt, u + 1)
                    v.tensor_copy(dT[:, u * 128:(u + 1) * 128], pt2[u % 2][:, :]).then_inc(vdt, 1)
                for G in range(4):
                    for tt in range(4):
                        eu = G * 4 + tt
                        v.wait_ge(met, eu + 1)
                        v.tensor_copy(e_fm[:, eu * 128:(eu + 1) * 128], pt2[eu % 2][:, :]).then_inc(vet, 1)
                    for tt in range(4):
                        gt = G * 4 + tt
                        v.wait_ge(mev, 32 * G + 28 + tt + 1)
                        if gt >= 2:
                            v.wait_ge(sst2, 5 * (gt - 2) + 3)
                        if gt >= 1:
                            v.wait_ge(vst2, 2 * gt)
                        for c in range(8):
                            ii = v.bn_stats(bns[:, 6 * c:6 * c + 6], ao[tt][:, 512 * c:512 * (c + 1)])
                        ii.then_inc(vst2, 1)
                        v.wait_ge(vst2, 2 * gt + 1)
                        v.bn_aggr(mv[gt % 2][:, :], bns[:, :]).then_inc(vst2, 1)
                        v.wait_ge(sst2, 5 * gt + 4)
                        if gt >= 1:
                            v.wait_ge(sx2, 16 * gt)
                        v.tensor_scalar(tmpc[:, :], ao[tt][:, :], mv[gt % 2][:, 0:1],
                                        sc[:, 1:2], ALU.subtract, ALU.mult).then_inc(cmb, 1)

            @blk.scalar
            def _(s):
                s.wait_ge(vst, 1)
                for i in range(NTT):
                    s.wait_ge(vst, 1 + 2 * (i + 1))
                    if i >= 1:
                        jj, kk = divmod(i - 1, 8)
                        s.wait_ge(sxn[jj], 16 * (kk + 1))
                    s.activation(sc[:, 0:1], mv[i % 2][:, 1:2], AF.Ln, bias=consts[:, 0:1]).then_inc(sst, 1)
                    s.wait_ge(sst, 5 * i + 1)
                    s.activation(sc[:, 1:2], sc[:, 0:1], AF.Exp, scale=consts[:, 1:2]).then_inc(sst, 1)
                    s.wait_ge(sst, 5 * i + 2)
                    s.activation(sc[:, 2:3], mv[i % 2][:, 0:1], AF.Identity, scale=sc[:, 1:2]).then_inc(sst, 1)
                    s.wait_ge(sst, 5 * i + 3)
                    s.activation(sc[:, 3:4], sc[:, 2:3], AF.Identity, scale=consts[:, 2:3]).then_inc(sst, 1)
                    s.wait_ge(sst, 5 * i + 4)
                    s.activation(xn_t[:, :], xt[0][:, :], AF.Identity,
                                 scale=sc[:, 1:2], bias=sc[:, 3:4]).then_inc(sst, 1)
                for u in range(HC):
                    s.wait_ge(mdc, u + 1)
                    if u >= 2:
                        s.wait_ge(mdt, u - 1)
                    s.activation(dsb[u % 2][:, :], pd[u % 2][:, :], AF.Identity,
                                 scale=consts[:, 3:4]).then_inc(dse, 1)
                for G in range(4):
                    s.wait_ge(sxn[G // 2], 16 * ((4 * G + 3) % 8 + 1))
                    if G >= 1:
                        s.wait_ge(mdd, 32 * G)
                    for c in range(HC):
                        s.dma_start_transpose(xnf[:, c * 512:(c + 1) * 512],
                                              xn_hbm[G * 512:(G + 1) * 512,
                                                     c * 128:(c + 1) * 128]).then_inc(xfl, 16)
                    for tt in range(4):
                        eu = G * 4 + tt
                        s.wait_ge(mec, eu + 1)
                        if eu >= 2:
                            s.wait_ge(met, eu - 1)
                        s.activation(esb[eu % 2][:, :], pd[eu % 2][:, :], AF.Identity).then_inc(ese, 1)
                    for oc in range(8):
                        for tt in range(4):
                            U = 32 * G + 4 * oc + tt
                            s.wait_ge(mdd, U + 1)
                            if G >= 1 and oc == 0:
                                s.wait_ge(cmb, 4 * (G - 1) + tt + 1)
                            s.activation(ao[tt][:, oc * 512:(oc + 1) * 512],
                                         ps[U % 4][:, :], AF.Identity).then_inc(mev, 1)
                    for tt in range(4):
                        gt = G * 4 + tt
                        s.wait_ge(vst2, 2 * (gt + 1))
                        s.activation(sc[:, 0:1], mv[gt % 2][:, 1:2], AF.Ln, bias=consts[:, 0:1]).then_inc(sst2, 1)
                        s.wait_ge(sst2, 5 * gt + 1)
                        s.activation(sc[:, 1:2], sc[:, 0:1], AF.Exp, scale=consts[:, 1:2]).then_inc(sst2, 1)
                        s.wait_ge(sst2, 5 * gt + 2)
                        s.activation(sc[:, 2:3], sc[:, 1:2], AF.Identity).then_inc(sst2, 1)
                        s.wait_ge(sst2, 5 * gt + 3)
                        s.activation(sc[:, 2:3], sc[:, 2:3], AF.Identity).then_inc(sst2, 1)
                        s.wait_ge(sst2, 5 * gt + 4)
                        s.activation(sc[:, 2:3], sc[:, 2:3], AF.Identity).then_inc(sst2, 1)

            @blk.tensor
            def _(t):
                t.wait_ge(ldc, 32)
                t.wait_ge(gc, 160)
                t.wait_ge(vcc, 4 * HC)
                for u in range(HC):
                    t.wait_ge(mtl, 16 * (u + 1))
                    if u >= 2:
                        t.wait_ge(dse, u - 1)
                    for c in range(HC):
                        t.matmul(pd[u % 2][:, :], mtsl[0][:, c * 128:(c + 1) * 128],
                                 cbf[:, c * 40:(c + 1) * 40],
                                 start=(c == 0), stop=(c == HC - 1)).then_inc(mdc, 0 if c < HC - 1 else 1)
                    t.wait_ge(dse, u + 1)
                    if u >= 2:
                        t.wait_ge(vdt, u - 1)
                    t.transpose(pt2[u % 2][:, :], dsb[u % 2][:, :], ident[:, :]).then_inc(mdt, 1)
                for G in range(4):
                    t.wait_ge(xfl, 16 * HC * (G + 1))
                    for tt in range(4):
                        eu = G * 4 + tt
                        gt = G * 4 + tt
                        if eu < 2:
                            t.wait_ge(dse, HC - 2 + eu + 1)
                        else:
                            t.wait_ge(ese, eu - 1)
                        for c in range(HC):
                            t.matmul(pd[eu % 2][:, :],
                                     xnf[:, c * 512 + tt * 128: c * 512 + tt * 128 + 128],
                                     encb[:, c * 40:(c + 1) * 40],
                                     start=(c == 0), stop=(c == HC - 1)).then_inc(mec, 0 if c < HC - 1 else 1)
                        t.wait_ge(ese, eu + 1)
                        if eu >= 2:
                            t.wait_ge(vet, eu - 1)
                        elif eu < 2:
                            t.wait_ge(vdt, HC - 2 + eu + 1)
                        t.transpose(pt2[eu % 2][:, :], esb[eu % 2][:, :], ident[:, :]).then_inc(met, 1)
                    t.wait_ge(vdt, HC)
                    for oc in range(8):
                        G8 = G * 8 + oc
                        t.wait_ge(wbl, 16 * (G8 + 1))
                        for tt in range(4):
                            U = 32 * G + 4 * oc + tt
                            gt = G * 4 + tt
                            t.wait_ge(vet, G * 4 + tt + 1)
                            if U >= 4:
                                t.wait_ge(mev, U - 3)
                            bank = U % 4
                            for c in range(HC):
                                t.matmul(ps[bank][:, :],
                                         xnf[:, c * 512 + tt * 128: c * 512 + tt * 128 + 128],
                                         wsl[:, c * 512:(c + 1) * 512],
                                         start=(c == 0), stop=False)
                            t.matmul(ps[bank][:, :], e_fm[:, gt * 128:(gt + 1) * 128],
                                     dT[:, oc * 512:(oc + 1) * 512], start=False, stop=False)
                            t.matmul(ps[bank][:, :], onesr[0:1, :],
                                     bb_b[0:1, oc * 512:(oc + 1) * 512],
                                     start=False, stop=True).then_inc(mdd, 1)

            @blk.sync
            def _(s2):
                s2.dma_start(out=onesr[0:1, :], in_=ones_h.ap()[0:1, :]).then_inc(ldc, 16)
                s2.dma_start(out=ident[:, :], in_=ident_h.ap()[:, :]).then_inc(ldc, 16)
                for i in range(NTT):
                    if i >= 1:
                        s2.wait_ge(sst, 5 * i)
                    s2.dma_start(out=xt[0][:, :],
                                 in_=x1_in[i * 128:(i + 1) * 128, :]).then_inc(ldx, 16)
                    if i >= 1:
                        s2.wait_ge(sst, 5 * i)
                        s2.dma_start(out=xn_hbm[(i - 1) * 128:i * 128, :],
                                     in_=xn_t[:, :]).then_inc(sxn[(i - 1) // 8], 16)
                s2.wait_ge(sst, 5 * NTT)
                s2.dma_start(out=xn_hbm[(NTT - 1) * 128:NTT * 128, :],
                             in_=xn_t[:, :]).then_inc(sxn[1], 16)
                for oc in range(HC):
                    if oc >= 1:
                        s2.wait_ge(vcc, 4 * oc)
                    s2.dma_start(out=c0sb[:, :], in_=c0p[oc * 128:(oc + 1) * 128, :]).then_inc(cfl[oc % 2], 16)
                    s2.dma_start(out=m0sb[:, :], in_=m0p[oc * 128:(oc + 1) * 128, :]).then_inc(cfl[oc % 2], 16)
                    s2.dma_start(out=c1sb[:, :], in_=c1p[oc * 128:(oc + 1) * 128, :]).then_inc(cfl[oc % 2], 16)
                    s2.dma_start(out=m1sb[:, :], in_=m1p[oc * 128:(oc + 1) * 128, :]).then_inc(cfl[oc % 2], 16)
                s2.wait_ge(gc, 160)
                for u in range(HC):
                    if u >= 1:
                        s2.wait_ge(mdc, u)
                    s2.dma_start(out=mtsl[0].ap().rearrange("p (c o) -> p c o", o=128),
                                 in_=mt_bf[:, u * 128:(u + 1) * 128]
                                 .rearrange("(c p) o -> p c o", p=128)).then_inc(mtl, 16)
                for G in range(4):
                    for oc in range(8):
                        G8 = G * 8 + oc
                        if G8 >= 1:
                            s2.wait_ge(mdd, 4 * G8)
                        s2.dma_start(out=wsl.ap().rearrange("p (c o) -> p c o", o=512),
                                     in_=wb_bf[:, (G8 % 8) * 512:(G8 % 8 + 1) * 512]
                                     .rearrange("(c p) o -> p c o", p=128)).then_inc(wbl, 16)
                    for tt in range(4):
                        gt = G * 4 + tt
                        if gt >= 1:
                            s2.wait_ge(cmb2, gt)
                        s2.dma_start(out=xr[:, :],
                                     in_=x1_in[gt * 128:(gt + 1) * 128, :]).then_inc(xld, 16)
                        s2.wait_ge(cmb2, gt + 1)
                        s2.dma_start(out=x2_out[gt * 128:(gt + 1) * 128, :],
                                     in_=tmpc[:, :]).then_inc(sx2, 16)
                s2.wait_ge(sx2, 16 * 16)
    return nc

def build_l1_qkv():
    nc = new_nc()
    x2_in = nc.dram_tensor("x2", [T, D], F32, kind="ExternalInput").ap()
    wcat = nc.dram_tensor("wcat", [D, 6144], F32, kind="ExternalInput").ap()
    cos_in = nc.dram_tensor("cos", [T, 64], F32, kind="ExternalInput").ap()
    sin_in = nc.dram_tensor("sin", [T, 64], F32, kind="ExternalInput").ap()
    qs_o = nc.dram_tensor("qs", [T, 4096], BF16, kind="ExternalOutput").ap()
    sq_o = nc.dram_tensor("sq", [T, 4096], BF16, kind="ExternalOutput").ap()
    ks_o = nc.dram_tensor("ks", [T, 1024], BF16, kind="ExternalOutput").ap()
    gs_o = nc.dram_tensor("gs", [T, 1024], BF16, kind="ExternalOutput").ap()
    sk_o = nc.dram_tensor("sk", [T, 1024], BF16, kind="ExternalOutput").ap()
    vv_o = nc.dram_tensor("vv", [T, 1024], BF16, kind="ExternalOutput").ap()
    xn_hbm = nc.dram_tensor("xnh", [T, D], BF16, kind="Internal").ap()
    w_bf = nc.dram_tensor("w_bf", [D, 6144], BF16, kind="Internal").ap()

    units = []
    for j in range(2):
        for uc in range(12):
            for ti in range(8):
                units.append(dict(j=j, uc=uc, ti=ti, g=j * 12 + uc,
                                  kind="q" if uc < 8 else "kv",
                                  trow=j * 1024 + ti * 128))
    NU = len(units)

    with ExitStack() as ctx:
        e = ctx.enter_context
        xt = [e(nc.sbuf_tensor(f"xt{i}", [128, D], F32)) for i in range(2)]
        bns = e(nc.sbuf_tensor("bns", [128, 48], F32))
        mv = [e(nc.sbuf_tensor(f"mv{i}", [128, 2], F32)) for i in range(2)]
        sc = e(nc.sbuf_tensor("sc", [128, 4], F32))
        consts = e(nc.sbuf_tensor("consts", [128, 5], F32))
        xn_t = e(nc.sbuf_tensor("xnt", [128, D], BF16))
        xnf = e(nc.sbuf_tensor("xnf", [128, HC * 1024], BF16))
        wsl = [e(nc.sbuf_tensor(f"wsl{i}", [128, HC * 512], BF16)) for i in range(2)]
        cs = e(nc.sbuf_tensor("cs", [128, NTT * 64], F32))
        sn = e(nc.sbuf_tensor("sn", [128, NTT * 64], F32))
        pb = [e(nc.sbuf_tensor(f"pb{i}", [128, 512], F32)) for i in range(2)]
        esq = [e(nc.sbuf_tensor(f"esq{i}", [128, 512], BF16)) for i in range(2)]
        nmx = [e(nc.sbuf_tensor(f"nmx{i}", [128, 4], F32)) for i in range(2)]
        sums = [e(nc.sbuf_tensor(f"sums{i}", [128, 4], F32)) for i in range(2)]
        rsc = [e(nc.sbuf_tensor(f"rsc{i}", [128, 4], F32)) for i in range(2)]
        qso = [e(nc.sbuf_tensor(f"qso{i}", [128, 512], BF16)) for i in range(2)]
        sqo = [e(nc.sbuf_tensor(f"sqo{i}", [128, 512], BF16)) for i in range(2)]
        gt1 = e(nc.sbuf_tensor("gt1", [128, 256], F32))
        gt2 = e(nc.sbuf_tensor("gt2", [128, 256], F32))
        gt3 = e(nc.sbuf_tensor("gt3", [128, 256], F32))
        gso = [e(nc.sbuf_tensor(f"gso{i}", [128, 256], BF16)) for i in range(2)]
        vvo = [e(nc.sbuf_tensor(f"vvo{i}", [128, 256], BF16)) for i in range(2)]
        tmA = e(nc.sbuf_tensor("tmA", [128, 512], F32))
        tmB = e(nc.sbuf_tensor("tmB", [128, 512], F32))
        ps = [e(nc.psum_tensor(f"ps{i}", [128, 512], F32)) for i in range(4)]

        ldc = e(nc.semaphore("ldc"))
        ldx = [e(nc.semaphore(f"ldx{i}")) for i in range(2)]
        wc = e(nc.semaphore("wc"))
        wl = [e(nc.semaphore(f"wl{i}")) for i in range(2)]
        xf = e(nc.semaphore("xf"))
        vst = e(nc.semaphore("vst"))
        sst = e(nc.semaphore("sst"))
        mm = e(nc.semaphore("mm"))
        acs = e(nc.semaphore("acs"))
        vd = e(nc.semaphore("vd"))
        rpp = e(nc.semaphore("rpp"))
        sxn = [e(nc.semaphore(f"sxn{i}")) for i in range(2)]
        sop = [e(nc.semaphore(f"sop{i}")) for i in range(2)]

        ACT_N = [5 if u["kind"] == "q" else 7 for u in units]
        ACT_PFX = [0]
        for k in ACT_N:
            ACT_PFX.append(ACT_PFX[-1] + k)
        DVE_N = [5 if u["kind"] == "q" else 8 for u in units]
        DVE_PFX = [0]
        for k in DVE_N:
            DVE_PFX.append(DVE_PFX[-1] + k)
        WLP = {}
        _a = [0, 0]
        for _g in range(24):
            _uc = _g % 12
            _a[_g % 2] += 16 if _uc < 8 else 32
            WLP[_g] = _a[_g % 2]
        SOP = {}
        _b = [0, 0]
        for _i, _u in enumerate(units):
            _b[_i % 2] += 32 if _u["kind"] == "q" else 64
            SOP[_i] = _b[_i % 2]

        def rope_pair(eng, pbv, rov, tm, nh, cslc, snlc, half, sem):
            w = 64 * nh
            eng.wait_ge(sem[0], sem[1])
            lo = pbv[:, :, 0:64]; hi = pbv[:, :, 64:128]
            t1 = tm.ap()[:, 0:w].rearrange("p (h d) -> p h d", d=64)
            t2 = tm.ap()[:, w:2 * w].rearrange("p (h d) -> p h d", d=64)
            first, second = (lo, hi) if half == "lo" else (hi, lo)
            aa, cc = bass.broadcast_tensor_aps(first, cslc)
            eng.tensor_tensor(t1, aa, cc, ALU.mult)
            bb2, ss2 = bass.broadcast_tensor_aps(second, snlc)
            eng.tensor_tensor(t2, bb2, ss2, ALU.mult).then_inc(sem[0], 1)
            sem[1] += 1
            eng.wait_ge(sem[0], sem[1])
            out = rov[:, :, 0:64] if half == "lo" else rov[:, :, 64:128]
            eng.tensor_tensor(out, t1, t2, ALU.subtract if half == "lo" else ALU.add).then_inc(sem[0], 1)
            sem[1] += 1

        with nc.Block() as blk:
            @blk.gpsimd
            def _(g):
                for c in range(8):
                    g.dma_start(out=w_bf[c * 512:(c + 1) * 512, :],
                                in_=wcat[c * 512:(c + 1) * 512, :]).then_inc(wc, 16)
                g.wait_ge(ldc, 32)
                semp = [rpp, 0]
                for i, u in enumerate(units):
                    par = i % 2
                    tcos = u["j"] * 8 + u["ti"]
                    cslc = cs.ap()[:, tcos * 64:(tcos + 1) * 64].rearrange("p (r d) -> p r d", r=1)
                    snlc = sn.ap()[:, tcos * 64:(tcos + 1) * 64].rearrange("p (r d) -> p r d", r=1)
                    g.wait_ge(acs, ACT_PFX[i] + 1)
                    if i >= 2:
                        g.wait_ge(sop[par], SOP[i - 2])
                    if u["kind"] == "q":
                        pbv = pb[par].ap()[:, 0:512].rearrange("p (h d) -> p h d", d=128)
                        rov = sqo[par].ap()[:, 0:512].rearrange("p (h d) -> p h d", d=128)
                        rope_pair(g, pbv, rov, tmB, 4, cslc, snlc, "hi", semp)
                    else:
                        pbv = pb[par].ap()[:, 0:256].rearrange("p (h d) -> p h d", d=128)
                        rov = sqo[par].ap()[:, 0:256].rearrange("p (h d) -> p h d", d=128)
                        rope_pair(g, pbv, rov, tmB, 2, cslc, snlc, "hi", semp)

            @blk.vector
            def _(v):
                v.memset(consts[:, 0:1], 1e-5)
                v.memset(consts[:, 1:2], -0.5)
                v.memset(consts[:, 2:3], -1.0)
                v.memset(consts[:, 3:4], 1.0)
                v.memset(consts[:, 4:5], 0.0625).then_inc(vst, 1)
                for i in range(NTT):
                    v.wait_ge(ldx[i % 2], 16 * (i // 2 + 1))
                    if i >= 1:
                        v.wait_ge(vst, 1 + 2 * i)
                    if i >= 2:
                        v.wait_ge(sst, 5 * (i - 2) + 3)
                    for c in range(8):
                        ii = v.bn_stats(bns[:, 6 * c:6 * c + 6], xt[i % 2][:, 512 * c:512 * (c + 1)])
                    ii.then_inc(vst, 1)
                    v.wait_ge(vst, 1 + 2 * i + 1)
                    v.bn_aggr(mv[i % 2][:, :], bns[:, :]).then_inc(vst, 1)
                v.wait_ge(ldc, 32)
                semv = [vd, 0]
                for i, u in enumerate(units):
                    par = i % 2
                    nh = 4 if u["kind"] == "q" else 2
                    tcos = u["j"] * 8 + u["ti"]
                    cslc = cs.ap()[:, tcos * 64:(tcos + 1) * 64].rearrange("p (r d) -> p r d", r=1)
                    snlc = sn.ap()[:, tcos * 64:(tcos + 1) * 64].rearrange("p (r d) -> p r d", r=1)
                    v.wait_ge(acs, ACT_PFX[i] + 1)
                    for hh in range(nh):
                        ii = v.reduce_max(nmx[par][:, hh:hh + 1],
                                          pb[par][:, hh * 128:(hh + 1) * 128],
                                          axis=AX.X, negate=True)
                    ii.then_inc(vd, 1); semv[1] += 1
                    v.wait_ge(acs, ACT_PFX[i] + 1 + nh)
                    for hh in range(nh):
                        ii = v.reciprocal(rsc[par][:, hh:hh + 1], sums[par][:, hh:hh + 1])
                    ii.then_inc(vd, 1); semv[1] += 1
                    v.wait_ge(vd, semv[1])
                    if i >= 2:
                        v.wait_ge(sop[par], SOP[i - 2])
                    for hh in range(nh):
                        ii = v.tensor_scalar(qso[par][:, hh * 128:(hh + 1) * 128],
                                             esq[par][:, hh * 128:(hh + 1) * 128],
                                             rsc[par][:, hh:hh + 1], None, ALU.mult)
                    ii.then_inc(vd, 1); semv[1] += 1
                    if u["kind"] == "kv":
                        v.tensor_scalar(gt2[:, :], pb[par][:, 0:256], 0.0, None, ALU.min).then_inc(vd, 1)
                        semv[1] += 1
                        v.wait_ge(acs, ACT_PFX[i] + 6)
                        v.wait_ge(vd, semv[1])
                        v.tensor_tensor(gt2[:, :], gt2[:, :], gt3[:, :], ALU.subtract).then_inc(vd, 1)
                        semv[1] += 1
                        v.wait_ge(vd, semv[1])
                        v.tensor_scalar(gso[par][:, :], gt2[:, :], consts[:, 4:5], None, ALU.mult).then_inc(vd, 1)
                        semv[1] += 1
                    if u["kind"] == "q":
                        pbv = pb[par].ap()[:, 0:512].rearrange("p (h d) -> p h d", d=128)
                        rov = sqo[par].ap()[:, 0:512].rearrange("p (h d) -> p h d", d=128)
                        rope_pair(v, pbv, rov, tmA, 4, cslc, snlc, "lo", semv)
                    else:
                        pbv = pb[par].ap()[:, 0:256].rearrange("p (h d) -> p h d", d=128)
                        rov = sqo[par].ap()[:, 0:256].rearrange("p (h d) -> p h d", d=128)
                        rope_pair(v, pbv, rov, tmA, 2, cslc, snlc, "lo", semv)

            @blk.scalar
            def _(s):
                s.wait_ge(vst, 1)
                for i in range(NTT):
                    s.wait_ge(vst, 1 + 2 * (i + 1))
                    if i >= 1:
                        jj, kk = divmod(i - 1, 8)
                        s.wait_ge(sxn[jj], 16 * (kk + 1))
                    s.activation(sc[:, 0:1], mv[i % 2][:, 1:2], AF.Ln, bias=consts[:, 0:1]).then_inc(sst, 1)
                    s.wait_ge(sst, 5 * i + 1)
                    s.activation(sc[:, 1:2], sc[:, 0:1], AF.Exp, scale=consts[:, 1:2]).then_inc(sst, 1)
                    s.wait_ge(sst, 5 * i + 2)
                    s.activation(sc[:, 2:3], mv[i % 2][:, 0:1], AF.Identity, scale=sc[:, 1:2]).then_inc(sst, 1)
                    s.wait_ge(sst, 5 * i + 3)
                    s.activation(sc[:, 3:4], sc[:, 2:3], AF.Identity, scale=consts[:, 2:3]).then_inc(sst, 1)
                    s.wait_ge(sst, 5 * i + 4)
                    s.activation(xn_t[:, :], xt[i % 2][:, :], AF.Identity,
                                 scale=sc[:, 1:2], bias=sc[:, 3:4]).then_inc(sst, 1)
                s.wait_ge(sxn[0], 16 * 8)
                for c in range(HC):
                    s.dma_start_transpose(xnf[:, c * 1024:(c + 1) * 1024],
                                          xn_hbm[0:1024, c * 128:(c + 1) * 128]).then_inc(xf, 16)
                for i, u in enumerate(units):
                    if i == 96:
                        s.wait_ge(sxn[1], 16 * 8)
                        s.wait_ge(mm, 96)
                        for c in range(HC):
                            s.dma_start_transpose(xnf[:, c * 1024:(c + 1) * 1024],
                                                  xn_hbm[1024:2048, c * 128:(c + 1) * 128]).then_inc(xf, 16)
                    par = i % 2
                    nh = 4 if u["kind"] == "q" else 2
                    s.wait_ge(mm, i + 1)
                    bank = i % 4
                    if i >= 2:
                        s.wait_ge(vd, DVE_PFX[i - 1] - 1)
                        s.wait_ge(rpp, 2 * (i - 1) - 1)
                    s.activation(pb[par][:, :], ps[bank][:, :], AF.Identity).then_inc(acs, 1)
                    s.wait_ge(vd, DVE_PFX[i] + 1)
                    if i >= 2:
                        s.wait_ge(vd, DVE_PFX[i - 2] + 3)
                    for hh in range(nh):
                        s.activation(esq[par][:, hh * 128:(hh + 1) * 128],
                                     pb[par][:, hh * 128:(hh + 1) * 128], AF.Exp,
                                     bias=nmx[par][:, hh:hh + 1],
                                     accum_out=sums[par][:, hh:hh + 1]).then_inc(acs, 1)
                    if u["kind"] == "kv":
                        s.activation(gt1[:, :], pb[par][:, 0:256], AF.Abs).then_inc(acs, 1)
                        s.wait_ge(acs, ACT_PFX[i] + 4)
                        s.activation(gt1[:, :], gt1[:, :], AF.Exp, scale=consts[:, 2:3]).then_inc(acs, 1)
                        s.wait_ge(acs, ACT_PFX[i] + 5)
                        if u["ti"] >= 1:
                            s.wait_ge(vd, DVE_PFX[i - 1] + 5)
                        s.activation(gt3[:, :], gt1[:, :], AF.Ln, bias=consts[:, 3:4]).then_inc(acs, 1)
                        if i >= 2:
                            s.wait_ge(sop[par], SOP[i - 2])
                        s.activation(vvo[par].ap()[:, 0:256], ps[bank].ap()[:, 256:512],
                                     AF.Identity).then_inc(acs, 1)

            @blk.tensor
            def _(t):
                ev_need = []
                for i, u in enumerate(units):
                    ev_need.append(ACT_PFX[i + 1] if u["kind"] == "kv" else ACT_PFX[i] + 1)
                for i, u in enumerate(units):
                    j = u["j"]; g = u["g"]
                    t.wait_ge(wl[g % 2], WLP[g])
                    t.wait_ge(xf, 16 * HC * (j + 1))
                    if i >= 4:
                        t.wait_ge(acs, ev_need[i - 4])
                    bank = i % 4
                    for c in range(HC):
                        t.matmul(ps[bank][:, :],
                                 xnf[:, c * 1024 + u["ti"] * 128: c * 1024 + u["ti"] * 128 + 128],
                                 wsl[g % 2][:, c * 512:(c + 1) * 512],
                                 start=(c == 0), stop=(c == HC - 1)).then_inc(mm, 0 if c < HC - 1 else 1)

            @blk.sync
            def _(s2):
                s2.dma_start(out=cs.ap().rearrange("p (n d) -> p n d", d=64),
                             in_=cos_in.rearrange("(n p) d -> p n d", p=128)).then_inc(ldc, 16)
                s2.dma_start(out=sn.ap().rearrange("p (n d) -> p n d", d=64),
                             in_=sin_in.rearrange("(n p) d -> p n d", p=128)).then_inc(ldc, 16)
                for i in range(NTT):
                    if i >= 2:
                        s2.wait_ge(sst, 5 * (i - 1))
                    s2.dma_start(out=xt[i % 2][:, :],
                                 in_=x2_in[i * 128:(i + 1) * 128, :]).then_inc(ldx[i % 2], 16)
                    if i >= 1:
                        s2.wait_ge(sst, 5 * i)
                        s2.dma_start(out=xn_hbm[(i - 1) * 128:i * 128, :],
                                     in_=xn_t[:, :]).then_inc(sxn[(i - 1) // 8], 16)
                s2.wait_ge(sst, 5 * NTT)
                s2.dma_start(out=xn_hbm[(NTT - 1) * 128:NTT * 128, :],
                             in_=xn_t[:, :]).then_inc(sxn[1], 16)
                s2.wait_ge(wc, 128)

                def slab_load(g):
                    uc2 = g % 12
                    if uc2 < 8:
                        s2.dma_start(out=wsl[g % 2].ap().rearrange("p (c o) -> p c o", o=512),
                                     in_=w_bf[:, uc2 * 512:(uc2 + 1) * 512]
                                     .rearrange("(c p) o -> p c o", p=128)).then_inc(wl[g % 2], 16)
                    else:
                        m2 = uc2 - 8
                        s2.dma_start(out=wsl[g % 2].ap().rearrange("p (c g o) -> p c g o", c=HC, g=2)[:, :, 0, :],
                                     in_=w_bf[:, 4096 + 256 * m2:4096 + 256 * m2 + 256]
                                     .rearrange("(c p) o -> p c o", p=128)).then_inc(wl[g % 2], 16)
                        s2.dma_start(out=wsl[g % 2].ap().rearrange("p (c g o) -> p c g o", c=HC, g=2)[:, :, 1, :],
                                     in_=w_bf[:, 5120 + 256 * m2:5120 + 256 * m2 + 256]
                                     .rearrange("(c p) o -> p c o", p=128)).then_inc(wl[g % 2], 16)

                slab_load(0)
                slab_load(1)
                for i, u in enumerate(units):
                    par = i % 2
                    tr = u["trow"]
                    if u["kind"] == "q":
                        s2.wait_ge(vd, DVE_PFX[i] + 3)
                        s2.dma_start(out=qs_o[tr:tr + 128, u["uc"] * 512:(u["uc"] + 1) * 512],
                                     in_=qso[par][:, :]).then_inc(sop[par], 16)
                        s2.wait_ge(vd, DVE_PFX[i + 1])
                        s2.wait_ge(rpp, 2 * (i + 1))
                        s2.dma_start(out=sq_o[tr:tr + 128, u["uc"] * 512:(u["uc"] + 1) * 512],
                                     in_=sqo[par][:, :]).then_inc(sop[par], 16)
                    else:
                        m = u["uc"] - 8
                        s2.wait_ge(vd, DVE_PFX[i] + 3)
                        s2.dma_start(out=ks_o[tr:tr + 128, m * 256:(m + 1) * 256],
                                     in_=qso[par][:, 0:256]).then_inc(sop[par], 16)
                        s2.wait_ge(vd, DVE_PFX[i] + 6)
                        s2.dma_start(out=gs_o[tr:tr + 128, m * 256:(m + 1) * 256],
                                     in_=gso[par][:, :]).then_inc(sop[par], 16)
                        s2.wait_ge(vd, DVE_PFX[i + 1])
                        s2.wait_ge(rpp, 2 * (i + 1))
                        s2.dma_start(out=sk_o[tr:tr + 128, m * 256:(m + 1) * 256],
                                     in_=sqo[par][:, 0:256]).then_inc(sop[par], 16)
                        s2.wait_ge(acs, ACT_PFX[i + 1])
                        s2.dma_start(out=vv_o[tr:tr + 128, m * 256:(m + 1) * 256],
                                     in_=vvo[par][:, 0:256]).then_inc(sop[par], 16)
                    if i % 8 == 7 and i // 8 + 2 < 24:
                        s2.wait_ge(mm, 8 * (i // 8 + 1))
                        slab_load(i // 8 + 2)
                s2.wait_ge(sop[0], SOP[190])
                s2.wait_ge(sop[1], SOP[191])
                s2.wait_ge(sxn[0], 128)
                s2.wait_ge(sxn[1], 128)
    return nc

def build_l1_gla(limit=128, strip=()):
    nc = new_nc()
    S4 = 4096
    qs_in = nc.dram_tensor("qs4", [S4, 2048], BF16, kind="ExternalInput").ap()
    ks_in = nc.dram_tensor("ks4", [S4, 512], BF16, kind="ExternalInput").ap()
    gs_in = nc.dram_tensor("gs4", [S4, 512], BF16, kind="ExternalInput").ap()
    vv_in = nc.dram_tensor("vv4", [S4, 512], BF16, kind="ExternalInput").ap()
    ol_o = nc.dram_tensor("ol", [16, 128, S4], BF16, kind="ExternalOutput").ap()
    cm = np.zeros((128, 128), dtype=BF)
    tk = np.arange(128)[:, None]; tq = np.arange(128)[None, :]
    cm[:, :] = (tk <= tq).astype(BF)
    cmask_h = nc.inline_tensor(cm, name="cmask128")
    ident_h = nc.inline_tensor(np.eye(128, dtype=BF), name="ident128b")
    NCH = 32

    with ExitStack() as ctx:
        e = ctx.enter_context
        qsf = [e(nc.sbuf_tensor(f"qsf{i}", [128, 4 * S4], BF16)) for i in range(2)]
        ksf = [e(nc.sbuf_tensor(f"ksf{i}", [128, S4], BF16)) for i in range(2)]
        gsf = [e(nc.sbuf_tensor(f"gsf{i}", [128, S4], BF16)) for i in range(2)]
        vsb = [e(nc.sbuf_tensor(f"vsb{i}", [128, S4], BF16)) for i in range(2)]
        ones1 = e(nc.sbuf_tensor("ones1", [128, 128], F32))
        cm1 = e(nc.sbuf_tensor("cm1", [128, 1], F32))
        cmask = e(nc.sbuf_tensor("cmask", [128, 128], BF16))
        ident = e(nc.sbuf_tensor("identb", [128, 128], BF16))
        Gf = [e(nc.sbuf_tensor(f"Gf{i}", [128, 128], F32)) for i in range(2)]
        gcol = [e(nc.sbuf_tensor(f"gcol{i}", [128, 2], F32)) for i in range(2)]
        e1 = [e(nc.sbuf_tensor(f"e1_{i}", [128, 128], F32)) for i in range(2)]
        e2 = [e(nc.sbuf_tensor(f"e2_{i}", [128, 128], F32)) for i in range(2)]
        eg = [e(nc.sbuf_tensor(f"eg_{i}", [128, 128], F32)) for i in range(2)]
        dtot = [e(nc.sbuf_tensor(f"dtot{i}", [128, 1], F32)) for i in range(2)]
        ktl = [e(nc.sbuf_tensor(f"ktl{i}", [128, 128], BF16)) for i in range(2)]
        ktm = [e(nc.sbuf_tensor(f"ktm{i}", [128, 128], BF16)) for i in range(2)]
        qtl = [e(nc.sbuf_tensor(f"qtl{i}", [128, 512], BF16)) for i in range(2)]
        qgl = [e(nc.sbuf_tensor(f"qgl{i}", [128, 512], BF16)) for i in range(2)]
        atm = [e(nc.sbuf_tensor(f"atm{i}", [128, 512], BF16)) for i in range(2)]
        Ssb = e(nc.sbuf_tensor("Ssb", [128, 512], F32))
        Sbf = e(nc.sbuf_tensor("Sbf", [128, 512], BF16))
        olt = [e(nc.sbuf_tensor(f"olt{i}", [128, 512], BF16)) for i in range(2)]
        pat = [e(nc.psum_tensor(f"pat{i}", [128, 128], F32)) for i in range(2)]
        po = [e(nc.psum_tensor(f"po{i}", [128, 512], F32)) for i in range(2)]
        ptk = e(nc.psum_tensor("ptk", [128, 128], BF16))
        pkv = e(nc.psum_tensor("pkv", [128, 128], F32))

        ldc = e(nc.semaphore("ldc"))
        sl = [e(nc.semaphore(f"sl{i}")) for i in range(2)]
        vg1 = e(nc.semaphore("vg1"))
        ae = e(nc.semaphore("ae"))
        vk = e(nc.semaphore("vk"))
        tmk = e(nc.semaphore("tmk"))
        vtm = e(nc.semaphore("vtm"))
        ma = e(nc.semaphore("ma"))
        vat = e(nc.semaphore("vat"))
        mo = e(nc.semaphore("mo"))
        mkv = e(nc.semaphore("mkv"))
        vsc = e(nc.semaphore("vsc"))
        ape = e(nc.semaphore("ape"))
        sst = [e(nc.semaphore(f"sst{i}")) for i in range(2)]

        NUNIT = 4 * NCH

        with nc.Block() as blk:
            @blk.gpsimd
            def _(g):
                pass

            @blk.vector
            def _(v):
                v.memset(ones1[:, :], 1.0)
                v.memset(cm1[:, 0:1], -1.0)
                v.memset(Ssb[:, :], 0.0)
                v.memset(Sbf[:, :], 0.0).then_inc(vg1, 1)
                nv = 1
                v.wait_ge(vg1, 1)
                v.wait_ge(ldc, 32)
                for kvh in range(4):
                    for ci in range(NCH):
                        u = kvh * NCH + ci
                        if u >= limit:
                            break
                        par = u % 2
                        kp = kvh % 2
                        v.wait_ge(sl[kp], 112 * (kvh // 2 + 1))
                        if u >= 2:
                            v.wait_ge(ae, 4 * (u - 1))
                        if "scn" in strip:
                            v.nop().then_inc(vg1, 1)
                        else:
                            v.tensor_tensor_scan(Gf[par][:, :], gsf[kp][:, ci * 128:(ci + 1) * 128],
                                                 ones1[:, :], 0.0, ALU.add, ALU.mult).then_inc(vg1, 1)
                        nv += 1
                        v.wait_ge(vg1, nv)
                        v.tensor_scalar(gcol[par][:, 0:1], Gf[par][:, 127:128], -1.0, None, ALU.mult).then_inc(vg1, 1)
                        nv += 1
                        v.wait_ge(ae, 4 * u + 4)
                        if u >= 2:
                            v.wait_ge(tmk, u - 1)
                        v.tensor_tensor(ktl[par][:, :], ksf[kp][:, ci * 128:(ci + 1) * 128],
                                        e2[par][:, :], ALU.mult).then_inc(vk, 1)
                        if u >= 2:
                            v.wait_ge(ma, 4 * (u - 1))
                            v.wait_ge(mo, 4 * (u - 1))
                        for hh in range(4):
                            qv = qsf[kp].ap()[:, hh * S4 + ci * 128: hh * S4 + (ci + 1) * 128]
                            v.tensor_tensor(qtl[par][:, hh * 128:(hh + 1) * 128], qv, e1[par][:, :], ALU.mult)
                            ii = v.tensor_tensor(qgl[par][:, hh * 128:(hh + 1) * 128], qv, eg[par][:, :], ALU.mult)
                        ii.then_inc(vk, 1)
                        v.wait_ge(tmk, u + 1)
                        v.tensor_copy(ktm[par][:, :], ptk[:, :]).then_inc(vtm, 1)
                        if u >= 2:
                            v.wait_ge(mo, 4 * (u - 1))
                        for hh in range(4):
                            v.wait_ge(ma, 4 * u + hh + 1)
                            v.tensor_tensor(atm[par][:, hh * 128:(hh + 1) * 128],
                                            pat[hh % 2][:, :], cmask[:, :], ALU.mult).then_inc(vat, 1)
                        v.wait_ge(mkv, u + 1)
                        v.wait_ge(mo, 4 * u + 4)
                        sw = Ssb[:, kvh * 128:(kvh + 1) * 128]
                        v.tensor_scalar(sw, sw, dtot[par][:, 0:1], None, ALU.mult).then_inc(vsc, 1)
                        v.wait_ge(vsc, 3 * u + 1)
                        v.tensor_tensor(sw, sw, pkv[:, :], ALU.add).then_inc(vsc, 1)
                        v.wait_ge(vsc, 3 * u + 2)
                        v.tensor_copy(Sbf[:, kvh * 128:(kvh + 1) * 128], sw).then_inc(vsc, 1)

            @blk.scalar
            def _(s):
                for kvh in range(4):
                    for ci in range(NCH):
                        u = kvh * NCH + ci
                        if u >= limit:
                            break
                        par = u % 2
                        s.wait_ge(vg1, 1 + 2 * (u + 1))
                        if u >= 2:
                            s.wait_ge(vk, 2 * (u - 1))
                            s.wait_ge(vsc, 3 * (u - 2) + 1)
                        if "ex" in strip:
                            s.nop().then_inc(ae, 4)
                        else:
                            s.activation(e1[par][:, :], Gf[par][:, :], AF.Exp, bias=gcol[par][:, 0:1]).then_inc(ae, 1)
                            s.activation(e2[par][:, :], Gf[par][:, :], AF.Exp, scale=cm1[:, 0:1],
                                         bias=Gf[par][:, 127:128]).then_inc(ae, 1)
                            s.activation(eg[par][:, :], Gf[par][:, :], AF.Exp).then_inc(ae, 1)
                            s.activation(dtot[par][:, 0:1], Gf[par][:, 127:128], AF.Exp).then_inc(ae, 1)
                        s.wait_ge(mo, 4 * u + 4)
                        if u >= 2:
                            s.wait_ge(sst[u % 2], 64 * ((u - 2) // 2 + 1))
                        if "ev" in strip:
                            s.nop().then_inc(ape, 4)
                        else:
                            s.activation(olt[par][:, :], po[par][:, :], AF.Identity).then_inc(ape, 4)

            @blk.tensor
            def _(t):
                t.wait_ge(ldc, 32)
                for kvh in range(4):
                    for ci in range(NCH):
                        u = kvh * NCH + ci
                        if u >= limit:
                            break
                        par = u % 2
                        kp = kvh % 2
                        t.wait_ge(vk, 2 * u + 1)
                        if u >= 1:
                            t.wait_ge(vtm, u)
                        t.transpose(ptk[:, :], ktl[par][:, :], ident[:, :]).then_inc(tmk, 1)
                        t.wait_ge(vk, 2 * u + 2)
                        for hh in range(4):
                            if u >= 1 or hh >= 2:
                                t.wait_ge(vat, 4 * u + hh - 1)
                            t.matmul(pat[hh % 2][:, :], ktl[par][:, :],
                                     qtl[par][:, hh * 128:(hh + 1) * 128],
                                     start=True, stop=True).then_inc(ma, 1)
                        for hh in range(4):
                            t.wait_ge(vat, 4 * u + hh + 1)
                            if ci > 0:
                                t.wait_ge(vsc, 3 * u)
                            if u >= 2:
                                t.wait_ge(ape, 4 * (u - 1))
                            ii = t.matmul(po[par][:, hh * 128:(hh + 1) * 128],
                                          vsb[kp][:, ci * 128:(ci + 1) * 128],
                                          atm[par][:, hh * 128:(hh + 1) * 128],
                                          start=True, stop=(ci == 0))
                            if ci > 0:
                                t.matmul(po[par][:, hh * 128:(hh + 1) * 128],
                                         Sbf[:, kvh * 128:(kvh + 1) * 128],
                                         qgl[par][:, hh * 128:(hh + 1) * 128],
                                         start=False, stop=True).then_inc(mo, 1)
                            else:
                                ii.then_inc(mo, 1)
                        t.wait_ge(vtm, u + 1)
                        if u >= 1:
                            t.wait_ge(vsc, 3 * u - 1)
                        t.matmul(pkv[:, :], ktm[par][:, :],
                                 vsb[kp][:, ci * 128:(ci + 1) * 128],
                                 start=True, stop=True).then_inc(mkv, 1)

            @blk.sync
            def _(s2):
                s2.dma_start(out=cmask[:, :], in_=cmask_h.ap()[:, :]).then_inc(ldc, 16)
                s2.dma_start(out=ident[:, :], in_=ident_h.ap()[:, :]).then_inc(ldc, 16)
                for kvh in range(min(4, (limit + NCH - 1) // NCH)):
                    kp = kvh % 2
                    if kvh >= 2:
                        s2.wait_ge(mo, 4 * NCH * (kvh - 1))
                    s2.dma_start_transpose(gsf[kp][:, :], gs_in[:, kvh * 128:(kvh + 1) * 128]).then_inc(sl[kp], 16)
                    s2.dma_start_transpose(ksf[kp][:, :], ks_in[:, kvh * 128:(kvh + 1) * 128]).then_inc(sl[kp], 16)
                    for hh in range(4):
                        s2.dma_start_transpose(qsf[kp][:, hh * S4:(hh + 1) * S4],
                                               qs_in[:, (kvh * 4 + hh) * 128:(kvh * 4 + hh + 1) * 128]).then_inc(sl[kp], 16)
                    s2.dma_start(out=vsb[kp].ap().rearrange("p (c o) -> p c o", o=128),
                                 in_=vv_in[:, kvh * 128:(kvh + 1) * 128]
                                 .rearrange("(c p) o -> p c o", p=128)).then_inc(sl[kp], 16)
                    for ci in range(NCH):
                        u = kvh * NCH + ci
                        if u >= limit:
                            break
                        par = u % 2
                        s2.wait_ge(ape, 4 * (u + 1))
                        for hh in range(4):
                            if "st" in strip:
                                s2.sem_inc(sst[par], 16)
                            else:
                                s2.dma_start(out=ol_o[kvh * 4 + hh, :, ci * 128:(ci + 1) * 128],
                                             in_=olt[par][:, hh * 128:(hh + 1) * 128]).then_inc(sst[par], 16)
                s2.wait_ge(sst[0], 64 * ((limit + 1) // 2))
                s2.wait_ge(sst[1], 64 * (limit // 2))
    return nc

def build_l1_swa(debug=False):
    nc = new_nc()
    EXT = 2176
    sq_in = nc.dram_tensor("sq", [T, 4096], BF16, kind="ExternalInput").ap()
    sk_in = nc.dram_tensor("ske", [EXT, 1024], BF16, kind="ExternalInput").ap()
    vv_in = nc.dram_tensor("vve", [EXT, 1024], BF16, kind="ExternalInput").ap()
    ol_in = nc.dram_tensor("oli", [32, 128, T], BF16, kind="ExternalInput").ap()
    bnd_in = nc.dram_tensor("bndc", [128, 1], F32, kind="ExternalInput").ap()
    x2_in = nc.dram_tensor("x2", [T, D], F32, kind="ExternalInput").ap()
    wot = nc.dram_tensor("wot", [D, D], F32, kind="ExternalInput").ap()
    x3_out = nc.dram_tensor("x3", [T, D], F32, kind="ExternalOutput").ap()
    ogd = (nc.dram_tensor("ogd", [128, 32 * 512], BF16, kind="ExternalOutput").ap()
           if debug == 1 else None)
    aod = (nc.dram_tensor("aod", [4, 128, D], BF16, kind="ExternalOutput").ap()
           if debug == 1 else None)
    if debug == 2:
        d2 = {n: nc.dram_tensor(n, [128, w], dt, kind="ExternalOutput").ap()
              for n, w, dt in [("dq", 512, BF16), ("dk", 640, BF16), ("dv", 640, BF16),
                               ("de0", 512, BF16), ("de1", 512, BF16), ("drb", 512, BF16),
                               ("dt2", 512, F32), ("dt3", 512, F32), ("dol", 512, BF16),
                               ("dog", 512, BF16)]}
    wo_bf = nc.dram_tensor("wo_bf", [D, D], BF16, kind="Internal").ap()
    ones_h = nc.inline_tensor(np.ones((1, 128), dtype=BF), name="ones_row")
    ones_c = nc.inline_tensor(np.ones((128, 1), dtype=BF), name="ones_col")
    tk = np.arange(128)[:, None]; tq = np.arange(512)[None, :]
    bands = []
    for s5 in range(5):
        rel = 128 - 128 * s5 + tq - tk
        bands.append(((rel >= 0) & (rel <= 64)).astype(BF))
    mAB_h = nc.inline_tensor(np.concatenate(bands, 1), name="mB5")

    with ExitStack() as ctx:
        e = ctx.enter_context
        skf = [e(nc.sbuf_tensor(f"skf{i}", [128, 640], BF16)) for i in range(2)]
        vsb = [e(nc.sbuf_tensor(f"vsb{i}", [128, 640], BF16)) for i in range(2)]
        qsl = [e(nc.sbuf_tensor(f"qsl{i}", [128, 512], BF16)) for i in range(2)]
        olt = [e(nc.sbuf_tensor(f"olt{i}", [128, 512], BF16)) for i in range(2)]
        expT = [e(nc.sbuf_tensor(f"expT{i}", [128, 512], BF16)) for i in range(4)]
        mAB = e(nc.sbuf_tensor("mAB_sb", [128, 5 * 512], BF16))
        bnd = e(nc.sbuf_tensor("bnd_sb", [128, 2], F32))
        smc = e(nc.sbuf_tensor("smc", [128, 1], F32))
        onesr = e(nc.sbuf_tensor("onesr", [1, 128], BF16))
        onesc = e(nc.sbuf_tensor("onesc", [128, 1], BF16))
        rcp = e(nc.sbuf_tensor("rcp", [1, 512], F32))
        rcpb = e(nc.sbuf_tensor("rcpb", [1, 512], BF16))
        rb_sb = e(nc.sbuf_tensor("rb_sb", [128, 512], BF16))
        t2b = e(nc.sbuf_tensor("t2b", [128, 512], F32))
        t3b = e(nc.sbuf_tensor("t3b", [128, 512], F32))
        og = e(nc.sbuf_tensor("og", [128, 32 * 512], BF16))
        wdsl = e(nc.sbuf_tensor("wdsl", [128, HC * 512], BF16))
        ao = [e(nc.sbuf_tensor(f"ao{i}", [128, D], BF16)) for i in range(4)]
        xt = [e(nc.sbuf_tensor("xt0", [128, D], F32))]
        tmpc = e(nc.sbuf_tensor("tmpc", [128, D], F32))
        bns = e(nc.sbuf_tensor("bns", [128, 48], F32))
        mv2 = [e(nc.sbuf_tensor(f"mv2{i}", [128, 2], F32)) for i in range(2)]
        sc2 = e(nc.sbuf_tensor("sc2", [128, 4], F32))
        consts = e(nc.sbuf_tensor("consts", [128, 4], F32))
        scb = [e(nc.psum_tensor(f"scb{i}", [128, 512], F32)) for i in range(4)]
        cps = [e(nc.psum_tensor(f"cps{i}", [128, 512], F32)) for i in range(2)]
        su = e(nc.psum_tensor("su", [33, 512], F32))
        rbp = e(nc.psum_tensor("rbp", [128, 512], F32))
        
        ldc = e(nc.semaphore("ldc"))
        wdc = e(nc.semaphore("wdc"))
        kgl = [e(nc.semaphore(f"kgl{i}")) for i in range(2)]
        qls = [e(nc.semaphore(f"qls{i}")) for i in range(2)]
        oll = [e(nc.semaphore(f"oll{i}")) for i in range(2)]
        msc = e(nc.semaphore("msc"))
        mpv = e(nc.semaphore("mpv"))
        msu = e(nc.semaphore("msu"))
        mrb = e(nc.semaphore("mrb"))
        exps = e(nc.semaphore("exps"))
        dmk = e(nc.semaphore("dmk"))
        vnr = e(nc.semaphore("vnr"))
        wdl = e(nc.semaphore("wdl"))
        mdd = e(nc.semaphore("mdd"))
        dev = e(nc.semaphore("dev"))
        vst2 = e(nc.semaphore("vst2"))
        sst2 = e(nc.semaphore("sst2"))
        cmb = e(nc.semaphore("cmb"))
        cmb2 = e(nc.semaphore("cmb2"))
        xld = e(nc.semaphore("xld"))
        sx1 = e(nc.semaphore("sx1"))
        dbg = e(nc.semaphore("dbg"))

        NHG = 128

        with nc.Block() as blk:
            @blk.gpsimd
            def _(g):
                for c in range(4):
                    g.dma_start(out=wo_bf[c * 1024:(c + 1) * 1024, :],
                                in_=wot[c * 1024:(c + 1) * 1024, :]).then_inc(wdc, 16)
                if debug == 1:
                    g.wait_ge(vnr, 160)
                    g.dma_start(out=ogd[:, :], in_=og[:, :]).then_inc(dbg, 16)
                    g.wait_ge(dev, 32)
                    for tt in range(4):
                        g.dma_start(out=aod[tt, :, :], in_=ao[tt][:, :]).then_inc(dbg, 16)
                if debug == 2:
                    g.wait_ge(vnr, 5)
                    for n, src in [("dq", qsl[0][:, :]), ("dk", skf[0][:, :]),
                                   ("dv", vsb[0][:, :]), ("de0", expT[0][:, :]),
                                   ("de1", expT[1][:, :]), ("drb", rb_sb[:, :]),
                                   ("dt2", t2b[:, :]), ("dt3", t3b[:, :]),
                                   ("dol", olt[0][:, :]), ("dog", og[:, 0:512])]:
                        g.dma_start(out=d2[n][:, :], in_=src).then_inc(dbg, 16)
                for gt in range(16):
                    g.wait_ge(cmb, gt + 1)
                    g.wait_ge(xld, 16 * (gt + 1))
                    if gt >= 1:
                        g.wait_ge(sx1, 16 * gt)
                    g.tensor_tensor(tmpc[:, :], xt[0][:, :], tmpc[:, :], ALU.add).then_inc(cmb2, 1)

            @blk.vector
            def _(v):
                v.memset(consts[:, 0:1], 1e-5)
                v.memset(consts[:, 1:2], -0.5)
                v.memset(consts[:, 2:3], -1.0)
                v.memset(consts[:, 3:4], 0.5)
                v.memset(smc[:, 0:1], 1.0 / np.sqrt(128.0))
                v.memset(bnd[:, 1:2], 0.0).then_inc(vst2, 1)
                v.wait_ge(ldc, 64)
                for hg in range(NHG):
                    grp = hg // 32
                    head = hg % 32
                    par = hg % 2
                    if debug == 2 and hg == 1:
                        v.wait_ge(dbg, 160)
                    for s in range(5):
                        v.wait_ge(exps, 5 * hg + s + 1)
                        eb = (5 * hg + s) % 4
                        v.tensor_tensor(expT[eb][:, :], expT[eb][:, :],
                                        mAB[:, s * 512:(s + 1) * 512],
                                        ALU.mult).then_inc(dmk, 1)
                    v.wait_ge(msu, hg + 1)
                    v.wait_ge(vnr, 5 * hg)
                    v.reciprocal(rcp[0:1, :], su[32 * par:32 * par + 1, :]).then_inc(vnr, 1)
                    v.wait_ge(vnr, 5 * hg + 1)
                    if hg >= 1:
                        v.wait_ge(mrb, hg)
                    v.tensor_scalar(rcpb[0:1, :], rcp[0:1, :], 0.5, None, ALU.mult).then_inc(vnr, 1)
                    v.wait_ge(mrb, hg + 1)
                    v.tensor_copy(rb_sb[:, :], rbp[:, :]).then_inc(vnr, 1)
                    v.wait_ge(mpv, 5 * (hg + 1))
                    v.wait_ge(vnr, 5 * hg + 3)
                    if grp >= 1:
                        v.wait_ge(mdd, 32 * grp)
                    v.wait_ge(oll[par], 16 * (hg // 2 + 1))
                    v.tensor_tensor(t2b[:, :], cps[par][:, :], rb_sb[:, :], ALU.mult)
                    v.tensor_scalar(t3b[:, :], olt[par][:, :], 0.5, None, ALU.mult).then_inc(vnr, 1)
                    v.wait_ge(vnr, 5 * hg + 4)
                    if debug == 1 and 32 <= hg < 64:
                        v.wait_ge(dbg, 80)
                    v.tensor_tensor(og[:, head * 512:(head + 1) * 512],
                                    t2b[:, :], t3b[:, :], ALU.add).then_inc(vnr, 1)
                    if hg % 32 == 31:
                        gg = grp
                        for tt in range(4):
                            gt = gg * 4 + tt
                            v.wait_ge(dev, 32 * gg + 28 + tt + 1)
                            if gt >= 2:
                                v.wait_ge(sst2, 5 * (gt - 2) + 3)
                            if gt >= 1:
                                v.wait_ge(vst2, 1 + 2 * gt)
                            for c in range(8):
                                ii = v.bn_stats(bns[:, 6 * c:6 * c + 6], ao[tt][:, 512 * c:512 * (c + 1)])
                            ii.then_inc(vst2, 1)
                            v.wait_ge(vst2, 1 + 2 * gt + 1)
                            v.bn_aggr(mv2[gt % 2][:, :], bns[:, :]).then_inc(vst2, 1)
                            v.wait_ge(sst2, 5 * gt + 4)
                            if gt >= 1:
                                v.wait_ge(sx1, 16 * gt)
                            v.tensor_scalar(tmpc[:, :], ao[tt][:, :], mv2[gt % 2][:, 0:1],
                                            sc2[:, 1:2], ALU.subtract, ALU.mult).then_inc(cmb, 1)

            @blk.scalar
            def _(s):
                s.wait_ge(ldc, 64)
                s.wait_ge(vst2, 1)
                s.dma_start_transpose(skf[0][:, :],
                                      sk_in[0:640, 0:128]).then_inc(kgl[0], 16)
                for hg in range(NHG):
                    grp = hg // 32
                    head = hg % 32
                    par = hg % 2
                    if debug == 2 and hg == 1:
                        s.wait_ge(dbg, 160)
                    if hg % 4 == 0 and hg // 4 + 1 < 32:
                        Kn = hg // 4 + 1
                        if Kn >= 2:
                            s.wait_ge(mpv, 5 * 4 * (Kn - 1))
                        s.dma_start_transpose(skf[Kn % 2][:, :],
                                              sk_in[(Kn // 8) * 512:(Kn // 8) * 512 + 640,
                                                    (Kn % 8) * 128:(Kn % 8 + 1) * 128]).then_inc(kgl[Kn % 2], 16)
                    if hg >= 2:
                        s.wait_ge(msc, 5 * (hg - 1))
                    s.dma_start_transpose(qsl[hg % 2][:, :],
                                          sq_in[grp * 512:(grp + 1) * 512,
                                                head * 128:(head + 1) * 128]).then_inc(qls[hg % 2], 16)
                    for seq in range(5):
                        gs = 5 * hg + seq
                        s.wait_ge(msc, gs + 1)
                        if gs >= 4:
                            s.wait_ge(mpv, gs - 3)
                        bcol = bnd[:, 0:1] if (grp == 0 and seq == 0) else bnd[:, 1:2]
                        s.activation(expT[gs % 4][:, :], scb[gs % 4][:, :], AF.Exp,
                                     scale=smc[:, 0:1], bias=bcol).then_inc(exps, 1)
                    if hg >= 2:
                        s.wait_ge(vnr, 5 * (hg - 2) + 5)
                    s.dma_start(out=olt[par][:, :],
                                in_=ol_in[head, :, grp * 512:(grp + 1) * 512]).then_inc(oll[par], 16)
                    if hg % 32 == 31:
                        gg = grp
                        for oc in range(8):
                            for tt in range(4):
                                k = 32 * gg + 4 * oc + tt
                                s.wait_ge(mdd, k + 1)
                                if debug == 1 and k == 32:
                                    s.wait_ge(dbg, 80)
                                if (gg >= 1 or oc >= 1) and oc == 0:
                                    s.wait_ge(cmb, 4 * (gg - 1) + tt + 1)
                                s.activation(ao[tt][:, oc * 512:(oc + 1) * 512],
                                             scb[k % 4][:, :], AF.Identity).then_inc(dev, 1)
                        for tt in range(4):
                            gt = gg * 4 + tt
                            s.wait_ge(vst2, 1 + 2 * (gt + 1))
                            s.activation(sc2[:, 0:1], mv2[gt % 2][:, 1:2], AF.Ln, bias=consts[:, 0:1]).then_inc(sst2, 1)
                            s.wait_ge(sst2, 5 * gt + 1)
                            s.activation(sc2[:, 1:2], sc2[:, 0:1], AF.Exp, scale=consts[:, 1:2]).then_inc(sst2, 1)
                            s.wait_ge(sst2, 5 * gt + 2)
                            s.activation(sc2[:, 2:3], sc2[:, 1:2], AF.Identity).then_inc(sst2, 1)
                            s.wait_ge(sst2, 5 * gt + 3)
                            s.activation(sc2[:, 2:3], sc2[:, 2:3], AF.Identity).then_inc(sst2, 1)
                            s.wait_ge(sst2, 5 * gt + 4)
                            s.activation(sc2[:, 2:3], sc2[:, 2:3], AF.Identity).then_inc(sst2, 1)

            @blk.tensor
            def _(t):
                t.wait_ge(ldc, 64)
                for hg in range(NHG):
                    grp = hg // 32
                    kvi = (hg % 32) // 4
                    par = hg % 2
                    K = grp * 8 + kvi
                    t.wait_ge(kgl[K % 2], 32 * (K // 2 + 1))
                    t.wait_ge(qls[hg % 2], 16 * (hg // 2 + 1))
                    if grp >= 1 and hg % 32 == 0:
                        t.wait_ge(dev, 32 * grp)

                    def score_mm(seq):
                        gs = 5 * hg + seq
                        if gs >= 4:
                            t.wait_ge(exps, gs - 3)
                        t.matmul(scb[gs % 4][:, :], skf[K % 2][:, seq * 128:(seq + 1) * 128],
                                 qsl[hg % 2][:, :], start=True, stop=True).then_inc(msc, 1)

                    def pv_mm(seq):
                        gs = 5 * hg + seq
                        t.wait_ge(exps, gs + 1)
                        t.wait_ge(dmk, gs + 1)
                        if seq == 0:
                            if hg >= 2:
                                t.wait_ge(vnr, 5 * (hg - 2) + 1)
                                t.wait_ge(vnr, 5 * (hg - 2) + 4)
                        jj = t.matmul(su[32 * par:32 * par + 1, :], onesc[:, 0:1],
                                      expT[gs % 4][:, :], start=(seq == 0), stop=(seq == 4))
                        if seq == 4:
                            jj.then_inc(msu, 1)
                        t.matmul(cps[par][:, :],
                                 vsb[K % 2][:, seq * 128:(seq + 1) * 128],
                                 expT[gs % 4][:, :], start=(seq == 0), stop=(seq == 4)).then_inc(mpv, 1)

                    def rb_mm(u):
                        t.wait_ge(vnr, 5 * u + 2)
                        if u >= 1:
                            t.wait_ge(vnr, 5 * (u - 1) + 3)
                        t.matmul(rbp[:, :], onesr[0:1, :], rcpb[0:1, :], start=True, stop=True).then_inc(mrb, 1)

                    score_mm(0)
                    score_mm(1)
                    score_mm(2)
                    if hg >= 1 and hg % 32 != 0:
                        rb_mm(hg - 1)
                    score_mm(3)
                    pv_mm(0)
                    score_mm(4)
                    pv_mm(1)
                    pv_mm(2)
                    pv_mm(3)
                    pv_mm(4)
                    if hg % 32 == 31:
                        rb_mm(hg)
                        gg = grp
                        t.wait_ge(vnr, 5 * 32 * (gg + 1))
                        for oc in range(8):
                            t.wait_ge(wdl, 16 * (8 * gg + oc + 1))
                            for tt in range(4):
                                k = 32 * gg + 4 * oc + tt
                                bank = k % 4
                                if k >= 4:
                                    t.wait_ge(dev, k - 3)
                                for hh in range(32):
                                    t.matmul(scb[bank][:, :],
                                             og[:, hh * 512 + tt * 128: hh * 512 + tt * 128 + 128],
                                             wdsl[:, hh * 512:(hh + 1) * 512],
                                             start=(hh == 0), stop=(hh == 31)).then_inc(mdd, 0 if hh < 31 else 1)

            @blk.sync
            def _(s2):
                s2.dma_start(out=onesr[0:1, :], in_=ones_h.ap()[0:1, :]).then_inc(ldc, 16)
                s2.dma_start(out=onesc[:, 0:1], in_=ones_c.ap()[:, 0:1]).then_inc(ldc, 16)
                s2.dma_start(out=mAB[:, :], in_=mAB_h.ap()[:, :]).then_inc(ldc, 16)
                s2.dma_start(out=bnd[:, 0:1], in_=bnd_in[:, 0:1]).then_inc(ldc, 16)
                for grp in range(4):
                    for kvi in range(8):
                        K = grp * 8 + kvi
                        if K >= 2:
                            s2.wait_ge(mpv, 5 * 4 * (K - 1))
                        s2.dma_start(out=vsb[K % 2].ap().rearrange("p (c o) -> p c o", o=128),
                                     in_=vv_in[grp * 512:grp * 512 + 640, kvi * 128:(kvi + 1) * 128]
                                     .rearrange("(c p) o -> p c o", p=128)).then_inc(kgl[K % 2], 16)
                    for oc in range(8):
                        G = grp * 8 + oc
                        if G >= 1:
                            s2.wait_ge(mdd, 4 * G)
                        else:
                            s2.wait_ge(wdc, 64)
                        s2.dma_start(out=wdsl.ap().rearrange("p (c o) -> p c o", o=512),
                                     in_=wo_bf[:, oc * 512:(oc + 1) * 512]
                                     .rearrange("(c p) o -> p c o", p=128)).then_inc(wdl, 16)
                    for tt in range(4):
                        gt = grp * 4 + tt
                        if gt >= 1:
                            s2.wait_ge(cmb2, gt)
                        s2.dma_start(out=xt[0][:, :],
                                     in_=x2_in[gt * 128:(gt + 1) * 128, :]).then_inc(xld, 16)
                        s2.wait_ge(cmb2, gt + 1)
                        s2.dma_start(out=x3_out[gt * 128:(gt + 1) * 128, :],
                                     in_=tmpc[:, :]).then_inc(sx1, 16)
                s2.wait_ge(sx1, 16 * 16)
    return nc

BSTAGES = [[(0,1),(2,3),(4,5),(6,7)], [(0,2),(1,3),(4,6),(5,7)], [(1,2),(5,6)],
           [(0,4),(1,5),(2,6),(3,7)], [(2,4),(3,5)], [(1,2),(3,4),(5,6)]]

def build_l1_ffn():
    nc = new_nc()
    x3_in = nc.dram_tensor("x3", [T, D], F32, kind="ExternalInput").ap()
    wbt = nc.dram_tensor("wb1t", [D, D], F32, kind="ExternalInput").ap()
    bbr = nc.dram_tensor("bb1r", [1, D], F32, kind="ExternalInput").ap()
    wrt = nc.dram_tensor("wrt", [D, 8], F32, kind="ExternalInput").ap()
    brr = nc.dram_tensor("brr", [1, 8], F32, kind="ExternalInput").ap()
    lat = nc.dram_tensor("lat", [D, 128], F32, kind="ExternalInput").ap()
    lbt = nc.dram_tensor("lbt", [128, D], F32, kind="ExternalInput").ap()
    xo_out = nc.dram_tensor("xo", [T, D], F32, kind="ExternalOutput").ap()
    xn_hbm = nc.dram_tensor("xnh", [T, D], BF16, kind="Internal").ap()
    wb_bf = nc.dram_tensor("wb_bf", [D, D], BF16, kind="Internal").ap()
    ones_h = nc.inline_tensor(np.ones((1, 128), dtype=BF), name="ones_row")
    ident_h = nc.inline_tensor(np.eye(128, dtype=BF), name="ident128")
    kv_h = nc.inline_tensor(np.tile(np.arange(1, 9, dtype=np.float32), (128, 1)),
                            name="kvec8")

    with ExitStack() as ctx:
        e = ctx.enter_context
        xt = [e(nc.sbuf_tensor("xt0", [128, D], F32))]
        bns = e(nc.sbuf_tensor("bns", [128, 48], F32))
        mv = [e(nc.sbuf_tensor(f"mv{i}", [128, 2], F32)) for i in range(2)]
        sc = e(nc.sbuf_tensor("sc", [128, 4], F32))
        consts = e(nc.sbuf_tensor("consts", [128, 4], F32))
        xn_t = e(nc.sbuf_tensor("xnt", [128, D], BF16))
        xnf = e(nc.sbuf_tensor("xnf", [128, HC * 512], BF16))
        wsl = e(nc.sbuf_tensor("wsl", [128, HC * 512], BF16))
        wr_b = e(nc.sbuf_tensor("wr_b", [128, HC * 8], BF16))
        la_b = e(nc.sbuf_tensor("la_b", [128, HC * 128], BF16))
        lb_b = e(nc.sbuf_tensor("lb_b", [128, D], BF16))
        bb_b = e(nc.sbuf_tensor("bb_b", [1, D], BF16))
        br_b = e(nc.sbuf_tensor("br_b", [1, 8], BF16))
        onesr = e(nc.sbuf_tensor("onesr", [1, 128], BF16))
        ident = e(nc.sbuf_tensor("identb", [128, 128], BF16))
        kvec = e(nc.sbuf_tensor("kvec", [128, 8], F32))
        ones8 = e(nc.sbuf_tensor("ones8", [128, 8], F32))
        zt = e(nc.sbuf_tensor("zt", [128, 8], F32))
        zs = e(nc.sbuf_tensor("zs", [128, 8], F32))
        zc = e(nc.sbuf_tensor("zc", [128, 8], F32))
        ztm = e(nc.sbuf_tensor("ztm", [128, 8], F32))
        sup = e(nc.sbuf_tensor("sup", [128, 8], F32))
        scr = e(nc.sbuf_tensor("scr", [128, 4], F32))
        wsb = e(nc.sbuf_tensor("wsb", [128, 8], F32))
        hsb = [e(nc.sbuf_tensor(f"hsb{i}", [128, 128], BF16)) for i in range(2)]
        hw = [e(nc.sbuf_tensor(f"hw{i}", [128, 128], BF16)) for i in range(2)]
        hwf = e(nc.sbuf_tensor("hwf", [128, 4 * 128], BF16))
        ao = [e(nc.sbuf_tensor(f"ao{i}", [128, D], BF16)) for i in range(4)]
        tmpc = e(nc.sbuf_tensor("tmpc", [128, D], F32))
        xr = e(nc.sbuf_tensor("xr", [128, D], F32))
        ps = [e(nc.psum_tensor(f"ps{i}", [128, 512], F32)) for i in range(2)]
        pr = e(nc.psum_tensor("pr", [128, 8], F32))
        ph = [e(nc.psum_tensor(f"ph{i}", [128, 128], F32)) for i in range(2)]
        pth = [e(nc.psum_tensor(f"pth{i}", [128, 128], BF16)) for i in range(2)]

        ldc = e(nc.semaphore("ldc"))
        ldx = e(nc.semaphore("ldx"))
        gc = e(nc.semaphore("gc"))
        wbl = e(nc.semaphore("wbl"))
        xfl = e(nc.semaphore("xfl"))
        vst = e(nc.semaphore("vst"))
        sst = e(nc.semaphore("sst"))
        mrr = e(nc.semaphore("mrr"))
        vsp = e(nc.semaphore("vsp"))
        mhh = e(nc.semaphore("mhh"))
        ahe = e(nc.semaphore("ahe"))
        vhw = e(nc.semaphore("vhw"))
        mht = e(nc.semaphore("mht"))
        mdd = e(nc.semaphore("mdd"))
        mev = e(nc.semaphore("mev"))
        vst2 = e(nc.semaphore("vst2"))
        sst2 = e(nc.semaphore("sst2"))
        cmb = e(nc.semaphore("cmb"))
        cmb2 = e(nc.semaphore("cmb2"))
        xld = e(nc.semaphore("xld"))
        sxn = [e(nc.semaphore(f"sxn{i}")) for i in range(2)]
        sxo = e(nc.semaphore("sxo"))

        SPN = 24

        with nc.Block() as blk:
            @blk.gpsimd
            def _(g):
                g.dma_start(out=bb_b[0:1, :], in_=bbr[0:1, :]).then_inc(gc, 16)
                g.dma_start(out=br_b[0:1, :], in_=brr[0:1, :]).then_inc(gc, 16)
                g.dma_start(out=wr_b.ap().rearrange("p (c r) -> p c r", r=8),
                            in_=wrt.rearrange("(c p) r -> p c r", p=128)).then_inc(gc, 16)
                g.dma_start(out=la_b.ap().rearrange("p (c r) -> p c r", r=128),
                            in_=lat.rearrange("(c p) r -> p c r", p=128)).then_inc(gc, 16)
                g.dma_start(out=lb_b[:, :], in_=lbt[:, :]).then_inc(gc, 16)
                for c in range(4):
                    g.dma_start(out=wb_bf[c * 1024:(c + 1) * 1024, :],
                                in_=wbt[c * 1024:(c + 1) * 1024, :]).then_inc(gc, 16)
                for gt in range(16):
                    g.wait_ge(cmb, gt + 1)
                    g.wait_ge(xld, 16 * (gt + 1))
                    if gt >= 1:
                        g.wait_ge(sxo, 16 * gt)
                    g.tensor_tensor(tmpc[:, :], xr[:, :], tmpc[:, :], ALU.add).then_inc(cmb2, 1)

            @blk.vector
            def _(v):
                v.memset(consts[:, 0:1], 1e-5)
                v.memset(consts[:, 1:2], -0.5)
                v.memset(consts[:, 2:3], -1.0)
                v.memset(consts[:, 3:4], 1.0)
                v.memset(ones8[:, :], 1.0).then_inc(vst, 1)
                for i in range(NTT):
                    v.wait_ge(ldx, 16 * (i + 1))
                    if i >= 1:
                        v.wait_ge(vst, 1 + 2 * i)
                    if i >= 2:
                        v.wait_ge(sst, 5 * (i - 2) + 3)
                    for c in range(8):
                        ii = v.bn_stats(bns[:, 6 * c:6 * c + 6], xt[0][:, 512 * c:512 * (c + 1)])
                    ii.then_inc(vst, 1)
                    v.wait_ge(vst, 1 + 2 * i + 1)
                    v.bn_aggr(mv[i % 2][:, :], bns[:, :]).then_inc(vst, 1)
                v.wait_ge(ldc, 48)
                nsp = 0
                for G in range(4):
                    for tt in range(4):
                        gt = G * 4 + tt
                        v.wait_ge(mrr, gt + 1)
                        v.tensor_copy(zt[:, :], pr[:, :]).then_inc(vsp, 1); nsp += 1
                        v.wait_ge(vsp, nsp)
                        v.tensor_copy(zs[:, :], zt[:, :]).then_inc(vsp, 1); nsp += 1
                        for stg in BSTAGES:
                            v.wait_ge(vsp, nsp)
                            zv = zs.ap()
                            ii = None
                            for (a, b2) in stg:
                                v.tensor_tensor(ztm[:, a:a + 1], zv[:, a:a + 1], zv[:, b2:b2 + 1], ALU.max)
                                ii = v.tensor_tensor(ztm[:, b2:b2 + 1], zv[:, a:a + 1], zv[:, b2:b2 + 1], ALU.min)
                            ii.then_inc(vsp, 1); nsp += 1
                            v.wait_ge(vsp, nsp)
                            for (a, b2) in stg:
                                v.tensor_copy(zs[:, a:a + 1], ztm[:, a:a + 1])
                                ii = v.tensor_copy(zs[:, b2:b2 + 1], ztm[:, b2:b2 + 1])
                            ii.then_inc(vsp, 1); nsp += 1
                        v.wait_ge(vsp, nsp)
                        v.tensor_tensor_scan(zc[:, :], zs[:, :], ones8[:, :],
                                             0.0, ALU.add, ALU.mult).then_inc(vsp, 1); nsp += 1
                        v.tensor_tensor(ztm[:, :], zs[:, :], kvec[:, :], ALU.mult).then_inc(vsp, 1); nsp += 1
                        v.wait_ge(vsp, nsp)
                        v.tensor_scalar(ztm[:, :], ztm[:, :], 1.0, None, ALU.add).then_inc(vsp, 1); nsp += 1
                        v.wait_ge(vsp, nsp)
                        v.tensor_tensor(sup[:, :], ztm[:, :], zc[:, :], ALU.is_gt).then_inc(vsp, 1); nsp += 1
                        v.wait_ge(vsp, nsp)
                        v.reduce_sum(scr[:, 0:1], sup[:, :], axis=AX.X).then_inc(vsp, 1); nsp += 1
                        v.tensor_tensor(ztm[:, :], zs[:, :], sup[:, :], ALU.mult).then_inc(vsp, 1); nsp += 1
                        v.wait_ge(vsp, nsp)
                        v.reduce_sum(scr[:, 1:2], ztm[:, :], axis=AX.X).then_inc(vsp, 1); nsp += 1
                        v.wait_ge(vsp, nsp)
                        v.reciprocal(scr[:, 2:3], scr[:, 0:1]).then_inc(vsp, 1); nsp += 1
                        v.wait_ge(vsp, nsp)
                        v.tensor_scalar(scr[:, 3:4], scr[:, 1:2], 1.0, scr[:, 2:3],
                                        ALU.subtract, ALU.mult).then_inc(vsp, 1); nsp += 1
                        v.wait_ge(vsp, nsp)
                        v.tensor_scalar(wsb[:, :], zt[:, :], scr[:, 3:4], 0.0,
                                        ALU.subtract, ALU.max).then_inc(vsp, 1); nsp += 1
                        v.wait_ge(ahe, gt + 1)
                        v.wait_ge(vsp, nsp)
                        for e8 in range(8):
                            ii = v.tensor_scalar(hw[gt % 2][:, e8 * 16:(e8 + 1) * 16],
                                                 hsb[gt % 2][:, e8 * 16:(e8 + 1) * 16],
                                                 wsb[:, e8:e8 + 1], None, ALU.mult)
                        ii.then_inc(vhw, 1)
                        v.wait_ge(mht, gt + 1)
                        v.tensor_copy(hwf[:, tt * 128:(tt + 1) * 128], pth[gt % 2][:, :]).then_inc(vhw, 1)
                    for tt in range(4):
                        gt = G * 4 + tt
                        v.wait_ge(mev, 32 * G + 28 + tt + 1)
                        if gt >= 2:
                            v.wait_ge(sst2, 5 * (gt - 2) + 3)
                        if gt >= 1:
                            v.wait_ge(vst2, 2 * gt)
                        for c in range(8):
                            ii = v.bn_stats(bns[:, 6 * c:6 * c + 6], ao[tt][:, 512 * c:512 * (c + 1)])
                        ii.then_inc(vst2, 1)
                        v.wait_ge(vst2, 2 * gt + 1)
                        v.bn_aggr(mv[gt % 2][:, :], bns[:, :]).then_inc(vst2, 1)
                        v.wait_ge(sst2, 5 * gt + 4)
                        if gt >= 1:
                            v.wait_ge(sxo, 16 * gt)
                        v.tensor_scalar(tmpc[:, :], ao[tt][:, :], mv[gt % 2][:, 0:1],
                                        sc[:, 1:2], ALU.subtract, ALU.mult).then_inc(cmb, 1)

            @blk.scalar
            def _(s):
                s.wait_ge(vst, 1)
                for i in range(NTT):
                    s.wait_ge(vst, 1 + 2 * (i + 1))
                    if i >= 1:
                        jj, kk = divmod(i - 1, 8)
                        s.wait_ge(sxn[jj], 16 * (kk + 1))
                    s.activation(sc[:, 0:1], mv[i % 2][:, 1:2], AF.Ln, bias=consts[:, 0:1]).then_inc(sst, 1)
                    s.wait_ge(sst, 5 * i + 1)
                    s.activation(sc[:, 1:2], sc[:, 0:1], AF.Exp, scale=consts[:, 1:2]).then_inc(sst, 1)
                    s.wait_ge(sst, 5 * i + 2)
                    s.activation(sc[:, 2:3], mv[i % 2][:, 0:1], AF.Identity, scale=sc[:, 1:2]).then_inc(sst, 1)
                    s.wait_ge(sst, 5 * i + 3)
                    s.activation(sc[:, 3:4], sc[:, 2:3], AF.Identity, scale=consts[:, 2:3]).then_inc(sst, 1)
                    s.wait_ge(sst, 5 * i + 4)
                    s.activation(xn_t[:, :], xt[0][:, :], AF.Identity,
                                 scale=sc[:, 1:2], bias=sc[:, 3:4]).then_inc(sst, 1)
                for G in range(4):
                    s.wait_ge(sxn[G // 2], 16 * ((4 * G + 3) % 8 + 1))
                    if G >= 1:
                        s.wait_ge(mdd, 32 * G)
                    for c in range(HC):
                        s.dma_start_transpose(xnf[:, c * 512:(c + 1) * 512],
                                              xn_hbm[G * 512:(G + 1) * 512,
                                                     c * 128:(c + 1) * 128]).then_inc(xfl, 16)
                    for tt in range(4):
                        gt = G * 4 + tt
                        s.wait_ge(mhh, gt + 1)
                        if gt >= 2:
                            s.wait_ge(vhw, 2 * (gt - 2) + 1)
                        s.activation(hsb[gt % 2][:, :], ph[gt % 2][:, :], AF.Identity).then_inc(ahe, 1)
                    for oc in range(8):
                        for tt in range(4):
                            U = 32 * G + 4 * oc + tt
                            s.wait_ge(mdd, U + 1)
                            if G >= 1 and oc == 0:
                                s.wait_ge(cmb, 4 * (G - 1) + tt + 1)
                            s.activation(ao[tt][:, oc * 512:(oc + 1) * 512],
                                         ps[U % 2][:, :], AF.Identity).then_inc(mev, 1)
                    for tt in range(4):
                        gt = G * 4 + tt
                        s.wait_ge(vst2, 2 * (gt + 1))
                        s.activation(sc[:, 0:1], mv[gt % 2][:, 1:2], AF.Ln, bias=consts[:, 0:1]).then_inc(sst2, 1)
                        s.wait_ge(sst2, 5 * gt + 1)
                        s.activation(sc[:, 1:2], sc[:, 0:1], AF.Exp, scale=consts[:, 1:2]).then_inc(sst2, 1)
                        s.wait_ge(sst2, 5 * gt + 2)
                        s.activation(sc[:, 2:3], sc[:, 1:2], AF.Identity).then_inc(sst2, 1)
                        s.wait_ge(sst2, 5 * gt + 3)
                        s.activation(sc[:, 2:3], sc[:, 2:3], AF.Identity).then_inc(sst2, 1)
                        s.wait_ge(sst2, 5 * gt + 4)
                        s.activation(sc[:, 2:3], sc[:, 2:3], AF.Identity).then_inc(sst2, 1)

            @blk.tensor
            def _(t):
                t.wait_ge(ldc, 48)
                t.wait_ge(gc, 144)
                for G in range(4):
                    t.wait_ge(xfl, 16 * HC * (G + 1))
                    for tt in range(4):
                        gt = G * 4 + tt
                        if gt >= 1:
                            t.wait_ge(vsp, SPN * gt - (SPN - 1))
                        t.matmul(pr[:, :], onesr[0:1, :], br_b[0:1, :], start=True, stop=False)
                        for c in range(HC):
                            t.matmul(pr[:, :], xnf[:, c * 512 + tt * 128: c * 512 + tt * 128 + 128],
                                     wr_b[:, c * 8:(c + 1) * 8],
                                     start=False, stop=(c == HC - 1)).then_inc(mrr, 0 if c < HC - 1 else 1)
                        if gt >= 2:
                            t.wait_ge(ahe, gt - 1)
                        for c in range(HC):
                            t.matmul(ph[gt % 2][:, :], xnf[:, c * 512 + tt * 128: c * 512 + tt * 128 + 128],
                                     la_b[:, c * 128:(c + 1) * 128],
                                     start=(c == 0), stop=(c == HC - 1)).then_inc(mhh, 0 if c < HC - 1 else 1)
                        t.wait_ge(vhw, 2 * gt + 1)
                        if gt >= 2:
                            t.wait_ge(vhw, 2 * (gt - 2) + 2)
                        t.transpose(pth[gt % 2][:, :], hw[gt % 2][:, :], ident[:, :]).then_inc(mht, 1)
                    t.wait_ge(vhw, 2 * (4 * G + 4))
                    for oc in range(8):
                        t.wait_ge(wbl, 16 * (8 * G + oc + 1))
                        for tt in range(4):
                            U = 32 * G + 4 * oc + tt
                            bank = U % 2
                            if U >= 2:
                                t.wait_ge(mev, U - 1)
                            for c in range(HC):
                                t.matmul(ps[bank][:, :],
                                         xnf[:, c * 512 + tt * 128: c * 512 + tt * 128 + 128],
                                         wsl[:, c * 512:(c + 1) * 512],
                                         start=(c == 0), stop=False)
                            t.matmul(ps[bank][:, :], hwf[:, tt * 128:(tt + 1) * 128],
                                     lb_b[:, oc * 512:(oc + 1) * 512], start=False, stop=False)
                            t.matmul(ps[bank][:, :], onesr[0:1, :],
                                     bb_b[0:1, oc * 512:(oc + 1) * 512],
                                     start=False, stop=True).then_inc(mdd, 1)

            @blk.sync
            def _(s2):
                s2.dma_start(out=onesr[0:1, :], in_=ones_h.ap()[0:1, :]).then_inc(ldc, 16)
                s2.dma_start(out=ident[:, :], in_=ident_h.ap()[:, :])\
                  .then_inc(ldc, 16)
                s2.dma_start(out=kvec[:, :], in_=kv_h.ap()[:, :]).then_inc(ldc, 16)
                for i in range(NTT):
                    if i >= 1:
                        s2.wait_ge(sst, 5 * i)
                    s2.dma_start(out=xt[0][:, :],
                                 in_=x3_in[i * 128:(i + 1) * 128, :]).then_inc(ldx, 16)
                    if i >= 1:
                        s2.dma_start(out=xn_hbm[(i - 1) * 128:i * 128, :],
                                     in_=xn_t[:, :]).then_inc(sxn[(i - 1) // 8], 16)
                s2.wait_ge(sst, 5 * NTT)
                s2.dma_start(out=xn_hbm[(NTT - 1) * 128:NTT * 128, :],
                             in_=xn_t[:, :]).then_inc(sxn[1], 16)
                s2.wait_ge(gc, 144)
                for G in range(4):
                    for oc in range(8):
                        G8 = G * 8 + oc
                        if G8 >= 1:
                            s2.wait_ge(mdd, 4 * G8)
                        s2.dma_start(out=wsl.ap().rearrange("p (c o) -> p c o", o=512),
                                     in_=wb_bf[:, oc * 512:(oc + 1) * 512]
                                     .rearrange("(c p) o -> p c o", p=128)).then_inc(wbl, 16)
                    for tt in range(4):
                        gt = G * 4 + tt
                        if gt >= 1:
                            s2.wait_ge(cmb2, gt)
                        s2.dma_start(out=xr[:, :],
                                     in_=x3_in[gt * 128:(gt + 1) * 128, :]).then_inc(xld, 16)
                        s2.wait_ge(cmb2, gt + 1)
                        s2.dma_start(out=xo_out[gt * 128:(gt + 1) * 128, :],
                                     in_=tmpc[:, :]).then_inc(sxo, 16)
                s2.wait_ge(sxo, 16 * 16)
    return nc


def irfft_basis(n=4096):
    t = np.arange(n)
    M = np.empty((n, n), dtype=np.float64)
    M[:, 0] = 1.0 / np.sqrt(n)
    f = np.arange(1, n // 2)
    ang = 2.0 * np.pi * np.outer(t, f) / n
    M[:, 1:n - 1:2] = np.sqrt(2.0 / n) * np.cos(ang)
    M[:, 2:n - 1:2] = np.sqrt(2.0 / n) * np.sin(ang)
    M[:, n - 1] = (-1.0) ** t / np.sqrt(n)
    return M

def l0_host_gather(h, k_all, v_all):
    kg = np.zeros((NS_TOT, 8, 128, 128), dtype=BF)
    vg = np.zeros((NS_TOT, 8, 128, 128), dtype=BF)
    biasc = np.full((128, NS_TOT), -1e30, dtype=np.float32)
    vcol = np.zeros((128, 16), dtype=np.float32)
    for g in range(4):
        plan = L0_PLANS[(h, g)]
        for s, sl in enumerate(plan):
            gs = NS_OFF[g] + s
            j = sl["j"]
            if not sl.get("dead"):
                kt = k_all[j * 128:(j + 1) * 128, :]
                kg[gs] = np.ascontiguousarray(
                    kt.reshape(128, 8, 128).transpose(1, 2, 0))
                vg[gs] = np.ascontiguousarray(
                    v_all[j * 128:(j + 1) * 128, :].reshape(128, 8, 128).transpose(1, 0, 2))
            biasc[0:64, gs] = 0.0 if sl["w0"] else -1e30
            biasc[64:128, gs] = 0.0 if sl["w1"] else -1e30
            if 4 <= s < 8:
                vcol[0:64, g * 4 + s - 4] = sl["v0"]
                vcol[64:128, g * 4 + s - 4] = sl["v1"]
    em = np.zeros((128, 16 * 512), dtype=BF)
    tkh = (np.arange(128)[:, None] >= 64).astype(np.int64)
    tqb = np.arange(512)[None, :] // 64
    for gi in range(16):
        cmk = (tqb < 2 * (gi % 4) + tkh).astype(np.float32)
        em[:, gi * 512:(gi + 1) * 512] = np.maximum(cmk, vcol[:, gi:gi + 1]).astype(BF)
    return kg, vg, np.ascontiguousarray(biasc), em

def _run(nc, in_maps):
    res = run_bass_kernel_spmd(nc, in_maps, core_ids=list(range(NCORES)))
    return res.results

def kernel(**inputs):
    x_in = inputs["L0_attn_hidden_states"]
    C = np.ascontiguousarray

    wt = C(inputs["L0_attn_w_qkv"].T)
    brow = C(inputs["L0_attn_b_qkv"][None, :])
    maps = []
    for c in range(NCORES):
        b, h = divmod(c, 2)
        pos = h * T + np.arange(T)
        cosv, sinv = rope_tm_tables(1e6, pos)
        maps.append(dict(x=C(x_in[b, h * T:(h + 1) * T]), wt=wt, brow=brow,
                         cos=cosv, sin=sinv))
    r1 = _run(build_l0_qkv(), maps)

    wdt = C(inputs["L0_attn_w_dense"].T)
    bdr = C(inputs["L0_attn_b_dense"][None, :])
    maps = []
    for c in range(NCORES):
        b, h = divmod(c, 2)
        k_all = np.concatenate([r1[2 * b]["kr"], r1[2 * b + 1]["kr"]], 0)
        v_all = np.concatenate([r1[2 * b]["vr"], r1[2 * b + 1]["vr"]], 0)
        kg, vg, biasc, em = l0_host_gather(h, k_all, v_all)
        maps.append(dict(qf=r1[c]["qr"], kg=kg, vg=vg, biasc=biasc, emaski=em,
                         x=C(x_in[b, h * T:(h + 1) * T]), wdt=wdt, bdr=bdr))
    r2 = _run(build_l0_attn(), maps)

    MT = C(irfft_basis().T).astype(np.float32)
    wbt = C(inputs["L0_ffn_W_base"].T)
    bbr = C(inputs["L0_ffn_b_base"][None, :])
    encc = C(np.concatenate([inputs["L0_ffn_enc0"].T, inputs["L0_ffn_enc1"].T], 1))
    c0p = C(inputs["L0_ffn_coefs0"].transpose(1, 0, 2).reshape(D, 96))
    m0p = C(inputs["L0_ffn_mask0"].transpose(1, 0, 2).reshape(D, 96)).view(np.uint8)
    c1p = C(inputs["L0_ffn_coefs1"].transpose(1, 0, 2).reshape(D, 24))
    m1p = C(inputs["L0_ffn_mask1"].transpose(1, 0, 2).reshape(D, 24)).view(np.uint8)
    maps = [dict(x1=r2[c]["x1"], wbt=wbt, bbr=bbr, encc=encc,
                 c0p=c0p, m0p=m0p, c1p=c1p, m1p=m1p, mtt=MT)
            for c in range(NCORES)]
    r3 = _run(build_l0_ffn(), maps)

    wcat = C(np.concatenate([inputs["L1_attn_Wq"], inputs["L1_attn_Wk"],
                             inputs["L1_attn_Wv"]], 1))
    maps = []
    for c in range(NCORES):
        b, h = divmod(c, 2)
        pos = h * T + np.arange(T)
        cosv, sinv = rope_tm_tables(1e4, pos)
        maps.append(dict(x2=r3[c]["x2"], wcat=wcat, cos=cosv, sin=sinv))
    r4 = _run(build_l1_qkv(), maps)

    maps = []
    for c in range(NCORES):
        b, p = divmod(c, 2)
        qs_f = np.concatenate([r4[2 * b]["qs"], r4[2 * b + 1]["qs"]], 0)
        ks_f = np.concatenate([r4[2 * b]["ks"], r4[2 * b + 1]["ks"]], 0)
        gs_f = np.concatenate([r4[2 * b]["gs"], r4[2 * b + 1]["gs"]], 0)
        vv_f = np.concatenate([r4[2 * b]["vv"], r4[2 * b + 1]["vv"]], 0)
        maps.append(dict(qs4=C(qs_f[:, p * 2048:(p + 1) * 2048]),
                         ks4=C(ks_f[:, p * 512:(p + 1) * 512]),
                         gs4=C(gs_f[:, p * 512:(p + 1) * 512]),
                         vv4=C(vv_f[:, p * 512:(p + 1) * 512])))
    r5 = _run(build_l1_gla(), maps)

    wot = inputs["L1_attn_Wo"]
    maps = []
    for c in range(NCORES):
        b, h = divmod(c, 2)
        sk_f = np.concatenate([r4[2 * b]["sk"], r4[2 * b + 1]["sk"]], 0)
        vv_f = np.concatenate([r4[2 * b]["vv"], r4[2 * b + 1]["vv"]], 0)
        ol_f = np.concatenate([r5[2 * b]["ol"], r5[2 * b + 1]["ol"]], 0)
        ske = np.zeros((2176, 1024), dtype=BF)
        vve = np.zeros((2176, 1024), dtype=BF)
        lo = h * T
        ske[128:, :] = sk_f[lo:lo + T]
        vve[128:, :] = vv_f[lo:lo + T]
        bnd = np.full((128, 1), -1e30, dtype=np.float32)
        if h == 1:
            ske[64:128, :] = sk_f[T - 64:T]
            vve[64:128, :] = vv_f[T - 64:T]
            bnd[64:128] = 0.0
        maps.append(dict(sq=r4[c]["sq"], ske=ske, vve=vve,
                         oli=C(ol_f[:, :, lo:lo + T]), bndc=bnd,
                         x2=r3[c]["x2"], wot=wot))
    r6 = _run(build_l1_swa(), maps)

    wb1t = C(inputs["L1_ffn_W_base"].T)
    bb1r = C(inputs["L1_ffn_b_base"][None, :])
    wrt = C(inputs["L1_ffn_W_router"].T)
    brr = C(inputs["L1_ffn_b_router"][None, :])
    lat = C(inputs["L1_ffn_lora_A"].transpose(1, 0, 2).reshape(D, 128))
    lbt = C(inputs["L1_ffn_lora_B"].reshape(128, D))
    maps = [dict(x3=r6[c]["x3"], wb1t=wb1t, bb1r=bb1r, wrt=wrt, brr=brr,
                 lat=lat, lbt=lbt) for c in range(NCORES)]
    r7 = _run(build_l1_ffn(), maps)

    out = np.empty((B, S, D), dtype=np.float32)
    for c in range(NCORES):
        b, h = divmod(c, 2)
        out[b, h * T:(h + 1) * T] = r7[c]["xo"]
    return out


# version 04 (final)
# speedup vs baseline: 1.1647x; 1.1647x over previous
import numpy as np
import ml_dtypes
from contextlib import ExitStack

import concourse.bass as bass
import concourse.mybir as mybir
from concourse.bass_utils import run_bass_kernel_spmd

F32 = mybir.dt.float32
BF16 = mybir.dt.bfloat16
U8 = mybir.dt.uint8
AF = mybir.ActivationFunctionType
ALU = mybir.AluOpType
AX = mybir.AxisListType
BF = ml_dtypes.bfloat16

NCORES = 8
B, S, D = 4, 4096, 4096
T = 2048
NTT = T // 128
HC = D // 128


def rope_tm_tables(theta, positions, dhead=128, rep=1):
    half = dhead // 2
    inv = 1.0 / theta ** (np.arange(0, half, dtype=np.float64) * 2.0 / dhead)
    fr = np.asarray(positions, dtype=np.float64)[:, None] * inv[None, :]
    c = np.cos(fr).astype(np.float32)
    s = np.sin(fr).astype(np.float32)
    return (np.ascontiguousarray(np.tile(c, (1, rep))),
            np.ascontiguousarray(np.tile(s, (1, rep))))

def l0_allowed_kb(qb):
    return [kb for kb in range(qb + 1) if qb - kb < 16 or (kb + 1) % 8 == 0]


NS_G = [14, 15, 16, 17]
NS_OFF = [0, 14, 29, 45]
NS_TOT = 62

def l0_tile_plan():
    plans = {}
    for g in range(4):
        for h in range(2):
            q0 = h * 32 + g * 8
            allowed = set()
            for qb in range(q0, q0 + 8):
                allowed.update(l0_allowed_kb(qb))
            slots = []
            for i in range(4):
                slots.append(dict(j=q0 // 2 + i, w0=1.0, w1=1.0, v0=0.0, v1=0.0))
            e0 = q0 - 16
            for i in range(4):
                if e0 >= 0:
                    j = e0 // 2 + i
                    kb0, kb1 = 2 * j, 2 * j + 1
                    slots.append(dict(j=j, w0=1.0, w1=1.0,
                                      v0=float((kb0 + 1) % 8 == 0),
                                      v1=float((kb1 + 1) % 8 == 0)))
                else:
                    slots.append(dict(j=0, w0=0.0, w1=0.0, v0=0.0, v1=0.0, dead=True))
            covered = {s["j"] for s in slots if not s.get("dead")}
            clean = sorted({kb // 2 for kb in allowed} - covered)
            for j in clean:
                slots.append(dict(j=j, w0=float(2 * j in allowed),
                                  w1=float(2 * j + 1 in allowed), v0=0.0, v1=0.0))
            while len(slots) < NS_G[g]:
                slots.append(dict(j=0, w0=0.0, w1=0.0, v0=0.0, v1=0.0, dead=True))
            assert len(slots) == NS_G[g], (h, g, len(slots))
            plans[(h, g)] = slots
    return plans

L0_PLANS = l0_tile_plan()


def new_nc():
    return bass.Bass("TRN2", target_bir_lowering=False, debug=False)

def build_l0_qkv(stage="full"):
    nc = new_nc()
    x = nc.dram_tensor("x", [T, D], F32, kind="ExternalInput").ap()
    wt = nc.dram_tensor("wt", [D, 6144], F32, kind="ExternalInput").ap()
    brow_in = nc.dram_tensor("brow", [1, 6144], F32, kind="ExternalInput").ap()
    cos_in = nc.dram_tensor("cos", [T, 64], F32, kind="ExternalInput").ap()
    sin_in = nc.dram_tensor("sin", [T, 64], F32, kind="ExternalInput").ap()
    qr = nc.dram_tensor("qr", [T, 4096], BF16, kind="ExternalOutput").ap()
    kr = nc.dram_tensor("kr", [T, 1024], BF16, kind="ExternalOutput").ap()
    vr = nc.dram_tensor("vr", [T, 1024], BF16, kind="ExternalOutput").ap()
    xn_out = nc.dram_tensor("xn", [T, D], BF16, kind="ExternalOutput").ap()
    wt_bf = nc.dram_tensor("wt_bf", [D, 6144], BF16, kind="Internal").ap()
    ones_h = nc.inline_tensor(np.ones((1, 128), dtype=BF), name="ones_row")

    units = []
    for j in range(2):
        for uc in range(12):
            for ti in range(8):
                units.append(dict(j=j, uc=uc, ti=ti, g=j * 12 + uc,
                                  kind="q" if uc < 8 else "kv",
                                  trow=j * 1024 + ti * 128))
    NU = len(units)

    with ExitStack() as ctx:
        e = ctx.enter_context
        xt = [e(nc.sbuf_tensor(f"xt{i}", [128, D], F32)) for i in range(2)]
        bns = e(nc.sbuf_tensor("bns", [128, 48], F32))
        mv = [e(nc.sbuf_tensor(f"mv{i}", [128, 2], F32)) for i in range(2)]
        sc = e(nc.sbuf_tensor("sc", [128, 4], F32))
        consts = e(nc.sbuf_tensor("consts", [128, 3], F32))
        xn_t = e(nc.sbuf_tensor("xnt", [128, D], BF16))
        xnf = e(nc.sbuf_tensor("xnf", [128, HC * 1024], BF16))
        wsl = [e(nc.sbuf_tensor(f"wsl{i}", [128, HC * 512], BF16)) for i in range(2)]
        browb = e(nc.sbuf_tensor("browb", [1, 6144], BF16))
        onesr = e(nc.sbuf_tensor("onesr", [1, 128], BF16))
        cs = e(nc.sbuf_tensor("cs", [128, NTT * 64], F32))
        sn = e(nc.sbuf_tensor("sn", [128, NTT * 64], F32))
        pb = [e(nc.sbuf_tensor(f"pb{i}", [128, 512], F32)) for i in range(2)]
        ro = [e(nc.sbuf_tensor(f"ro{i}", [128, 512], BF16)) for i in range(2)]
        vb = [e(nc.sbuf_tensor(f"vb{i}", [128, 256], BF16)) for i in range(2)]
        tmA = e(nc.sbuf_tensor("tmA", [128, 512], F32))
        tmB = e(nc.sbuf_tensor("tmB", [128, 512], F32))
        ps = [e(nc.psum_tensor(f"ps{i}", [128, 512], F32)) for i in range(4)]

        ldc = e(nc.semaphore("ldc"))
        ldx = [e(nc.semaphore(f"ldx{i}")) for i in range(2)]
        wc = e(nc.semaphore("wc"))
        wcb = e(nc.semaphore("wcb"))
        wl = [e(nc.semaphore(f"wl{i}")) for i in range(2)]
        xf = e(nc.semaphore("xf"))
        vst = e(nc.semaphore("vst"))
        sst = e(nc.semaphore("sst"))
        mm = e(nc.semaphore("mm"))
        evs = e(nc.semaphore("evs"))
        rpv = e(nc.semaphore("rpv"))
        rpp = e(nc.semaphore("rpp"))
        sxn = [e(nc.semaphore(f"sxn{i}")) for i in range(2)]
        sop = [e(nc.semaphore(f"sop{i}")) for i in range(2)]

        WLP = {}
        _a = [0, 0]
        for _g in range(24):
            _uc = _g % 12
            _a[_g % 2] += 16 if _uc < 8 else 32
            WLP[_g] = _a[_g % 2]
        SOP = {}
        _b = [0, 0]
        for _i, _u in enumerate(units):
            _b[_i % 2] += 16 if _u["kind"] == "q" else 32
            SOP[_i] = _b[_i % 2]
        ev_per_unit = [2 if u["kind"] == "kv" else 1 for u in units]
        ev_pfx = [0]
        for k in ev_per_unit:
            ev_pfx.append(ev_pfx[-1] + k)

        def emit_rope(eng, pbv, rov, tm, nh, cslc, snlc, half, sem):
            w = 64 * nh
            eng.wait_ge(sem[0], sem[1])
            lo = pbv[:, :, 0:64]; hi = pbv[:, :, 64:128]
            t1 = tm.ap()[:, 0:w].rearrange("p (h d) -> p h d", d=64)
            t2 = tm.ap()[:, w:2 * w].rearrange("p (h d) -> p h d", d=64)
            first, second = (lo, hi) if half == "lo" else (hi, lo)
            aa, cc = bass.broadcast_tensor_aps(first, cslc)
            eng.tensor_tensor(t1, aa, cc, ALU.mult)
            bb, ss = bass.broadcast_tensor_aps(second, snlc)
            i1 = eng.tensor_tensor(t2, bb, ss, ALU.mult)
            i1.then_inc(sem[0], 1); sem[1] += 1
            eng.wait_ge(sem[0], sem[1])
            out = rov[:, :, 0:64] if half == "lo" else rov[:, :, 64:128]
            i2 = eng.tensor_tensor(out, t1, t2, ALU.subtract if half == "lo" else ALU.add)
            i2.then_inc(sem[0], 1); sem[1] += 1

        with nc.Block() as blk:
            @blk.gpsimd
            def _(g):
                if stage != "lnnocast":
                    g.dma_start(out=browb[0:1, :], in_=brow_in[0:1, :]).then_inc(wcb, 16)
                    for c in range(8):
                        g.dma_start(out=wt_bf[c * 512:(c + 1) * 512, :],
                                    in_=wt[c * 512:(c + 1) * 512, :]).then_inc(wc, 16)
                if stage != "full":
                    return
                g.wait_ge(ldc, 48)
                semp = [rpp, 0]
                for i, u in enumerate(units):
                    par = i % 2
                    tcos = u["j"] * 8 + u["ti"]
                    cslc = cs.ap()[:, tcos * 64:(tcos + 1) * 64].rearrange("p (r d) -> p r d", r=1)
                    snlc = sn.ap()[:, tcos * 64:(tcos + 1) * 64].rearrange("p (r d) -> p r d", r=1)
                    g.wait_ge(evs, ev_pfx[i] + 1)
                    if i >= 2:
                        g.wait_ge(sop[par], SOP[i - 2])
                    if u["kind"] == "q":
                        pbv = pb[par].ap()[:, 0:512].rearrange("p (h d) -> p h d", d=128)
                        rov = ro[par].ap()[:, 0:512].rearrange("p (h d) -> p h d", d=128)
                        emit_rope(g, pbv, rov, tmB, 4, cslc, snlc, "hi", semp)
                    else:
                        pbv = pb[par].ap()[:, 0:512].rearrange("p (h d) -> p h d", d=256)[:, :, 0:128]
                        rov = ro[par].ap()[:, 0:256].rearrange("p (h d) -> p h d", d=128)
                        emit_rope(g, pbv, rov, tmB, 2, cslc, snlc, "hi", semp)

            @blk.vector
            def _(v):
                v.memset(consts[:, 0:1], 1e-5)
                v.memset(consts[:, 1:2], -0.5)
                v.memset(consts[:, 2:3], -1.0).then_inc(vst, 1)
                for i in range(NTT):
                    v.wait_ge(ldx[i % 2], 16 * (i // 2 + 1))
                    if i >= 1:
                        v.wait_ge(vst, 1 + 2 * i)
                    if i >= 2:
                        v.wait_ge(sst, 5 * (i - 2) + 3)
                    for c in range(8):
                        ii = v.bn_stats(bns[:, 6 * c:6 * c + 6], xt[i % 2][:, 512 * c:512 * (c + 1)])
                    ii.then_inc(vst, 1)
                    v.wait_ge(vst, 1 + 2 * i + 1)
                    v.bn_aggr(mv[i % 2][:, :], bns[:, :]).then_inc(vst, 1)
                if stage != "full":
                    return
                v.wait_ge(ldc, 48)
                semv = [rpv, 0]
                for i, u in enumerate(units):
                    par = i % 2
                    tcos = u["j"] * 8 + u["ti"]
                    cslc = cs.ap()[:, tcos * 64:(tcos + 1) * 64].rearrange("p (r d) -> p r d", r=1)
                    snlc = sn.ap()[:, tcos * 64:(tcos + 1) * 64].rearrange("p (r d) -> p r d", r=1)
                    v.wait_ge(evs, ev_pfx[i] + 1)
                    if i >= 2:
                        v.wait_ge(sop[par], SOP[i - 2])
                    if u["kind"] == "q":
                        pbv = pb[par].ap()[:, 0:512].rearrange("p (h d) -> p h d", d=128)
                        rov = ro[par].ap()[:, 0:512].rearrange("p (h d) -> p h d", d=128)
                        emit_rope(v, pbv, rov, tmA, 4, cslc, snlc, "lo", semv)
                    else:
                        pbv = pb[par].ap()[:, 0:512].rearrange("p (h d) -> p h d", d=256)[:, :, 0:128]
                        rov = ro[par].ap()[:, 0:256].rearrange("p (h d) -> p h d", d=128)
                        emit_rope(v, pbv, rov, tmA, 2, cslc, snlc, "lo", semv)

            @blk.scalar
            def _(s):
                s.wait_ge(vst, 1)
                for i in range(NTT):
                    s.wait_ge(vst, 1 + 2 * (i + 1))
                    if i >= 1:
                        jj, kk = divmod(i - 1, 8)
                        s.wait_ge(sxn[jj], 16 * (kk + 1))
                    s.activation(sc[:, 0:1], mv[i % 2][:, 1:2], AF.Ln, bias=consts[:, 0:1]).then_inc(sst, 1)
                    s.wait_ge(sst, 5 * i + 1)
                    s.activation(sc[:, 1:2], sc[:, 0:1], AF.Exp, scale=consts[:, 1:2]).then_inc(sst, 1)
                    s.wait_ge(sst, 5 * i + 2)
                    s.activation(sc[:, 2:3], mv[i % 2][:, 0:1], AF.Identity, scale=sc[:, 1:2]).then_inc(sst, 1)
                    s.wait_ge(sst, 5 * i + 3)
                    s.activation(sc[:, 3:4], sc[:, 2:3], AF.Identity, scale=consts[:, 2:3]).then_inc(sst, 1)
                    s.wait_ge(sst, 5 * i + 4)
                    s.activation(xn_t[:, :], xt[i % 2][:, :], AF.Identity,
                                 scale=sc[:, 1:2], bias=sc[:, 3:4]).then_inc(sst, 1)
                if stage != "full":
                    return
                s.wait_ge(sxn[0], 16 * 8)
                for c in range(HC):
                    s.dma_start_transpose(xnf[:, c * 1024:(c + 1) * 1024],
                                          xn_out[0:1024, c * 128:(c + 1) * 128]).then_inc(xf, 16)
                for i, u in enumerate(units):
                    if i == 96:
                        s.wait_ge(sxn[1], 16 * 8)
                        s.wait_ge(mm, 96)
                        for c in range(HC):
                            s.dma_start_transpose(xnf[:, c * 1024:(c + 1) * 1024],
                                                  xn_out[1024:2048, c * 128:(c + 1) * 128]).then_inc(xf, 16)
                    s.wait_ge(mm, i + 1)
                    bank = i % 4
                    par = i % 2
                    if i >= 2:
                        s.wait_ge(rpv, 2 * (i - 2) + 1)
                        s.wait_ge(rpp, 2 * (i - 2) + 1)
                    s.activation(pb[par][:, :], ps[bank][:, :], AF.Identity).then_inc(evs, 1)
                    if u["kind"] == "kv":
                        if i >= 2:
                            s.wait_ge(sop[par], SOP[i - 2])
                        s.activation(vb[par].ap()[:, 0:256].rearrange("p (h d) -> p h d", d=128),
                                     ps[bank].ap()[:, 0:512].rearrange("p (h d) -> p h d", d=256)[:, :, 128:256],
                                     AF.Identity).then_inc(evs, 1)

            @blk.tensor
            def _(t):
                if stage != "full":
                    return
                t.wait_ge(ldc, 48)
                t.wait_ge(wcb, 16)
                for i, u in enumerate(units):
                    j = u["j"]; g = u["g"]
                    t.wait_ge(wl[g % 2], WLP[g])
                    t.wait_ge(xf, 16 * HC * (j + 1))
                    if i >= 4:
                        t.wait_ge(evs, ev_pfx[i - 3])
                    bank = i % 4
                    for c in range(HC):
                        t.matmul(ps[bank][:, :],
                                 xnf[:, c * 1024 + u["ti"] * 128: c * 1024 + u["ti"] * 128 + 128],
                                 wsl[g % 2][:, c * 512:(c + 1) * 512],
                                 start=(c == 0), stop=False)
                    if u["uc"] < 8:
                        bslice = browb[0:1, u["uc"] * 768:u["uc"] * 768 + 512]
                    else:
                        m = u["uc"] - 8
                        bslice = browb.ap().rearrange("p (kv o) -> p kv o", o=768)[:, 2 * m:2 * m + 2, 512:768]
                    t.matmul(ps[bank][:, :], onesr[0:1, :], bslice,
                             start=False, stop=True).then_inc(mm, 1)

            @blk.sync
            def _(s2):
                s2.dma_start(out=onesr[0:1, :], in_=ones_h.ap()[0:1, :]).then_inc(ldc, 16)
                s2.dma_start(out=cs.ap().rearrange("p (n d) -> p n d", d=64),
                             in_=cos_in.rearrange("(n p) d -> p n d", p=128)).then_inc(ldc, 16)
                s2.dma_start(out=sn.ap().rearrange("p (n d) -> p n d", d=64),
                             in_=sin_in.rearrange("(n p) d -> p n d", p=128)).then_inc(ldc, 16)
                for i in range(NTT):
                    if i >= 2:
                        s2.wait_ge(sst, 5 * (i - 1))
                    s2.dma_start(out=xt[i % 2][:, :],
                                 in_=x[i * 128:(i + 1) * 128, :]).then_inc(ldx[i % 2], 16)
                    if i >= 1:
                        s2.wait_ge(sst, 5 * i)
                        s2.dma_start(out=xn_out[(i - 1) * 128:i * 128, :],
                                     in_=xn_t[:, :]).then_inc(sxn[(i - 1) // 8], 16)
                s2.wait_ge(sst, 5 * NTT)
                s2.dma_start(out=xn_out[(NTT - 1) * 128:NTT * 128, :],
                             in_=xn_t[:, :]).then_inc(sxn[1], 16)
                if stage != "full":
                    s2.wait_ge(sxn[0], 128)
                    s2.wait_ge(sxn[1], 128)
                    return
                s2.wait_ge(wc, 128)

                def slab_load(g):
                    uc2 = g % 12
                    if uc2 < 8:
                        s2.dma_start(out=wsl[g % 2].ap().rearrange("p (c o) -> p c o", o=512),
                                     in_=wt_bf[:, uc2 * 768:uc2 * 768 + 512]
                                     .rearrange("(c p) o -> p c o", p=128)).then_inc(wl[g % 2], 16)
                    else:
                        m2 = uc2 - 8
                        s2.dma_start(out=wsl[g % 2].ap().rearrange("p (c g o) -> p c g o", c=HC, g=2)[:, :, 0, :],
                                     in_=wt_bf[:, 1536 * m2 + 512:1536 * m2 + 512 + 256]
                                     .rearrange("(c p) o -> p c o", p=128)).then_inc(wl[g % 2], 16)
                        s2.dma_start(out=wsl[g % 2].ap().rearrange("p (c g o) -> p c g o", c=HC, g=2)[:, :, 1, :],
                                     in_=wt_bf[:, 1536 * m2 + 1280:1536 * m2 + 1280 + 256]
                                     .rearrange("(c p) o -> p c o", p=128)).then_inc(wl[g % 2], 16)

                slab_load(0)
                slab_load(1)
                for i, u in enumerate(units):
                    s2.wait_ge(rpv, 2 * (i + 1))
                    s2.wait_ge(rpp, 2 * (i + 1))
                    tr = u["trow"]
                    if u["kind"] == "q":
                        s2.dma_start(out=qr[tr:tr + 128, u["uc"] * 512:(u["uc"] + 1) * 512],
                                     in_=ro[i % 2][:, :]).then_inc(sop[i % 2], 16)
                    else:
                        s2.wait_ge(evs, ev_pfx[i + 1])
                        m = u["uc"] - 8
                        s2.dma_start(out=kr[tr:tr + 128, m * 256:(m + 1) * 256],
                                     in_=ro[i % 2][:, 0:256]).then_inc(sop[i % 2], 16)
                        s2.dma_start(out=vr[tr:tr + 128, m * 256:(m + 1) * 256],
                                     in_=vb[i % 2][:, 0:256]).then_inc(sop[i % 2], 16)
                    if i % 8 == 7 and i // 8 + 2 < 24:
                        s2.wait_ge(mm, 8 * (i // 8 + 1))
                        slab_load(i // 8 + 2)
                s2.wait_ge(sop[0], SOP[190])
                s2.wait_ge(sop[1], SOP[191])
                s2.wait_ge(sxn[0], 128)
                s2.wait_ge(sxn[1], 128)
    return nc


def build_l0_attn():
    nc = new_nc()
    qf = nc.dram_tensor("qf", [T, 4096], BF16, kind="ExternalInput").ap()
    kg_in = nc.dram_tensor("kg", [NS_TOT, 8, 128, 128], BF16, kind="ExternalInput").ap()
    vg_in = nc.dram_tensor("vg", [NS_TOT, 8, 128, 128], BF16, kind="ExternalInput").ap()
    biasc_in = nc.dram_tensor("biasc", [128, NS_TOT], F32, kind="ExternalInput").ap()
    emask_in = nc.dram_tensor("emaski", [128, 16 * 512], BF16, kind="ExternalInput").ap()
    x_in = nc.dram_tensor("x", [T, D], F32, kind="ExternalInput").ap()
    wdt = nc.dram_tensor("wdt", [D, D], F32, kind="ExternalInput").ap()
    bdr = nc.dram_tensor("bdr", [1, D], F32, kind="ExternalInput").ap()
    x1_out = nc.dram_tensor("x1", [T, D], F32, kind="ExternalOutput").ap()
    wd_bf = nc.dram_tensor("wd_bf", [D, D], BF16, kind="Internal").ap()
    ones_h = nc.inline_tensor(np.ones((1, 128), dtype=BF), name="ones_row")
    ones_c = nc.inline_tensor(np.ones((128, 1), dtype=BF), name="ones_col")
    dm = np.zeros((128, 4 * 512), dtype=BF)
    for i in range(4):
        tk = np.arange(128)[:, None] + 128 * i
        tq = np.arange(512)[None, :]
        dm[:, i * 512:(i + 1) * 512] = (tk <= tq).astype(BF)
    dmask_h = nc.inline_tensor(dm, name="dmask4")


    with ExitStack() as ctx:
        e = ctx.enter_context
        kg = [e(nc.sbuf_tensor(f"kg{i}", [128, 17 * 128], BF16)) for i in range(2)]
        vg = [e(nc.sbuf_tensor(f"vg{i}", [128, 17 * 128], BF16)) for i in range(2)]
        qsl = [e(nc.sbuf_tensor(f"qsl{i}", [128, 512], BF16)) for i in range(2)]
        expT = [e(nc.sbuf_tensor(f"expT{i}", [128, 512], BF16)) for i in range(4)]
        dmask = e(nc.sbuf_tensor("dmask", [128, 4 * 512], BF16))
        emask = e(nc.sbuf_tensor("emask", [128, 16 * 512], BF16))
        biasc = e(nc.sbuf_tensor("biasc_sb", [128, NS_TOT], F32))
        sm_sc = e(nc.sbuf_tensor("sm_sc", [128, 1], F32))
        onesr = e(nc.sbuf_tensor("onesr", [1, 128], BF16))
        onesc = e(nc.sbuf_tensor("onesc", [128, 1], BF16))
        rcp = e(nc.sbuf_tensor("rcp", [1, 512], F32))
        rcpb = e(nc.sbuf_tensor("rcpb", [1, 512], BF16))
        rb_sb = e(nc.sbuf_tensor("rb_sb", [128, 512], BF16))
        ctxe = [e(nc.sbuf_tensor(f"ctxe{i}", [128, 512], BF16)) for i in range(2)]
        ctx_sb = e(nc.sbuf_tensor("ctx_sb", [128, 32 * 512], BF16))
        wdsl = e(nc.sbuf_tensor("wdsl", [128, HC * 512], BF16))
        bd_b = e(nc.sbuf_tensor("bd_b", [1, D], BF16))
        ao = [e(nc.sbuf_tensor(f"ao{i}", [128, D], BF16)) for i in range(4)]
        xt = [e(nc.sbuf_tensor("xt0", [128, D], F32))]
        tmpc = e(nc.sbuf_tensor("tmpc", [128, D], F32))
        bns = e(nc.sbuf_tensor("bns", [128, 48], F32))
        mv2 = [e(nc.sbuf_tensor(f"mv2{i}", [128, 2], F32)) for i in range(2)]
        sc2 = e(nc.sbuf_tensor("sc2", [128, 4], F32))
        consts = e(nc.sbuf_tensor("consts", [128, 3], F32))
        scb = [e(nc.psum_tensor(f"scb{i}", [128, 512], F32)) for i in range(4)]
        cps = [e(nc.psum_tensor(f"cps{i}", [128, 512], F32)) for i in range(2)]
        su = e(nc.psum_tensor("su", [33, 512], F32))
        rbp = e(nc.psum_tensor("rbp", [128, 512], F32))

        ldc = e(nc.semaphore("ldc"))
        wdc = e(nc.semaphore("wdc"))
        wdcb = e(nc.semaphore("wdcb"))
        kgl = [e(nc.semaphore(f"kgl{i}")) for i in range(2)]
        qls = [e(nc.semaphore(f"qls{i}")) for i in range(2)]
        msc = e(nc.semaphore("msc"))
        mpv = e(nc.semaphore("mpv"))
        msu = e(nc.semaphore("msu"))
        mrb = e(nc.semaphore("mrb"))
        exps = e(nc.semaphore("exps"))
        dmk = e(nc.semaphore("dmk"))
        vnr = e(nc.semaphore("vnr"))
        cts = e(nc.semaphore("cts"))
        wdl = e(nc.semaphore("wdl"))
        mdd = e(nc.semaphore("mdd"))
        dev = e(nc.semaphore("dev"))
        vst2 = e(nc.semaphore("vst2"))
        sst2 = e(nc.semaphore("sst2"))
        cmb = e(nc.semaphore("cmb"))
        cmb2 = e(nc.semaphore("cmb2"))
        xld = e(nc.semaphore("xld"))
        sx1 = e(nc.semaphore("sx1"))

        NHG = 128
        SLOT_PFX = [0]
        for _hg in range(NHG):
            SLOT_PFX.append(SLOT_PFX[-1] + NS_G[_hg // 32])

        with nc.Block() as blk:
            @blk.gpsimd
            def _(g):
                g.dma_start(out=bd_b[0:1, :], in_=bdr[0:1, :]).then_inc(wdcb, 16)
                for c in range(4):
                    g.dma_start(out=wd_bf[c * 1024:(c + 1) * 1024, :],
                                in_=wdt[c * 1024:(c + 1) * 1024, :]).then_inc(wdc, 16)
                for gt in range(16):
                    g.wait_ge(cmb, gt + 1)
                    g.wait_ge(xld, 16 * (gt + 1))
                    if gt >= 1:
                        g.wait_ge(sx1, 16 * gt)
                    g.tensor_tensor(tmpc[:, :], xt[0][:, :], tmpc[:, :], ALU.add).then_inc(cmb2, 1)

            @blk.vector
            def _(v):
                v.memset(consts[:, 0:1], 1e-5)
                v.memset(consts[:, 1:2], -0.5)
                v.memset(consts[:, 2:3], -1.0)
                v.memset(sm_sc[:, 0:1], 1.0 / 128.0).then_inc(vst2, 1)
                v.wait_ge(ldc, 80)
                for hg in range(NHG):
                    grp = hg // 32
                    head = hg % 32
                    for s in range(8):
                        v.wait_ge(exps, SLOT_PFX[hg] + s + 1)
                        msl = (dmask[:, s * 512:(s + 1) * 512] if s < 4 else
                               emask[:, (grp * 4 + s - 4) * 512:(grp * 4 + s - 3) * 512])
                        eb = (SLOT_PFX[hg] + s) % 4
                        v.tensor_tensor(expT[eb][:, :], expT[eb][:, :],
                                        msl, ALU.mult).then_inc(dmk, 1)
                    v.wait_ge(msu, hg + 1)
                    v.wait_ge(vnr, 4 * hg)
                    v.reciprocal(rcp[0:1, :], su[32 * (hg % 2):32 * (hg % 2) + 1, :]).then_inc(vnr, 1)
                    v.wait_ge(vnr, 4 * hg + 1)
                    if hg >= 1:
                        v.wait_ge(mrb, hg)
                    v.tensor_copy(rcpb[0:1, :], rcp[0:1, :]).then_inc(vnr, 1)
                    v.wait_ge(mrb, hg + 1)
                    v.tensor_copy(rb_sb[:, :], rbp[:, :]).then_inc(vnr, 1)
                    v.wait_ge(cts, hg + 1)
                    v.wait_ge(vnr, 4 * hg + 3)
                    if grp >= 1:
                        v.wait_ge(mdd, 32 * grp)
                    v.tensor_tensor(ctx_sb[:, head * 512:(head + 1) * 512],
                                    ctxe[hg % 2][:, :], rb_sb[:, :], ALU.mult).then_inc(vnr, 1)
                    if hg % 32 == 31:
                        gg = grp
                        for tt in range(4):
                            gt = gg * 4 + tt
                            v.wait_ge(dev, 32 * gg + 28 + tt + 1)
                            if gt >= 2:
                                v.wait_ge(sst2, 5 * (gt - 2) + 3)
                            if gt >= 1:
                                v.wait_ge(vst2, 1 + 2 * gt)
                            for c in range(8):
                                ii = v.bn_stats(bns[:, 6 * c:6 * c + 6], ao[tt][:, 512 * c:512 * (c + 1)])
                            ii.then_inc(vst2, 1)
                            v.wait_ge(vst2, 1 + 2 * gt + 1)
                            v.bn_aggr(mv2[gt % 2][:, :], bns[:, :]).then_inc(vst2, 1)
                            v.wait_ge(sst2, 5 * gt + 4)
                            if gt >= 1:
                                v.wait_ge(sx1, 16 * gt)
                            v.tensor_scalar(tmpc[:, :], ao[tt][:, :], mv2[gt % 2][:, 0:1],
                                            sc2[:, 1:2], ALU.subtract, ALU.mult).then_inc(cmb, 1)

            @blk.scalar
            def _(s):
                s.wait_ge(ldc, 80)
                s.wait_ge(vst2, 1)
                for hg in range(NHG):
                    grp = hg // 32
                    head = hg % 32
                    if hg >= 2:
                        s.wait_ge(msc, SLOT_PFX[hg - 1])
                    s.dma_start_transpose(qsl[hg % 2][:, :],
                                          qf[grp * 512:(grp + 1) * 512,
                                             head * 128:(head + 1) * 128]).then_inc(qls[hg % 2], 16)
                    for seq in range(NS_G[grp]):
                        gs = SLOT_PFX[hg] + seq
                        s.wait_ge(msc, gs + 1)
                        if gs >= 4:
                            s.wait_ge(mpv, gs - 3)
                        s.activation(expT[gs % 4][:, :], scb[gs % 4][:, :], AF.Exp,
                                     scale=sm_sc[:, 0:1],
                                     bias=biasc[:, NS_OFF[grp] + seq:NS_OFF[grp] + seq + 1]).then_inc(exps, 1)
                    s.wait_ge(mpv, SLOT_PFX[hg + 1])
                    s.activation(ctxe[hg % 2][:, :], cps[hg % 2][:, :], AF.Identity).then_inc(cts, 1)
                    if hg % 32 == 31:
                        gg = grp
                        for oc in range(8):
                            for tt in range(4):
                                k = 32 * gg + 4 * oc + tt
                                s.wait_ge(mdd, k + 1)
                                if gg >= 1 or oc >= 1:
                                    if oc == 0:
                                        s.wait_ge(cmb, 4 * (gg - 1) + tt + 1)
                                s.activation(ao[tt][:, oc * 512:(oc + 1) * 512],
                                             scb[k % 4][:, :], AF.Identity).then_inc(dev, 1)
                        for tt in range(4):
                            gt = gg * 4 + tt
                            s.wait_ge(vst2, 1 + 2 * (gt + 1))
                            s.activation(sc2[:, 0:1], mv2[gt % 2][:, 1:2], AF.Ln, bias=consts[:, 0:1]).then_inc(sst2, 1)
                            s.wait_ge(sst2, 5 * gt + 1)
                            s.activation(sc2[:, 1:2], sc2[:, 0:1], AF.Exp, scale=consts[:, 1:2]).then_inc(sst2, 1)
                            s.wait_ge(sst2, 5 * gt + 2)
                            s.activation(sc2[:, 2:3], sc2[:, 1:2], AF.Identity).then_inc(sst2, 1)
                            s.wait_ge(sst2, 5 * gt + 3)
                            s.activation(sc2[:, 3:4], sc2[:, 2:3], AF.Identity).then_inc(sst2, 1)
                            s.wait_ge(sst2, 5 * gt + 4)
                            s.activation(sc2[:, 3:4], sc2[:, 3:4], AF.Identity).then_inc(sst2, 1)

            @blk.tensor
            def _(t):
                t.wait_ge(ldc, 80)
                t.wait_ge(wdcb, 16)
                for hg in range(NHG):
                    grp = hg // 32
                    kvi = (hg % 32) // 4
                    K = grp * 8 + kvi
                    t.wait_ge(kgl[K % 2], 32 * (K // 2 + 1))
                    t.wait_ge(qls[hg % 2], 16 * (hg // 2 + 1))
                    if grp >= 1 and hg % 32 == 0:
                        t.wait_ge(dev, 32 * grp)
                    nsg = NS_G[grp]

                    def score_mm(seq):
                        gs = SLOT_PFX[hg] + seq
                        if gs >= 4:
                            t.wait_ge(exps, gs - 3)
                        t.matmul(scb[gs % 4][:, :], kg[K % 2][:, seq * 128:(seq + 1) * 128],
                                 qsl[hg % 2][:, :], start=True, stop=True).then_inc(msc, 1)

                    def pv_mm(seq):
                        gs = SLOT_PFX[hg] + seq
                        t.wait_ge(exps, gs + 1)
                        if seq < 8:
                            t.wait_ge(dmk, 8 * hg + seq + 1)
                        if seq == 0:
                            if hg >= 2:
                                t.wait_ge(vnr, 4 * (hg - 2) + 1)
                                t.wait_ge(cts, hg - 1)
                        jj = t.matmul(su[32 * (hg % 2):32 * (hg % 2) + 1, :], onesc[:, 0:1], expT[gs % 4][:, :],
                                      start=(seq == 0), stop=(seq == nsg - 1))
                        if seq == nsg - 1:
                            jj.then_inc(msu, 1)
                        t.matmul(cps[hg % 2][:, :], vg[K % 2][:, seq * 128:(seq + 1) * 128],
                                 expT[gs % 4][:, :], start=(seq == 0), stop=(seq == nsg - 1)).then_inc(mpv, 1)

                    def rb_mm(u):
                        t.wait_ge(vnr, 4 * u + 2)
                        if u >= 1:
                            t.wait_ge(vnr, 4 * (u - 1) + 3)
                        t.matmul(rbp[:, :], onesr[0:1, :], rcpb[0:1, :], start=True, stop=True).then_inc(mrb, 1)

                    score_mm(0)
                    score_mm(1)
                    score_mm(2)
                    if hg >= 1 and hg % 32 != 0:
                        rb_mm(hg - 1)
                    for seq in range(3, nsg):
                        score_mm(seq)
                        pv_mm(seq - 3)
                    pv_mm(nsg - 3)
                    pv_mm(nsg - 2)
                    pv_mm(nsg - 1)
                    if hg % 32 == 31:
                        rb_mm(hg)
                        gg = grp
                        t.wait_ge(vnr, 4 * 32 * (gg + 1))
                        for oc in range(8):
                            t.wait_ge(wdl, 16 * (8 * gg + oc + 1))
                            for tt in range(4):
                                k = 32 * gg + 4 * oc + tt
                                bank = k % 4
                                if k >= 4:
                                    t.wait_ge(dev, k - 3)
                                for hh in range(32):
                                    t.matmul(scb[bank][:, :],
                                             ctx_sb[:, hh * 512 + tt * 128: hh * 512 + tt * 128 + 128],
                                             wdsl[:, hh * 512:(hh + 1) * 512],
                                             start=(hh == 0), stop=False)
                                t.matmul(scb[bank][:, :], onesr[0:1, :],
                                         bd_b[0:1, oc * 512:(oc + 1) * 512],
                                         start=False, stop=True).then_inc(mdd, 1)

            @blk.sync
            def _(s2):
                s2.dma_start(out=onesr[0:1, :], in_=ones_h.ap()[0:1, :]).then_inc(ldc, 16)
                s2.dma_start(out=onesc[:, 0:1], in_=ones_c.ap()[:, 0:1]).then_inc(ldc, 16)
                s2.dma_start(out=dmask[:, :], in_=dmask_h.ap()[:, :]).then_inc(ldc, 16)
                s2.dma_start(out=biasc[:, :], in_=biasc_in[:, :]).then_inc(ldc, 16)
                s2.dma_start(out=emask[:, :], in_=emask_in[:, :]).then_inc(ldc, 16)
                for grp in range(4):
                    for kvi in range(8):
                        K = grp * 8 + kvi
                        nsg = NS_G[grp]
                        if K >= 2:
                            s2.wait_ge(mpv, SLOT_PFX[4 * (K - 1)])
                        s2.dma_start(out=kg[K % 2].ap().rearrange("p (s c) -> p s c", c=128)[:, 0:nsg, :],
                                     in_=kg_in[NS_OFF[grp]:NS_OFF[grp] + nsg, kvi, :, :].rearrange("s p c -> p s c")).then_inc(kgl[K % 2], 16)
                        s2.dma_start(out=vg[K % 2].ap().rearrange("p (s c) -> p s c", c=128)[:, 0:nsg, :],
                                     in_=vg_in[NS_OFF[grp]:NS_OFF[grp] + nsg, kvi, :, :].rearrange("s p c -> p s c")).then_inc(kgl[K % 2], 16)
                    for oc in range(8):
                        G = grp * 8 + oc
                        if G >= 1:
                            s2.wait_ge(mdd, 4 * G)
                        else:
                            s2.wait_ge(wdc, 64)
                        s2.dma_start(out=wdsl.ap().rearrange("p (c o) -> p c o", o=512),
                                     in_=wd_bf[:, oc * 512:(oc + 1) * 512]
                                     .rearrange("(c p) o -> p c o", p=128)).then_inc(wdl, 16)
                    for tt in range(4):
                        gt = grp * 4 + tt
                        if gt >= 1:
                            s2.wait_ge(cmb2, gt)
                        s2.dma_start(out=xt[0][:, :],
                                     in_=x_in[gt * 128:(gt + 1) * 128, :]).then_inc(xld, 16)
                        s2.wait_ge(cmb2, gt + 1)
                        s2.dma_start(out=x1_out[gt * 128:(gt + 1) * 128, :],
                                     in_=tmpc[:, :]).then_inc(sx1, 16)
                s2.wait_ge(sx1, 16 * 16)
    return nc

def build_l0_ffn():
    nc = new_nc()
    x1_in = nc.dram_tensor("x1", [T, D], F32, kind="ExternalInput").ap()
    wbt = nc.dram_tensor("wbt", [D, D], F32, kind="ExternalInput").ap()
    bbr = nc.dram_tensor("bbr", [1, D], F32, kind="ExternalInput").ap()
    enc_in = nc.dram_tensor("encc", [D, 40], F32, kind="ExternalInput").ap()
    c0p = nc.dram_tensor("c0p", [D, 96], F32, kind="ExternalInput").ap()
    m0p = nc.dram_tensor("m0p", [D, 96], U8, kind="ExternalInput").ap()
    c1p = nc.dram_tensor("c1p", [D, 24], F32, kind="ExternalInput").ap()
    m1p = nc.dram_tensor("m1p", [D, 24], U8, kind="ExternalInput").ap()
    mtt = nc.dram_tensor("mtt", [D, D], F32, kind="ExternalInput").ap()
    x2_out = nc.dram_tensor("x2", [T, D], F32, kind="ExternalOutput").ap()
    xn_hbm = nc.dram_tensor("xnh", [T, D], BF16, kind="Internal").ap()
    wb_bf = nc.dram_tensor("wb_bf", [D, D], BF16, kind="Internal").ap()
    mt_bf = nc.dram_tensor("mt_bf", [D, D], BF16, kind="Internal").ap()
    ones_h = nc.inline_tensor(np.ones((1, 128), dtype=BF), name="ones_row")
    ident_h = nc.inline_tensor(np.eye(128, dtype=BF), name="ident128")

    with ExitStack() as ctx:
        e = ctx.enter_context
        xt = [e(nc.sbuf_tensor("xt0", [128, D], F32))]
        bns = e(nc.sbuf_tensor("bns", [128, 48], F32))
        mv = [e(nc.sbuf_tensor(f"mv{i}", [128, 2], F32)) for i in range(2)]
        sc = e(nc.sbuf_tensor("sc", [128, 4], F32))
        consts = e(nc.sbuf_tensor("consts", [128, 4], F32))
        xn_t = e(nc.sbuf_tensor("xnt", [128, D], BF16))
        xnf = e(nc.sbuf_tensor("xnf", [128, HC * 512], BF16))
        wsl = e(nc.sbuf_tensor("wsl", [128, HC * 512], BF16))
        mtsl = [e(nc.sbuf_tensor("mtsl0", [128, HC * 128], BF16))]
        encb = e(nc.sbuf_tensor("encb", [128, HC * 40], BF16))
        bb_b = e(nc.sbuf_tensor("bb_b", [1, D], BF16))
        onesr = e(nc.sbuf_tensor("onesr", [1, 128], BF16))
        ident = e(nc.sbuf_tensor("ident", [128, 128], BF16))
        c0sb = e(nc.sbuf_tensor("c0sb", [128, 96], F32))
        m0sb = e(nc.sbuf_tensor("m0sb", [128, 96], U8))
        c1sb = e(nc.sbuf_tensor("c1sb", [128, 24], F32))
        m1sb = e(nc.sbuf_tensor("m1sb", [128, 24], U8))
        mf0 = e(nc.sbuf_tensor("mf0", [128, 96], F32))
        mf1 = e(nc.sbuf_tensor("mf1", [128, 24], F32))
        tsum = e(nc.sbuf_tensor("tsum", [128, 40], F32))
        cbf = e(nc.sbuf_tensor("cbf", [128, HC * 40], BF16))
        dsb = [e(nc.sbuf_tensor(f"dsb{i}", [128, 40], BF16)) for i in range(2)]
        esb = [e(nc.sbuf_tensor(f"esb{i}", [128, 40], BF16)) for i in range(2)]
        dT = e(nc.sbuf_tensor("dT", [40, D], BF16))
        e_fm = e(nc.sbuf_tensor("e_fm", [40, NTT * 128], BF16))
        ao = [e(nc.sbuf_tensor(f"ao{i}", [128, D], BF16)) for i in range(4)]
        tmpc = e(nc.sbuf_tensor("tmpc", [128, D], F32))
        xr = e(nc.sbuf_tensor("xr", [128, D], F32))
        ps = [e(nc.psum_tensor(f"ps{i}", [128, 512], F32)) for i in range(4)]
        pd = [e(nc.psum_tensor(f"pd{i}", [128, 40], F32)) for i in range(2)]
        pt2 = [e(nc.psum_tensor(f"pt2{i}", [40, 128], BF16)) for i in range(2)]

        ldc = e(nc.semaphore("ldc"))
        ldx = e(nc.semaphore("ldx"))
        gc = e(nc.semaphore("gc"))
        wcb = e(nc.semaphore("wcb"))
        mcc = e(nc.semaphore("mcc"))
        cfl = [e(nc.semaphore(f"cfl{i}")) for i in range(2)]
        mtl = e(nc.semaphore("mtl"))
        wbl = e(nc.semaphore("wbl"))
        xfl = e(nc.semaphore("xfl"))
        vst = e(nc.semaphore("vst"))
        sst = e(nc.semaphore("sst"))
        vcc = e(nc.semaphore("vcc"))
        mdc = e(nc.semaphore("mdc"))
        dse = e(nc.semaphore("dse"))
        mdt = e(nc.semaphore("mdt"))
        vdt = e(nc.semaphore("vdt"))
        mec = e(nc.semaphore("mec"))
        ese = e(nc.semaphore("ese"))
        met = e(nc.semaphore("met"))
        vet = e(nc.semaphore("vet"))
        mdd = e(nc.semaphore("mdd"))
        mev = e(nc.semaphore("mev"))
        vst2 = e(nc.semaphore("vst2"))
        sst2 = e(nc.semaphore("sst2"))
        cmb = e(nc.semaphore("cmb"))
        cmb2 = e(nc.semaphore("cmb2"))
        xld = e(nc.semaphore("xld"))
        sxn = [e(nc.semaphore(f"sxn{i}")) for i in range(2)]
        sx2 = e(nc.semaphore("sx2"))

        with nc.Block() as blk:
            @blk.gpsimd
            def _(g):
                g.dma_start(out=bb_b[0:1, :], in_=bbr[0:1, :]).then_inc(gc, 16)
                g.dma_start(out=encb.ap().rearrange("p (c r) -> p c r", r=40),
                            in_=enc_in.rearrange("(c p) r -> p c r", p=128)).then_inc(gc, 16)
                for c in range(4):
                    if c >= 1:
                        g.wait_ge(mcc, 16 * c)
                    g.dma_start(out=mt_bf[:, c * 1024:(c + 1) * 1024],
                                in_=mtt[:, c * 1024:(c + 1) * 1024]).then_inc(mcc, 16)
                for oc in range(8):
                    g.wait_ge(wcb, 16 * oc) if oc >= 1 else g.wait_ge(mcc, 64)
                    g.dma_start(out=wb_bf[:, oc * 512:(oc + 1) * 512],
                                in_=wbt[:, oc * 512:(oc + 1) * 512]).then_inc(wcb, 16)
                for gt in range(16):
                    g.wait_ge(cmb, gt + 1)
                    g.wait_ge(xld, 16 * (gt + 1))
                    if gt >= 1:
                        g.wait_ge(sx2, 16 * gt)
                    g.tensor_tensor(tmpc[:, :], xr[:, :], tmpc[:, :], ALU.add).then_inc(cmb2, 1)

            @blk.vector
            def _(v):
                v.memset(consts[:, 0:1], 1e-5)
                v.memset(consts[:, 1:2], -0.5)
                v.memset(consts[:, 2:3], -1.0)
                v.memset(consts[:, 3:4], 0.5).then_inc(vst, 1)
                for i in range(NTT):
                    v.wait_ge(ldx, 16 * (i + 1))
                    if i >= 1:
                        v.wait_ge(vst, 1 + 2 * i)
                    if i >= 2:
                        v.wait_ge(sst, 5 * (i - 2) + 3)
                    for c in range(8):
                        ii = v.bn_stats(bns[:, 6 * c:6 * c + 6], xt[0][:, 512 * c:512 * (c + 1)])
                    ii.then_inc(vst, 1)
                    v.wait_ge(vst, 1 + 2 * i + 1)
                    v.bn_aggr(mv[i % 2][:, :], bns[:, :]).then_inc(vst, 1)
                nvc = 0
                for oc in range(HC):
                    v.wait_ge(cfl[oc % 2], 64 * (oc // 2 + 1))
                    v.tensor_copy(mf0[:, :], m0sb[:, :])
                    v.tensor_copy(mf1[:, :], m1sb[:, :]).then_inc(vcc, 1); nvc += 1
                    v.wait_ge(vcc, nvc)
                    v.tensor_tensor(mf0[:, :], mf0[:, :], c0sb[:, :], ALU.mult)
                    v.tensor_tensor(mf1[:, :], mf1[:, :], c1sb[:, :], ALU.mult).then_inc(vcc, 1); nvc += 1
                    v.wait_ge(vcc, nvc)
                    v.tensor_tensor(tsum[:, 0:32], mf0[:, 0:32], mf0[:, 32:64], ALU.add)
                    v.tensor_tensor(tsum[:, 32:40], mf1[:, 0:8], mf1[:, 8:16], ALU.add).then_inc(vcc, 1); nvc += 1
                    v.wait_ge(vcc, nvc)
                    v.tensor_tensor(cbf[:, oc * 40:oc * 40 + 32], tsum[:, 0:32], mf0[:, 64:96], ALU.add)
                    v.tensor_tensor(cbf[:, oc * 40 + 32:oc * 40 + 40], tsum[:, 32:40], mf1[:, 16:24], ALU.add).then_inc(vcc, 1); nvc += 1
                for u in range(HC):
                    v.wait_ge(mdt, u + 1)
                    v.tensor_copy(dT[:, u * 128:(u + 1) * 128], pt2[u % 2][:, :]).then_inc(vdt, 1)
                for G in range(4):
                    for tt in range(4):
                        eu = G * 4 + tt
                        v.wait_ge(met, eu + 1)
                        v.tensor_copy(e_fm[:, eu * 128:(eu + 1) * 128], pt2[eu % 2][:, :]).then_inc(vet, 1)
                    for tt in range(4):
                        gt = G * 4 + tt
                        v.wait_ge(mev, 32 * G + 28 + tt + 1)
                        if gt >= 2:
                            v.wait_ge(sst2, 5 * (gt - 2) + 3)
                        if gt >= 1:
                            v.wait_ge(vst2, 2 * gt)
                        for c in range(8):
                            ii = v.bn_stats(bns[:, 6 * c:6 * c + 6], ao[tt][:, 512 * c:512 * (c + 1)])
                        ii.then_inc(vst2, 1)
                        v.wait_ge(vst2, 2 * gt + 1)
                        v.bn_aggr(mv[gt % 2][:, :], bns[:, :]).then_inc(vst2, 1)
                        v.wait_ge(sst2, 5 * gt + 4)
                        if gt >= 1:
                            v.wait_ge(sx2, 16 * gt)
                        v.tensor_scalar(tmpc[:, :], ao[tt][:, :], mv[gt % 2][:, 0:1],
                                        sc[:, 1:2], ALU.subtract, ALU.mult).then_inc(cmb, 1)

            @blk.scalar
            def _(s):
                s.wait_ge(vst, 1)
                for i in range(NTT):
                    s.wait_ge(vst, 1 + 2 * (i + 1))
                    if i >= 1:
                        jj, kk = divmod(i - 1, 8)
                        s.wait_ge(sxn[jj], 16 * (kk + 1))
                    s.activation(sc[:, 0:1], mv[i % 2][:, 1:2], AF.Ln, bias=consts[:, 0:1]).then_inc(sst, 1)
                    s.wait_ge(sst, 5 * i + 1)
                    s.activation(sc[:, 1:2], sc[:, 0:1], AF.Exp, scale=consts[:, 1:2]).then_inc(sst, 1)
                    s.wait_ge(sst, 5 * i + 2)
                    s.activation(sc[:, 2:3], mv[i % 2][:, 0:1], AF.Identity, scale=sc[:, 1:2]).then_inc(sst, 1)
                    s.wait_ge(sst, 5 * i + 3)
                    s.activation(sc[:, 3:4], sc[:, 2:3], AF.Identity, scale=consts[:, 2:3]).then_inc(sst, 1)
                    s.wait_ge(sst, 5 * i + 4)
                    s.activation(xn_t[:, :], xt[0][:, :], AF.Identity,
                                 scale=sc[:, 1:2], bias=sc[:, 3:4]).then_inc(sst, 1)
                for u in range(HC):
                    s.wait_ge(mdc, u + 1)
                    if u >= 2:
                        s.wait_ge(mdt, u - 1)
                    s.activation(dsb[u % 2][:, :], pd[u % 2][:, :], AF.Identity,
                                 scale=consts[:, 3:4]).then_inc(dse, 1)
                for G in range(4):
                    s.wait_ge(sxn[G // 2], 16 * ((4 * G + 3) % 8 + 1))
                    if G >= 1:
                        s.wait_ge(mdd, 32 * G)
                    for c in range(HC):
                        s.dma_start_transpose(xnf[:, c * 512:(c + 1) * 512],
                                              xn_hbm[G * 512:(G + 1) * 512,
                                                     c * 128:(c + 1) * 128]).then_inc(xfl, 16)
                    for tt in range(4):
                        eu = G * 4 + tt
                        s.wait_ge(mec, eu + 1)
                        if eu >= 2:
                            s.wait_ge(met, eu - 1)
                        s.activation(esb[eu % 2][:, :], pd[eu % 2][:, :], AF.Identity).then_inc(ese, 1)
                    for oc in range(8):
                        for tt in range(4):
                            U = 32 * G + 4 * oc + tt
                            s.wait_ge(mdd, U + 1)
                            if G >= 1 and oc == 0:
                                s.wait_ge(cmb, 4 * (G - 1) + tt + 1)
                            s.activation(ao[tt][:, oc * 512:(oc + 1) * 512],
                                         ps[U % 4][:, :], AF.Identity).then_inc(mev, 1)
                    for tt in range(4):
                        gt = G * 4 + tt
                        s.wait_ge(vst2, 2 * (gt + 1))
                        s.activation(sc[:, 0:1], mv[gt % 2][:, 1:2], AF.Ln, bias=consts[:, 0:1]).then_inc(sst2, 1)
                        s.wait_ge(sst2, 5 * gt + 1)
                        s.activation(sc[:, 1:2], sc[:, 0:1], AF.Exp, scale=consts[:, 1:2]).then_inc(sst2, 1)
                        s.wait_ge(sst2, 5 * gt + 2)
                        s.activation(sc[:, 2:3], sc[:, 1:2], AF.Identity).then_inc(sst2, 1)
                        s.wait_ge(sst2, 5 * gt + 3)
                        s.activation(sc[:, 2:3], sc[:, 2:3], AF.Identity).then_inc(sst2, 1)
                        s.wait_ge(sst2, 5 * gt + 4)
                        s.activation(sc[:, 2:3], sc[:, 2:3], AF.Identity).then_inc(sst2, 1)

            @blk.tensor
            def _(t):
                t.wait_ge(ldc, 32)
                t.wait_ge(gc, 32)
                t.wait_ge(vcc, 4 * HC)
                for u in range(HC):
                    t.wait_ge(mtl, 16 * (u + 1))
                    if u >= 2:
                        t.wait_ge(dse, u - 1)
                    for c in range(HC):
                        t.matmul(pd[u % 2][:, :], mtsl[0][:, c * 128:(c + 1) * 128],
                                 cbf[:, c * 40:(c + 1) * 40],
                                 start=(c == 0), stop=(c == HC - 1)).then_inc(mdc, 0 if c < HC - 1 else 1)
                    t.wait_ge(dse, u + 1)
                    if u >= 2:
                        t.wait_ge(vdt, u - 1)
                    t.transpose(pt2[u % 2][:, :], dsb[u % 2][:, :], ident[:, :]).then_inc(mdt, 1)
                for G in range(4):
                    t.wait_ge(xfl, 16 * HC * (G + 1))
                    for tt in range(4):
                        eu = G * 4 + tt
                        gt = G * 4 + tt
                        if eu < 2:
                            t.wait_ge(dse, HC - 2 + eu + 1)
                        else:
                            t.wait_ge(ese, eu - 1)
                        for c in range(HC):
                            t.matmul(pd[eu % 2][:, :],
                                     xnf[:, c * 512 + tt * 128: c * 512 + tt * 128 + 128],
                                     encb[:, c * 40:(c + 1) * 40],
                                     start=(c == 0), stop=(c == HC - 1)).then_inc(mec, 0 if c < HC - 1 else 1)
                        t.wait_ge(ese, eu + 1)
                        if eu >= 2:
                            t.wait_ge(vet, eu - 1)
                        elif eu < 2:
                            t.wait_ge(vdt, HC - 2 + eu + 1)
                        t.transpose(pt2[eu % 2][:, :], esb[eu % 2][:, :], ident[:, :]).then_inc(met, 1)
                    t.wait_ge(vdt, HC)
                    for oc in range(8):
                        G8 = G * 8 + oc
                        t.wait_ge(wbl, 16 * (G8 + 1))
                        for tt in range(4):
                            U = 32 * G + 4 * oc + tt
                            gt = G * 4 + tt
                            t.wait_ge(vet, G * 4 + tt + 1)
                            if U >= 4:
                                t.wait_ge(mev, U - 3)
                            bank = U % 4
                            for c in range(HC):
                                t.matmul(ps[bank][:, :],
                                         xnf[:, c * 512 + tt * 128: c * 512 + tt * 128 + 128],
                                         wsl[:, c * 512:(c + 1) * 512],
                                         start=(c == 0), stop=False)
                            t.matmul(ps[bank][:, :], e_fm[:, gt * 128:(gt + 1) * 128],
                                     dT[:, oc * 512:(oc + 1) * 512], start=False, stop=False)
                            t.matmul(ps[bank][:, :], onesr[0:1, :],
                                     bb_b[0:1, oc * 512:(oc + 1) * 512],
                                     start=False, stop=True).then_inc(mdd, 1)

            @blk.sync
            def _(s2):
                s2.dma_start(out=onesr[0:1, :], in_=ones_h.ap()[0:1, :]).then_inc(ldc, 16)
                s2.dma_start(out=ident[:, :], in_=ident_h.ap()[:, :]).then_inc(ldc, 16)
                for i in range(NTT):
                    if i >= 1:
                        s2.wait_ge(sst, 5 * i)
                    s2.dma_start(out=xt[0][:, :],
                                 in_=x1_in[i * 128:(i + 1) * 128, :]).then_inc(ldx, 16)
                    if i >= 1:
                        s2.wait_ge(sst, 5 * i)
                        s2.dma_start(out=xn_hbm[(i - 1) * 128:i * 128, :],
                                     in_=xn_t[:, :]).then_inc(sxn[(i - 1) // 8], 16)
                s2.wait_ge(sst, 5 * NTT)
                s2.dma_start(out=xn_hbm[(NTT - 1) * 128:NTT * 128, :],
                             in_=xn_t[:, :]).then_inc(sxn[1], 16)
                for oc in range(HC):
                    if oc >= 1:
                        s2.wait_ge(vcc, 4 * oc)
                    s2.dma_start(out=c0sb[:, :], in_=c0p[oc * 128:(oc + 1) * 128, :]).then_inc(cfl[oc % 2], 16)
                    s2.dma_start(out=m0sb[:, :], in_=m0p[oc * 128:(oc + 1) * 128, :]).then_inc(cfl[oc % 2], 16)
                    s2.dma_start(out=c1sb[:, :], in_=c1p[oc * 128:(oc + 1) * 128, :]).then_inc(cfl[oc % 2], 16)
                    s2.dma_start(out=m1sb[:, :], in_=m1p[oc * 128:(oc + 1) * 128, :]).then_inc(cfl[oc % 2], 16)
                for u in range(HC):
                    if u >= 1:
                        s2.wait_ge(mdc, u)
                    if u % 8 == 0:
                        s2.wait_ge(mcc, 16 * (u // 8 + 1))
                    s2.dma_start(out=mtsl[0].ap().rearrange("p (c o) -> p c o", o=128),
                                 in_=mt_bf[:, u * 128:(u + 1) * 128]
                                 .rearrange("(c p) o -> p c o", p=128)).then_inc(mtl, 16)
                for G in range(4):
                    for oc in range(8):
                        G8 = G * 8 + oc
                        if G8 >= 1:
                            s2.wait_ge(mdd, 4 * G8)
                        if G == 0:
                            s2.wait_ge(wcb, 16 * (oc + 1))
                        s2.dma_start(out=wsl.ap().rearrange("p (c o) -> p c o", o=512),
                                     in_=wb_bf[:, (G8 % 8) * 512:(G8 % 8 + 1) * 512]
                                     .rearrange("(c p) o -> p c o", p=128)).then_inc(wbl, 16)
                    for tt in range(4):
                        gt = G * 4 + tt
                        if gt >= 1:
                            s2.wait_ge(cmb2, gt)
                        s2.dma_start(out=xr[:, :],
                                     in_=x1_in[gt * 128:(gt + 1) * 128, :]).then_inc(xld, 16)
                        s2.wait_ge(cmb2, gt + 1)
                        s2.dma_start(out=x2_out[gt * 128:(gt + 1) * 128, :],
                                     in_=tmpc[:, :]).then_inc(sx2, 16)
                s2.wait_ge(sx2, 16 * 16)
    return nc

def build_l1_qkv():
    nc = new_nc()
    x2_in = nc.dram_tensor("x2", [T, D], F32, kind="ExternalInput").ap()
    wcat = nc.dram_tensor("wcat", [D, 6144], F32, kind="ExternalInput").ap()
    cos_in = nc.dram_tensor("cos", [T, 64], F32, kind="ExternalInput").ap()
    sin_in = nc.dram_tensor("sin", [T, 64], F32, kind="ExternalInput").ap()
    qs_o = nc.dram_tensor("qs", [T, 4096], BF16, kind="ExternalOutput").ap()
    sq_o = nc.dram_tensor("sq", [T, 4096], BF16, kind="ExternalOutput").ap()
    ks_o = nc.dram_tensor("ks", [T, 1024], BF16, kind="ExternalOutput").ap()
    gs_o = nc.dram_tensor("gs", [T, 1024], BF16, kind="ExternalOutput").ap()
    sk_o = nc.dram_tensor("sk", [T, 1024], BF16, kind="ExternalOutput").ap()
    vv_o = nc.dram_tensor("vv", [T, 1024], BF16, kind="ExternalOutput").ap()
    xn_hbm = nc.dram_tensor("xnh", [T, D], BF16, kind="Internal").ap()
    w_bf = nc.dram_tensor("w_bf", [D, 6144], BF16, kind="Internal").ap()

    units = []
    for j in range(2):
        for uc in range(12):
            for ti in range(8):
                units.append(dict(j=j, uc=uc, ti=ti, g=j * 12 + uc,
                                  kind="q" if uc < 8 else "kv",
                                  trow=j * 1024 + ti * 128))
    NU = len(units)

    with ExitStack() as ctx:
        e = ctx.enter_context
        xt = [e(nc.sbuf_tensor(f"xt{i}", [128, D], F32)) for i in range(2)]
        bns = e(nc.sbuf_tensor("bns", [128, 48], F32))
        mv = [e(nc.sbuf_tensor(f"mv{i}", [128, 2], F32)) for i in range(2)]
        sc = e(nc.sbuf_tensor("sc", [128, 4], F32))
        consts = e(nc.sbuf_tensor("consts", [128, 5], F32))
        xn_t = e(nc.sbuf_tensor("xnt", [128, D], BF16))
        xnf = e(nc.sbuf_tensor("xnf", [128, HC * 1024], BF16))
        wsl = [e(nc.sbuf_tensor(f"wsl{i}", [128, HC * 512], BF16)) for i in range(2)]
        cs = e(nc.sbuf_tensor("cs", [128, NTT * 64], F32))
        sn = e(nc.sbuf_tensor("sn", [128, NTT * 64], F32))
        pb = [e(nc.sbuf_tensor(f"pb{i}", [128, 512], F32)) for i in range(2)]
        esq = [e(nc.sbuf_tensor(f"esq{i}", [128, 512], BF16)) for i in range(2)]
        nmx = [e(nc.sbuf_tensor(f"nmx{i}", [128, 4], F32)) for i in range(2)]
        sums = [e(nc.sbuf_tensor(f"sums{i}", [128, 4], F32)) for i in range(2)]
        rsc = [e(nc.sbuf_tensor(f"rsc{i}", [128, 4], F32)) for i in range(2)]
        qso = [e(nc.sbuf_tensor(f"qso{i}", [128, 512], BF16)) for i in range(2)]
        sqo = [e(nc.sbuf_tensor(f"sqo{i}", [128, 512], BF16)) for i in range(2)]
        gt1 = e(nc.sbuf_tensor("gt1", [128, 256], F32))
        gt2 = e(nc.sbuf_tensor("gt2", [128, 256], F32))
        gt3 = e(nc.sbuf_tensor("gt3", [128, 256], F32))
        gso = [e(nc.sbuf_tensor(f"gso{i}", [128, 256], BF16)) for i in range(2)]
        vvo = [e(nc.sbuf_tensor(f"vvo{i}", [128, 256], BF16)) for i in range(2)]
        tmA = e(nc.sbuf_tensor("tmA", [128, 512], F32))
        tmB = e(nc.sbuf_tensor("tmB", [128, 512], F32))
        ps = [e(nc.psum_tensor(f"ps{i}", [128, 512], F32)) for i in range(4)]

        ldc = e(nc.semaphore("ldc"))
        ldx = [e(nc.semaphore(f"ldx{i}")) for i in range(2)]
        wc = e(nc.semaphore("wc"))
        wl = [e(nc.semaphore(f"wl{i}")) for i in range(2)]
        xf = e(nc.semaphore("xf"))
        vst = e(nc.semaphore("vst"))
        sst = e(nc.semaphore("sst"))
        mm = e(nc.semaphore("mm"))
        acs = e(nc.semaphore("acs"))
        vd = e(nc.semaphore("vd"))
        rpp = e(nc.semaphore("rpp"))
        sxn = [e(nc.semaphore(f"sxn{i}")) for i in range(2)]
        sop = [e(nc.semaphore(f"sop{i}")) for i in range(2)]

        ACT_N = [5 if u["kind"] == "q" else 7 for u in units]
        ACT_PFX = [0]
        for k in ACT_N:
            ACT_PFX.append(ACT_PFX[-1] + k)
        DVE_N = [5 if u["kind"] == "q" else 8 for u in units]
        DVE_PFX = [0]
        for k in DVE_N:
            DVE_PFX.append(DVE_PFX[-1] + k)
        WLP = {}
        _a = [0, 0]
        for _g in range(24):
            _uc = _g % 12
            _a[_g % 2] += 16 if _uc < 8 else 32
            WLP[_g] = _a[_g % 2]
        SOP = {}
        _b = [0, 0]
        for _i, _u in enumerate(units):
            _b[_i % 2] += 32 if _u["kind"] == "q" else 64
            SOP[_i] = _b[_i % 2]

        def rope_pair(eng, pbv, rov, tm, nh, cslc, snlc, half, sem):
            w = 64 * nh
            eng.wait_ge(sem[0], sem[1])
            lo = pbv[:, :, 0:64]; hi = pbv[:, :, 64:128]
            t1 = tm.ap()[:, 0:w].rearrange("p (h d) -> p h d", d=64)
            t2 = tm.ap()[:, w:2 * w].rearrange("p (h d) -> p h d", d=64)
            first, second = (lo, hi) if half == "lo" else (hi, lo)
            aa, cc = bass.broadcast_tensor_aps(first, cslc)
            eng.tensor_tensor(t1, aa, cc, ALU.mult)
            bb2, ss2 = bass.broadcast_tensor_aps(second, snlc)
            eng.tensor_tensor(t2, bb2, ss2, ALU.mult).then_inc(sem[0], 1)
            sem[1] += 1
            eng.wait_ge(sem[0], sem[1])
            out = rov[:, :, 0:64] if half == "lo" else rov[:, :, 64:128]
            eng.tensor_tensor(out, t1, t2, ALU.subtract if half == "lo" else ALU.add).then_inc(sem[0], 1)
            sem[1] += 1

        with nc.Block() as blk:
            @blk.gpsimd
            def _(g):
                for c in range(8):
                    g.dma_start(out=w_bf[c * 512:(c + 1) * 512, :],
                                in_=wcat[c * 512:(c + 1) * 512, :]).then_inc(wc, 16)
                g.wait_ge(ldc, 32)
                semp = [rpp, 0]
                for i, u in enumerate(units):
                    par = i % 2
                    tcos = u["j"] * 8 + u["ti"]
                    cslc = cs.ap()[:, tcos * 64:(tcos + 1) * 64].rearrange("p (r d) -> p r d", r=1)
                    snlc = sn.ap()[:, tcos * 64:(tcos + 1) * 64].rearrange("p (r d) -> p r d", r=1)
                    g.wait_ge(acs, ACT_PFX[i] + 1)
                    if i >= 2:
                        g.wait_ge(sop[par], SOP[i - 2])
                    if u["kind"] == "q":
                        pbv = pb[par].ap()[:, 0:512].rearrange("p (h d) -> p h d", d=128)
                        rov = sqo[par].ap()[:, 0:512].rearrange("p (h d) -> p h d", d=128)
                        rope_pair(g, pbv, rov, tmB, 4, cslc, snlc, "hi", semp)
                    else:
                        pbv = pb[par].ap()[:, 0:256].rearrange("p (h d) -> p h d", d=128)
                        rov = sqo[par].ap()[:, 0:256].rearrange("p (h d) -> p h d", d=128)
                        rope_pair(g, pbv, rov, tmB, 2, cslc, snlc, "hi", semp)

            @blk.vector
            def _(v):
                v.memset(consts[:, 0:1], 1e-5)
                v.memset(consts[:, 1:2], -0.5)
                v.memset(consts[:, 2:3], -1.0)
                v.memset(consts[:, 3:4], 1.0)
                v.memset(consts[:, 4:5], 0.0625).then_inc(vst, 1)
                for i in range(NTT):
                    v.wait_ge(ldx[i % 2], 16 * (i // 2 + 1))
                    if i >= 1:
                        v.wait_ge(vst, 1 + 2 * i)
                    if i >= 2:
                        v.wait_ge(sst, 5 * (i - 2) + 3)
                    for c in range(8):
                        ii = v.bn_stats(bns[:, 6 * c:6 * c + 6], xt[i % 2][:, 512 * c:512 * (c + 1)])
                    ii.then_inc(vst, 1)
                    v.wait_ge(vst, 1 + 2 * i + 1)
                    v.bn_aggr(mv[i % 2][:, :], bns[:, :]).then_inc(vst, 1)
                v.wait_ge(ldc, 32)
                semv = [vd, 0]
                for i, u in enumerate(units):
                    par = i % 2
                    nh = 4 if u["kind"] == "q" else 2
                    tcos = u["j"] * 8 + u["ti"]
                    cslc = cs.ap()[:, tcos * 64:(tcos + 1) * 64].rearrange("p (r d) -> p r d", r=1)
                    snlc = sn.ap()[:, tcos * 64:(tcos + 1) * 64].rearrange("p (r d) -> p r d", r=1)
                    v.wait_ge(acs, ACT_PFX[i] + 1)
                    for hh in range(nh):
                        ii = v.reduce_max(nmx[par][:, hh:hh + 1],
                                          pb[par][:, hh * 128:(hh + 1) * 128],
                                          axis=AX.X, negate=True)
                    ii.then_inc(vd, 1); semv[1] += 1
                    v.wait_ge(acs, ACT_PFX[i] + 1 + nh)
                    for hh in range(nh):
                        ii = v.reciprocal(rsc[par][:, hh:hh + 1], sums[par][:, hh:hh + 1])
                    ii.then_inc(vd, 1); semv[1] += 1
                    v.wait_ge(vd, semv[1])
                    if i >= 2:
                        v.wait_ge(sop[par], SOP[i - 2])
                    for hh in range(nh):
                        ii = v.tensor_scalar(qso[par][:, hh * 128:(hh + 1) * 128],
                                             esq[par][:, hh * 128:(hh + 1) * 128],
                                             rsc[par][:, hh:hh + 1], None, ALU.mult)
                    ii.then_inc(vd, 1); semv[1] += 1
                    if u["kind"] == "kv":
                        v.tensor_scalar(gt2[:, :], pb[par][:, 0:256], 0.0, None, ALU.min).then_inc(vd, 1)
                        semv[1] += 1
                        v.wait_ge(acs, ACT_PFX[i] + 6)
                        v.wait_ge(vd, semv[1])
                        v.tensor_tensor(gt2[:, :], gt2[:, :], gt3[:, :], ALU.subtract).then_inc(vd, 1)
                        semv[1] += 1
                        v.wait_ge(vd, semv[1])
                        v.tensor_scalar(gso[par][:, :], gt2[:, :], consts[:, 4:5], None, ALU.mult).then_inc(vd, 1)
                        semv[1] += 1
                    if u["kind"] == "q":
                        pbv = pb[par].ap()[:, 0:512].rearrange("p (h d) -> p h d", d=128)
                        rov = sqo[par].ap()[:, 0:512].rearrange("p (h d) -> p h d", d=128)
                        rope_pair(v, pbv, rov, tmA, 4, cslc, snlc, "lo", semv)
                    else:
                        pbv = pb[par].ap()[:, 0:256].rearrange("p (h d) -> p h d", d=128)
                        rov = sqo[par].ap()[:, 0:256].rearrange("p (h d) -> p h d", d=128)
                        rope_pair(v, pbv, rov, tmA, 2, cslc, snlc, "lo", semv)

            @blk.scalar
            def _(s):
                s.wait_ge(vst, 1)
                for i in range(NTT):
                    s.wait_ge(vst, 1 + 2 * (i + 1))
                    if i >= 1:
                        jj, kk = divmod(i - 1, 8)
                        s.wait_ge(sxn[jj], 16 * (kk + 1))
                    s.activation(sc[:, 0:1], mv[i % 2][:, 1:2], AF.Ln, bias=consts[:, 0:1]).then_inc(sst, 1)
                    s.wait_ge(sst, 5 * i + 1)
                    s.activation(sc[:, 1:2], sc[:, 0:1], AF.Exp, scale=consts[:, 1:2]).then_inc(sst, 1)
                    s.wait_ge(sst, 5 * i + 2)
                    s.activation(sc[:, 2:3], mv[i % 2][:, 0:1], AF.Identity, scale=sc[:, 1:2]).then_inc(sst, 1)
                    s.wait_ge(sst, 5 * i + 3)
                    s.activation(sc[:, 3:4], sc[:, 2:3], AF.Identity, scale=consts[:, 2:3]).then_inc(sst, 1)
                    s.wait_ge(sst, 5 * i + 4)
                    s.activation(xn_t[:, :], xt[i % 2][:, :], AF.Identity,
                                 scale=sc[:, 1:2], bias=sc[:, 3:4]).then_inc(sst, 1)
                s.wait_ge(sxn[0], 16 * 8)
                for c in range(HC):
                    s.dma_start_transpose(xnf[:, c * 1024:(c + 1) * 1024],
                                          xn_hbm[0:1024, c * 128:(c + 1) * 128]).then_inc(xf, 16)
                for i, u in enumerate(units):
                    if i == 96:
                        s.wait_ge(sxn[1], 16 * 8)
                        s.wait_ge(mm, 96)
                        for c in range(HC):
                            s.dma_start_transpose(xnf[:, c * 1024:(c + 1) * 1024],
                                                  xn_hbm[1024:2048, c * 128:(c + 1) * 128]).then_inc(xf, 16)
                    par = i % 2
                    nh = 4 if u["kind"] == "q" else 2
                    s.wait_ge(mm, i + 1)
                    bank = i % 4
                    if i >= 2:
                        s.wait_ge(vd, DVE_PFX[i - 1] - 1)
                        s.wait_ge(rpp, 2 * (i - 1) - 1)
                    s.activation(pb[par][:, :], ps[bank][:, :], AF.Identity).then_inc(acs, 1)
                    s.wait_ge(vd, DVE_PFX[i] + 1)
                    if i >= 2:
                        s.wait_ge(vd, DVE_PFX[i - 2] + 3)
                    for hh in range(nh):
                        s.activation(esq[par][:, hh * 128:(hh + 1) * 128],
                                     pb[par][:, hh * 128:(hh + 1) * 128], AF.Exp,
                                     bias=nmx[par][:, hh:hh + 1],
                                     accum_out=sums[par][:, hh:hh + 1]).then_inc(acs, 1)
                    if u["kind"] == "kv":
                        s.activation(gt1[:, :], pb[par][:, 0:256], AF.Abs).then_inc(acs, 1)
                        s.wait_ge(acs, ACT_PFX[i] + 4)
                        s.activation(gt1[:, :], gt1[:, :], AF.Exp, scale=consts[:, 2:3]).then_inc(acs, 1)
                        s.wait_ge(acs, ACT_PFX[i] + 5)
                        if u["ti"] >= 1:
                            s.wait_ge(vd, DVE_PFX[i - 1] + 5)
                        s.activation(gt3[:, :], gt1[:, :], AF.Ln, bias=consts[:, 3:4]).then_inc(acs, 1)
                        if i >= 2:
                            s.wait_ge(sop[par], SOP[i - 2])
                        s.activation(vvo[par].ap()[:, 0:256], ps[bank].ap()[:, 256:512],
                                     AF.Identity).then_inc(acs, 1)

            @blk.tensor
            def _(t):
                ev_need = []
                for i, u in enumerate(units):
                    ev_need.append(ACT_PFX[i + 1] if u["kind"] == "kv" else ACT_PFX[i] + 1)
                for i, u in enumerate(units):
                    j = u["j"]; g = u["g"]
                    t.wait_ge(wl[g % 2], WLP[g])
                    t.wait_ge(xf, 16 * HC * (j + 1))
                    if i >= 4:
                        t.wait_ge(acs, ev_need[i - 4])
                    bank = i % 4
                    for c in range(HC):
                        t.matmul(ps[bank][:, :],
                                 xnf[:, c * 1024 + u["ti"] * 128: c * 1024 + u["ti"] * 128 + 128],
                                 wsl[g % 2][:, c * 512:(c + 1) * 512],
                                 start=(c == 0), stop=(c == HC - 1)).then_inc(mm, 0 if c < HC - 1 else 1)

            @blk.sync
            def _(s2):
                s2.dma_start(out=cs.ap().rearrange("p (n d) -> p n d", d=64),
                             in_=cos_in.rearrange("(n p) d -> p n d", p=128)).then_inc(ldc, 16)
                s2.dma_start(out=sn.ap().rearrange("p (n d) -> p n d", d=64),
                             in_=sin_in.rearrange("(n p) d -> p n d", p=128)).then_inc(ldc, 16)
                for i in range(NTT):
                    if i >= 2:
                        s2.wait_ge(sst, 5 * (i - 1))
                    s2.dma_start(out=xt[i % 2][:, :],
                                 in_=x2_in[i * 128:(i + 1) * 128, :]).then_inc(ldx[i % 2], 16)
                    if i >= 1:
                        s2.wait_ge(sst, 5 * i)
                        s2.dma_start(out=xn_hbm[(i - 1) * 128:i * 128, :],
                                     in_=xn_t[:, :]).then_inc(sxn[(i - 1) // 8], 16)
                s2.wait_ge(sst, 5 * NTT)
                s2.dma_start(out=xn_hbm[(NTT - 1) * 128:NTT * 128, :],
                             in_=xn_t[:, :]).then_inc(sxn[1], 16)
                s2.wait_ge(wc, 128)

                def slab_load(g):
                    uc2 = g % 12
                    if uc2 < 8:
                        s2.dma_start(out=wsl[g % 2].ap().rearrange("p (c o) -> p c o", o=512),
                                     in_=w_bf[:, uc2 * 512:(uc2 + 1) * 512]
                                     .rearrange("(c p) o -> p c o", p=128)).then_inc(wl[g % 2], 16)
                    else:
                        m2 = uc2 - 8
                        s2.dma_start(out=wsl[g % 2].ap().rearrange("p (c g o) -> p c g o", c=HC, g=2)[:, :, 0, :],
                                     in_=w_bf[:, 4096 + 256 * m2:4096 + 256 * m2 + 256]
                                     .rearrange("(c p) o -> p c o", p=128)).then_inc(wl[g % 2], 16)
                        s2.dma_start(out=wsl[g % 2].ap().rearrange("p (c g o) -> p c g o", c=HC, g=2)[:, :, 1, :],
                                     in_=w_bf[:, 5120 + 256 * m2:5120 + 256 * m2 + 256]
                                     .rearrange("(c p) o -> p c o", p=128)).then_inc(wl[g % 2], 16)

                slab_load(0)
                slab_load(1)
                for i, u in enumerate(units):
                    par = i % 2
                    tr = u["trow"]
                    if u["kind"] == "q":
                        s2.wait_ge(vd, DVE_PFX[i] + 3)
                        s2.dma_start(out=qs_o[tr:tr + 128, u["uc"] * 512:(u["uc"] + 1) * 512],
                                     in_=qso[par][:, :]).then_inc(sop[par], 16)
                        s2.wait_ge(vd, DVE_PFX[i + 1])
                        s2.wait_ge(rpp, 2 * (i + 1))
                        s2.dma_start(out=sq_o[tr:tr + 128, u["uc"] * 512:(u["uc"] + 1) * 512],
                                     in_=sqo[par][:, :]).then_inc(sop[par], 16)
                    else:
                        m = u["uc"] - 8
                        s2.wait_ge(vd, DVE_PFX[i] + 3)
                        s2.dma_start(out=ks_o[tr:tr + 128, m * 256:(m + 1) * 256],
                                     in_=qso[par][:, 0:256]).then_inc(sop[par], 16)
                        s2.wait_ge(vd, DVE_PFX[i] + 6)
                        s2.dma_start(out=gs_o[tr:tr + 128, m * 256:(m + 1) * 256],
                                     in_=gso[par][:, :]).then_inc(sop[par], 16)
                        s2.wait_ge(vd, DVE_PFX[i + 1])
                        s2.wait_ge(rpp, 2 * (i + 1))
                        s2.dma_start(out=sk_o[tr:tr + 128, m * 256:(m + 1) * 256],
                                     in_=sqo[par][:, 0:256]).then_inc(sop[par], 16)
                        s2.wait_ge(acs, ACT_PFX[i + 1])
                        s2.dma_start(out=vv_o[tr:tr + 128, m * 256:(m + 1) * 256],
                                     in_=vvo[par][:, 0:256]).then_inc(sop[par], 16)
                    if i % 8 == 7 and i // 8 + 2 < 24:
                        s2.wait_ge(mm, 8 * (i // 8 + 1))
                        slab_load(i // 8 + 2)
                s2.wait_ge(sop[0], SOP[190])
                s2.wait_ge(sop[1], SOP[191])
                s2.wait_ge(sxn[0], 128)
                s2.wait_ge(sxn[1], 128)
    return nc

def build_l1_gla(limit=128, strip=()):
    nc = new_nc()
    S4 = 4096
    qs_in = nc.dram_tensor("qs4", [S4, 2048], BF16, kind="ExternalInput").ap()
    ks_in = nc.dram_tensor("ks4", [S4, 512], BF16, kind="ExternalInput").ap()
    gs_in = nc.dram_tensor("gs4", [S4, 512], BF16, kind="ExternalInput").ap()
    vv_in = nc.dram_tensor("vv4", [S4, 512], BF16, kind="ExternalInput").ap()
    ol_o = nc.dram_tensor("ol", [16, 128, S4], BF16, kind="ExternalOutput").ap()
    cm = np.zeros((128, 128), dtype=BF)
    tk = np.arange(128)[:, None]; tq = np.arange(128)[None, :]
    cm[:, :] = (tk <= tq).astype(BF)
    cmask_h = nc.inline_tensor(cm, name="cmask128")
    ident_h = nc.inline_tensor(np.eye(128, dtype=BF), name="ident128b")
    NCH = 32

    with ExitStack() as ctx:
        e = ctx.enter_context
        qsf = [e(nc.sbuf_tensor(f"qsf{i}", [128, 4 * S4], BF16)) for i in range(2)]
        ksf = [e(nc.sbuf_tensor(f"ksf{i}", [128, S4], BF16)) for i in range(2)]
        gsf = [e(nc.sbuf_tensor(f"gsf{i}", [128, S4], BF16)) for i in range(2)]
        vsb = [e(nc.sbuf_tensor(f"vsb{i}", [128, S4], BF16)) for i in range(2)]
        ones1 = e(nc.sbuf_tensor("ones1", [128, 128], F32))
        cm1 = e(nc.sbuf_tensor("cm1", [128, 1], F32))
        cmask = e(nc.sbuf_tensor("cmask", [128, 128], BF16))
        ident = e(nc.sbuf_tensor("identb", [128, 128], BF16))
        Gf = [e(nc.sbuf_tensor(f"Gf{i}", [128, 128], F32)) for i in range(2)]
        gcol = [e(nc.sbuf_tensor(f"gcol{i}", [128, 2], F32)) for i in range(2)]
        e1 = [e(nc.sbuf_tensor(f"e1_{i}", [128, 128], F32)) for i in range(2)]
        e2 = [e(nc.sbuf_tensor(f"e2_{i}", [128, 128], F32)) for i in range(2)]
        eg = [e(nc.sbuf_tensor(f"eg_{i}", [128, 128], F32)) for i in range(2)]
        dtot = [e(nc.sbuf_tensor(f"dtot{i}", [128, 1], F32)) for i in range(2)]
        ktl = [e(nc.sbuf_tensor(f"ktl{i}", [128, 128], BF16)) for i in range(2)]
        ktm = [e(nc.sbuf_tensor(f"ktm{i}", [128, 128], BF16)) for i in range(2)]
        qtl = [e(nc.sbuf_tensor(f"qtl{i}", [128, 512], BF16)) for i in range(2)]
        qgl = [e(nc.sbuf_tensor(f"qgl{i}", [128, 512], BF16)) for i in range(2)]
        atm = [e(nc.sbuf_tensor(f"atm{i}", [128, 512], BF16)) for i in range(2)]
        Ssb = e(nc.sbuf_tensor("Ssb", [128, 512], F32))
        Sbf = e(nc.sbuf_tensor("Sbf", [128, 512], BF16))
        olt = [e(nc.sbuf_tensor(f"olt{i}", [128, 512], BF16)) for i in range(2)]
        pat = [e(nc.psum_tensor(f"pat{i}", [128, 128], F32)) for i in range(2)]
        po = [e(nc.psum_tensor(f"po{i}", [128, 512], F32)) for i in range(2)]
        ptk = e(nc.psum_tensor("ptk", [128, 128], BF16))
        pkv = e(nc.psum_tensor("pkv", [128, 128], F32))

        ldc = e(nc.semaphore("ldc"))
        sl = [e(nc.semaphore(f"sl{i}")) for i in range(2)]
        vg1 = e(nc.semaphore("vg1"))
        ae = e(nc.semaphore("ae"))
        vk = e(nc.semaphore("vk"))
        tmk = e(nc.semaphore("tmk"))
        vtm = e(nc.semaphore("vtm"))
        ma = e(nc.semaphore("ma"))
        vat = e(nc.semaphore("vat"))
        mo = e(nc.semaphore("mo"))
        mkv = e(nc.semaphore("mkv"))
        vsc = e(nc.semaphore("vsc"))
        ape = e(nc.semaphore("ape"))
        sst = [e(nc.semaphore(f"sst{i}")) for i in range(2)]

        NUNIT = 4 * NCH

        with nc.Block() as blk:
            @blk.gpsimd
            def _(g):
                pass

            @blk.vector
            def _(v):
                v.memset(ones1[:, :], 1.0)
                v.memset(cm1[:, 0:1], -1.0)
                v.memset(Ssb[:, :], 0.0)
                v.memset(Sbf[:, :], 0.0).then_inc(vg1, 1)
                nv = 1
                v.wait_ge(vg1, 1)
                v.wait_ge(ldc, 32)
                for kvh in range(4):
                    for ci in range(NCH):
                        u = kvh * NCH + ci
                        if u >= limit:
                            break
                        par = u % 2
                        kp = kvh % 2
                        v.wait_ge(sl[kp], 112 * (kvh // 2 + 1))
                        if u >= 2:
                            v.wait_ge(ae, 4 * (u - 1))
                        if "scn" in strip:
                            v.nop().then_inc(vg1, 1)
                        else:
                            v.tensor_tensor_scan(Gf[par][:, :], gsf[kp][:, ci * 128:(ci + 1) * 128],
                                                 ones1[:, :], 0.0, ALU.add, ALU.mult).then_inc(vg1, 1)
                        nv += 1
                        v.wait_ge(vg1, nv)
                        v.tensor_scalar(gcol[par][:, 0:1], Gf[par][:, 127:128], -1.0, None, ALU.mult).then_inc(vg1, 1)
                        nv += 1
                        v.wait_ge(ae, 4 * u + 4)
                        if u >= 2:
                            v.wait_ge(tmk, u - 1)
                        v.tensor_tensor(ktl[par][:, :], ksf[kp][:, ci * 128:(ci + 1) * 128],
                                        e2[par][:, :], ALU.mult).then_inc(vk, 1)
                        if u >= 2:
                            v.wait_ge(ma, 4 * (u - 1))
                            v.wait_ge(mo, 4 * (u - 1))
                        for hh in range(4):
                            qv = qsf[kp].ap()[:, hh * S4 + ci * 128: hh * S4 + (ci + 1) * 128]
                            v.tensor_tensor(qtl[par][:, hh * 128:(hh + 1) * 128], qv, e1[par][:, :], ALU.mult)
                            ii = v.tensor_tensor(qgl[par][:, hh * 128:(hh + 1) * 128], qv, eg[par][:, :], ALU.mult)
                        ii.then_inc(vk, 1)
                        v.wait_ge(tmk, u + 1)
                        v.tensor_copy(ktm[par][:, :], ptk[:, :]).then_inc(vtm, 1)
                        if u >= 2:
                            v.wait_ge(mo, 4 * (u - 1))
                        for hh in range(4):
                            v.wait_ge(ma, 4 * u + hh + 1)
                            v.tensor_tensor(atm[par][:, hh * 128:(hh + 1) * 128],
                                            pat[hh % 2][:, :], cmask[:, :], ALU.mult).then_inc(vat, 1)
                        v.wait_ge(mkv, u + 1)
                        v.wait_ge(mo, 4 * u + 4)
                        sw = Ssb[:, kvh * 128:(kvh + 1) * 128]
                        v.tensor_scalar(sw, sw, dtot[par][:, 0:1], None, ALU.mult).then_inc(vsc, 1)
                        v.wait_ge(vsc, 3 * u + 1)
                        v.tensor_tensor(sw, sw, pkv[:, :], ALU.add).then_inc(vsc, 1)
                        v.wait_ge(vsc, 3 * u + 2)
                        v.tensor_copy(Sbf[:, kvh * 128:(kvh + 1) * 128], sw).then_inc(vsc, 1)

            @blk.scalar
            def _(s):
                for kvh in range(4):
                    for ci in range(NCH):
                        u = kvh * NCH + ci
                        if u >= limit:
                            break
                        par = u % 2
                        s.wait_ge(vg1, 1 + 2 * (u + 1))
                        if u >= 2:
                            s.wait_ge(vk, 2 * (u - 1))
                            s.wait_ge(vsc, 3 * (u - 2) + 1)
                        if "ex" in strip:
                            s.nop().then_inc(ae, 4)
                        else:
                            s.activation(e1[par][:, :], Gf[par][:, :], AF.Exp, bias=gcol[par][:, 0:1]).then_inc(ae, 1)
                            s.activation(e2[par][:, :], Gf[par][:, :], AF.Exp, scale=cm1[:, 0:1],
                                         bias=Gf[par][:, 127:128]).then_inc(ae, 1)
                            s.activation(eg[par][:, :], Gf[par][:, :], AF.Exp).then_inc(ae, 1)
                            s.activation(dtot[par][:, 0:1], Gf[par][:, 127:128], AF.Exp).then_inc(ae, 1)
                        s.wait_ge(mo, 4 * u + 4)
                        if u >= 2:
                            s.wait_ge(sst[u % 2], 64 * ((u - 2) // 2 + 1))
                        if "ev" in strip:
                            s.nop().then_inc(ape, 4)
                        else:
                            s.activation(olt[par][:, :], po[par][:, :], AF.Identity).then_inc(ape, 4)

            @blk.tensor
            def _(t):
                t.wait_ge(ldc, 32)
                for kvh in range(4):
                    for ci in range(NCH):
                        u = kvh * NCH + ci
                        if u >= limit:
                            break
                        par = u % 2
                        kp = kvh % 2
                        t.wait_ge(vk, 2 * u + 1)
                        if u >= 1:
                            t.wait_ge(vtm, u)
                        t.transpose(ptk[:, :], ktl[par][:, :], ident[:, :]).then_inc(tmk, 1)
                        t.wait_ge(vk, 2 * u + 2)
                        for hh in range(4):
                            if u >= 1 or hh >= 2:
                                t.wait_ge(vat, 4 * u + hh - 1)
                            t.matmul(pat[hh % 2][:, :], ktl[par][:, :],
                                     qtl[par][:, hh * 128:(hh + 1) * 128],
                                     start=True, stop=True).then_inc(ma, 1)
                        for hh in range(4):
                            t.wait_ge(vat, 4 * u + hh + 1)
                            if ci > 0:
                                t.wait_ge(vsc, 3 * u)
                            if u >= 2:
                                t.wait_ge(ape, 4 * (u - 1))
                            ii = t.matmul(po[par][:, hh * 128:(hh + 1) * 128],
                                          vsb[kp][:, ci * 128:(ci + 1) * 128],
                                          atm[par][:, hh * 128:(hh + 1) * 128],
                                          start=True, stop=(ci == 0))
                            if ci > 0:
                                t.matmul(po[par][:, hh * 128:(hh + 1) * 128],
                                         Sbf[:, kvh * 128:(kvh + 1) * 128],
                                         qgl[par][:, hh * 128:(hh + 1) * 128],
                                         start=False, stop=True).then_inc(mo, 1)
                            else:
                                ii.then_inc(mo, 1)
                        t.wait_ge(vtm, u + 1)
                        if u >= 1:
                            t.wait_ge(vsc, 3 * u - 1)
                        t.matmul(pkv[:, :], ktm[par][:, :],
                                 vsb[kp][:, ci * 128:(ci + 1) * 128],
                                 start=True, stop=True).then_inc(mkv, 1)

            @blk.sync
            def _(s2):
                s2.dma_start(out=cmask[:, :], in_=cmask_h.ap()[:, :]).then_inc(ldc, 16)
                s2.dma_start(out=ident[:, :], in_=ident_h.ap()[:, :]).then_inc(ldc, 16)
                for kvh in range(min(4, (limit + NCH - 1) // NCH)):
                    kp = kvh % 2
                    if kvh >= 2:
                        s2.wait_ge(mo, 4 * NCH * (kvh - 1))
                    s2.dma_start_transpose(gsf[kp][:, :], gs_in[:, kvh * 128:(kvh + 1) * 128]).then_inc(sl[kp], 16)
                    s2.dma_start_transpose(ksf[kp][:, :], ks_in[:, kvh * 128:(kvh + 1) * 128]).then_inc(sl[kp], 16)
                    for hh in range(4):
                        s2.dma_start_transpose(qsf[kp][:, hh * S4:(hh + 1) * S4],
                                               qs_in[:, (kvh * 4 + hh) * 128:(kvh * 4 + hh + 1) * 128]).then_inc(sl[kp], 16)
                    s2.dma_start(out=vsb[kp].ap().rearrange("p (c o) -> p c o", o=128),
                                 in_=vv_in[:, kvh * 128:(kvh + 1) * 128]
                                 .rearrange("(c p) o -> p c o", p=128)).then_inc(sl[kp], 16)
                    for ci in range(NCH):
                        u = kvh * NCH + ci
                        if u >= limit:
                            break
                        par = u % 2
                        s2.wait_ge(ape, 4 * (u + 1))
                        for hh in range(4):
                            if "st" in strip:
                                s2.sem_inc(sst[par], 16)
                            else:
                                s2.dma_start(out=ol_o[kvh * 4 + hh, :, ci * 128:(ci + 1) * 128],
                                             in_=olt[par][:, hh * 128:(hh + 1) * 128]).then_inc(sst[par], 16)
                s2.wait_ge(sst[0], 64 * ((limit + 1) // 2))
                s2.wait_ge(sst[1], 64 * (limit // 2))
    return nc

def build_l1_swa(debug=False):
    nc = new_nc()
    EXT = 2176
    sq_in = nc.dram_tensor("sq", [T, 4096], BF16, kind="ExternalInput").ap()
    sk_in = nc.dram_tensor("ske", [EXT, 1024], BF16, kind="ExternalInput").ap()
    vv_in = nc.dram_tensor("vve", [EXT, 1024], BF16, kind="ExternalInput").ap()
    ol_in = nc.dram_tensor("oli", [32, 128, T], BF16, kind="ExternalInput").ap()
    bnd_in = nc.dram_tensor("bndc", [128, 1], F32, kind="ExternalInput").ap()
    x2_in = nc.dram_tensor("x2", [T, D], F32, kind="ExternalInput").ap()
    wot = nc.dram_tensor("wot", [D, D], F32, kind="ExternalInput").ap()
    x3_out = nc.dram_tensor("x3", [T, D], F32, kind="ExternalOutput").ap()
    ogd = (nc.dram_tensor("ogd", [128, 32 * 512], BF16, kind="ExternalOutput").ap()
           if debug == 1 else None)
    aod = (nc.dram_tensor("aod", [4, 128, D], BF16, kind="ExternalOutput").ap()
           if debug == 1 else None)
    if debug == 2:
        d2 = {n: nc.dram_tensor(n, [128, w], dt, kind="ExternalOutput").ap()
              for n, w, dt in [("dq", 512, BF16), ("dk", 640, BF16), ("dv", 640, BF16),
                               ("de0", 512, BF16), ("de1", 512, BF16), ("drb", 512, BF16),
                               ("dt2", 512, F32), ("dt3", 512, F32), ("dol", 512, BF16),
                               ("dog", 512, BF16)]}
    wo_bf = nc.dram_tensor("wo_bf", [D, D], BF16, kind="Internal").ap()
    ones_h = nc.inline_tensor(np.ones((1, 128), dtype=BF), name="ones_row")
    ones_c = nc.inline_tensor(np.ones((128, 1), dtype=BF), name="ones_col")
    tk = np.arange(128)[:, None]; tq = np.arange(512)[None, :]
    bands = []
    for s5 in range(5):
        rel = 128 - 128 * s5 + tq - tk
        bands.append(((rel >= 0) & (rel <= 64)).astype(BF))
    mAB_h = nc.inline_tensor(np.concatenate(bands, 1), name="mB5")

    with ExitStack() as ctx:
        e = ctx.enter_context
        skf = [e(nc.sbuf_tensor(f"skf{i}", [128, 640], BF16)) for i in range(2)]
        vsb = [e(nc.sbuf_tensor(f"vsb{i}", [128, 640], BF16)) for i in range(2)]
        qsl = [e(nc.sbuf_tensor(f"qsl{i}", [128, 512], BF16)) for i in range(2)]
        olt = [e(nc.sbuf_tensor(f"olt{i}", [128, 512], BF16)) for i in range(2)]
        expT = [e(nc.sbuf_tensor(f"expT{i}", [128, 512], BF16)) for i in range(4)]
        mAB = e(nc.sbuf_tensor("mAB_sb", [128, 5 * 512], BF16))
        bnd = e(nc.sbuf_tensor("bnd_sb", [128, 2], F32))
        smc = e(nc.sbuf_tensor("smc", [128, 1], F32))
        onesr = e(nc.sbuf_tensor("onesr", [1, 128], BF16))
        onesc = e(nc.sbuf_tensor("onesc", [128, 1], BF16))
        rcp = e(nc.sbuf_tensor("rcp", [1, 512], F32))
        rcpb = e(nc.sbuf_tensor("rcpb", [1, 512], BF16))
        rb_sb = e(nc.sbuf_tensor("rb_sb", [128, 512], BF16))
        t2b = e(nc.sbuf_tensor("t2b", [128, 512], F32))
        t3b = e(nc.sbuf_tensor("t3b", [128, 512], F32))
        og = e(nc.sbuf_tensor("og", [128, 32 * 512], BF16))
        wdsl = e(nc.sbuf_tensor("wdsl", [128, HC * 512], BF16))
        ao = [e(nc.sbuf_tensor(f"ao{i}", [128, D], BF16)) for i in range(4)]
        xt = [e(nc.sbuf_tensor("xt0", [128, D], F32))]
        tmpc = e(nc.sbuf_tensor("tmpc", [128, D], F32))
        bns = e(nc.sbuf_tensor("bns", [128, 48], F32))
        mv2 = [e(nc.sbuf_tensor(f"mv2{i}", [128, 2], F32)) for i in range(2)]
        sc2 = e(nc.sbuf_tensor("sc2", [128, 4], F32))
        consts = e(nc.sbuf_tensor("consts", [128, 4], F32))
        scb = [e(nc.psum_tensor(f"scb{i}", [128, 512], F32)) for i in range(4)]
        cps = [e(nc.psum_tensor(f"cps{i}", [128, 512], F32)) for i in range(2)]
        su = e(nc.psum_tensor("su", [33, 512], F32))
        rbp = e(nc.psum_tensor("rbp", [128, 512], F32))
        
        ldc = e(nc.semaphore("ldc"))
        wdc = e(nc.semaphore("wdc"))
        kgl = [e(nc.semaphore(f"kgl{i}")) for i in range(2)]
        qls = [e(nc.semaphore(f"qls{i}")) for i in range(2)]
        oll = [e(nc.semaphore(f"oll{i}")) for i in range(2)]
        msc = e(nc.semaphore("msc"))
        mpv = e(nc.semaphore("mpv"))
        msu = e(nc.semaphore("msu"))
        mrb = e(nc.semaphore("mrb"))
        exps = e(nc.semaphore("exps"))
        dmk = e(nc.semaphore("dmk"))
        vnr = e(nc.semaphore("vnr"))
        wdl = e(nc.semaphore("wdl"))
        mdd = e(nc.semaphore("mdd"))
        dev = e(nc.semaphore("dev"))
        vst2 = e(nc.semaphore("vst2"))
        sst2 = e(nc.semaphore("sst2"))
        cmb = e(nc.semaphore("cmb"))
        cmb2 = e(nc.semaphore("cmb2"))
        xld = e(nc.semaphore("xld"))
        sx1 = e(nc.semaphore("sx1"))
        dbg = e(nc.semaphore("dbg"))

        NHG = 128

        with nc.Block() as blk:
            @blk.gpsimd
            def _(g):
                for oc in range(8):
                    if oc >= 1:
                        g.wait_ge(wdc, 16 * oc)
                    g.dma_start(out=wo_bf[:, oc * 512:(oc + 1) * 512],
                                in_=wot[:, oc * 512:(oc + 1) * 512]).then_inc(wdc, 16)
                if debug == 1:
                    g.wait_ge(vnr, 160)
                    g.dma_start(out=ogd[:, :], in_=og[:, :]).then_inc(dbg, 16)
                    g.wait_ge(dev, 32)
                    for tt in range(4):
                        g.dma_start(out=aod[tt, :, :], in_=ao[tt][:, :]).then_inc(dbg, 16)
                if debug == 2:
                    g.wait_ge(vnr, 5)
                    for n, src in [("dq", qsl[0][:, :]), ("dk", skf[0][:, :]),
                                   ("dv", vsb[0][:, :]), ("de0", expT[0][:, :]),
                                   ("de1", expT[1][:, :]), ("drb", rb_sb[:, :]),
                                   ("dt2", t2b[:, :]), ("dt3", t3b[:, :]),
                                   ("dol", olt[0][:, :]), ("dog", og[:, 0:512])]:
                        g.dma_start(out=d2[n][:, :], in_=src).then_inc(dbg, 16)
                for gt in range(16):
                    g.wait_ge(cmb, gt + 1)
                    g.wait_ge(xld, 16 * (gt + 1))
                    if gt >= 1:
                        g.wait_ge(sx1, 16 * gt)
                    g.tensor_tensor(tmpc[:, :], xt[0][:, :], tmpc[:, :], ALU.add).then_inc(cmb2, 1)

            @blk.vector
            def _(v):
                v.memset(consts[:, 0:1], 1e-5)
                v.memset(consts[:, 1:2], -0.5)
                v.memset(consts[:, 2:3], -1.0)
                v.memset(consts[:, 3:4], 0.5)
                v.memset(smc[:, 0:1], 1.0 / np.sqrt(128.0))
                v.memset(bnd[:, 1:2], 0.0).then_inc(vst2, 1)
                v.wait_ge(ldc, 64)
                for hg in range(NHG):
                    grp = hg // 32
                    head = hg % 32
                    par = hg % 2
                    if debug == 2 and hg == 1:
                        v.wait_ge(dbg, 160)
                    for s in range(5):
                        v.wait_ge(exps, 5 * hg + s + 1)
                        eb = (5 * hg + s) % 4
                        v.tensor_tensor(expT[eb][:, :], expT[eb][:, :],
                                        mAB[:, s * 512:(s + 1) * 512],
                                        ALU.mult).then_inc(dmk, 1)
                    v.wait_ge(msu, hg + 1)
                    v.wait_ge(vnr, 5 * hg)
                    v.reciprocal(rcp[0:1, :], su[32 * par:32 * par + 1, :]).then_inc(vnr, 1)
                    v.wait_ge(vnr, 5 * hg + 1)
                    if hg >= 1:
                        v.wait_ge(mrb, hg)
                    v.tensor_scalar(rcpb[0:1, :], rcp[0:1, :], 0.5, None, ALU.mult).then_inc(vnr, 1)
                    v.wait_ge(mrb, hg + 1)
                    v.tensor_copy(rb_sb[:, :], rbp[:, :]).then_inc(vnr, 1)
                    v.wait_ge(mpv, 5 * (hg + 1))
                    v.wait_ge(vnr, 5 * hg + 3)
                    if grp >= 1:
                        v.wait_ge(mdd, 32 * grp)
                    v.wait_ge(oll[par], 16 * (hg // 2 + 1))
                    v.tensor_tensor(t2b[:, :], cps[par][:, :], rb_sb[:, :], ALU.mult)
                    v.tensor_scalar(t3b[:, :], olt[par][:, :], 0.5, None, ALU.mult).then_inc(vnr, 1)
                    v.wait_ge(vnr, 5 * hg + 4)
                    if debug == 1 and 32 <= hg < 64:
                        v.wait_ge(dbg, 80)
                    v.tensor_tensor(og[:, head * 512:(head + 1) * 512],
                                    t2b[:, :], t3b[:, :], ALU.add).then_inc(vnr, 1)
                    if hg % 32 == 31:
                        gg = grp
                        for tt in range(4):
                            gt = gg * 4 + tt
                            v.wait_ge(dev, 32 * gg + 28 + tt + 1)
                            if gt >= 2:
                                v.wait_ge(sst2, 5 * (gt - 2) + 3)
                            if gt >= 1:
                                v.wait_ge(vst2, 1 + 2 * gt)
                            for c in range(8):
                                ii = v.bn_stats(bns[:, 6 * c:6 * c + 6], ao[tt][:, 512 * c:512 * (c + 1)])
                            ii.then_inc(vst2, 1)
                            v.wait_ge(vst2, 1 + 2 * gt + 1)
                            v.bn_aggr(mv2[gt % 2][:, :], bns[:, :]).then_inc(vst2, 1)
                            v.wait_ge(sst2, 5 * gt + 4)
                            if gt >= 1:
                                v.wait_ge(sx1, 16 * gt)
                            v.tensor_scalar(tmpc[:, :], ao[tt][:, :], mv2[gt % 2][:, 0:1],
                                            sc2[:, 1:2], ALU.subtract, ALU.mult).then_inc(cmb, 1)

            @blk.scalar
            def _(s):
                s.wait_ge(ldc, 64)
                s.wait_ge(vst2, 1)
                s.dma_start_transpose(skf[0][:, :],
                                      sk_in[0:640, 0:128]).then_inc(kgl[0], 16)
                for hg in range(NHG):
                    grp = hg // 32
                    head = hg % 32
                    par = hg % 2
                    if debug == 2 and hg == 1:
                        s.wait_ge(dbg, 160)
                    if hg % 4 == 0 and hg // 4 + 1 < 32:
                        Kn = hg // 4 + 1
                        if Kn >= 2:
                            s.wait_ge(mpv, 5 * 4 * (Kn - 1))
                        s.dma_start_transpose(skf[Kn % 2][:, :],
                                              sk_in[(Kn // 8) * 512:(Kn // 8) * 512 + 640,
                                                    (Kn % 8) * 128:(Kn % 8 + 1) * 128]).then_inc(kgl[Kn % 2], 16)
                    if hg >= 2:
                        s.wait_ge(msc, 5 * (hg - 1))
                    s.dma_start_transpose(qsl[hg % 2][:, :],
                                          sq_in[grp * 512:(grp + 1) * 512,
                                                head * 128:(head + 1) * 128]).then_inc(qls[hg % 2], 16)
                    for seq in range(5):
                        gs = 5 * hg + seq
                        s.wait_ge(msc, gs + 1)
                        if gs >= 4:
                            s.wait_ge(mpv, gs - 3)
                        bcol = bnd[:, 0:1] if (grp == 0 and seq == 0) else bnd[:, 1:2]
                        s.activation(expT[gs % 4][:, :], scb[gs % 4][:, :], AF.Exp,
                                     scale=smc[:, 0:1], bias=bcol).then_inc(exps, 1)
                    if hg >= 2:
                        s.wait_ge(vnr, 5 * (hg - 2) + 5)
                    s.dma_start(out=olt[par][:, :],
                                in_=ol_in[head, :, grp * 512:(grp + 1) * 512]).then_inc(oll[par], 16)
                    if hg % 32 == 31:
                        gg = grp
                        for oc in range(8):
                            for tt in range(4):
                                k = 32 * gg + 4 * oc + tt
                                s.wait_ge(mdd, k + 1)
                                if debug == 1 and k == 32:
                                    s.wait_ge(dbg, 80)
                                if (gg >= 1 or oc >= 1) and oc == 0:
                                    s.wait_ge(cmb, 4 * (gg - 1) + tt + 1)
                                s.activation(ao[tt][:, oc * 512:(oc + 1) * 512],
                                             scb[k % 4][:, :], AF.Identity).then_inc(dev, 1)
                        for tt in range(4):
                            gt = gg * 4 + tt
                            s.wait_ge(vst2, 1 + 2 * (gt + 1))
                            s.activation(sc2[:, 0:1], mv2[gt % 2][:, 1:2], AF.Ln, bias=consts[:, 0:1]).then_inc(sst2, 1)
                            s.wait_ge(sst2, 5 * gt + 1)
                            s.activation(sc2[:, 1:2], sc2[:, 0:1], AF.Exp, scale=consts[:, 1:2]).then_inc(sst2, 1)
                            s.wait_ge(sst2, 5 * gt + 2)
                            s.activation(sc2[:, 2:3], sc2[:, 1:2], AF.Identity).then_inc(sst2, 1)
                            s.wait_ge(sst2, 5 * gt + 3)
                            s.activation(sc2[:, 2:3], sc2[:, 2:3], AF.Identity).then_inc(sst2, 1)
                            s.wait_ge(sst2, 5 * gt + 4)
                            s.activation(sc2[:, 2:3], sc2[:, 2:3], AF.Identity).then_inc(sst2, 1)

            @blk.tensor
            def _(t):
                t.wait_ge(ldc, 64)
                for hg in range(NHG):
                    grp = hg // 32
                    kvi = (hg % 32) // 4
                    par = hg % 2
                    K = grp * 8 + kvi
                    t.wait_ge(kgl[K % 2], 32 * (K // 2 + 1))
                    t.wait_ge(qls[hg % 2], 16 * (hg // 2 + 1))
                    if grp >= 1 and hg % 32 == 0:
                        t.wait_ge(dev, 32 * grp)

                    def score_mm(seq):
                        gs = 5 * hg + seq
                        if gs >= 4:
                            t.wait_ge(exps, gs - 3)
                        t.matmul(scb[gs % 4][:, :], skf[K % 2][:, seq * 128:(seq + 1) * 128],
                                 qsl[hg % 2][:, :], start=True, stop=True).then_inc(msc, 1)

                    def pv_mm(seq):
                        gs = 5 * hg + seq
                        t.wait_ge(exps, gs + 1)
                        t.wait_ge(dmk, gs + 1)
                        if seq == 0:
                            if hg >= 2:
                                t.wait_ge(vnr, 5 * (hg - 2) + 1)
                                t.wait_ge(vnr, 5 * (hg - 2) + 4)
                        jj = t.matmul(su[32 * par:32 * par + 1, :], onesc[:, 0:1],
                                      expT[gs % 4][:, :], start=(seq == 0), stop=(seq == 4))
                        if seq == 4:
                            jj.then_inc(msu, 1)
                        t.matmul(cps[par][:, :],
                                 vsb[K % 2][:, seq * 128:(seq + 1) * 128],
                                 expT[gs % 4][:, :], start=(seq == 0), stop=(seq == 4)).then_inc(mpv, 1)

                    def rb_mm(u):
                        t.wait_ge(vnr, 5 * u + 2)
                        if u >= 1:
                            t.wait_ge(vnr, 5 * (u - 1) + 3)
                        t.matmul(rbp[:, :], onesr[0:1, :], rcpb[0:1, :], start=True, stop=True).then_inc(mrb, 1)

                    score_mm(0)
                    score_mm(1)
                    score_mm(2)
                    if hg >= 1 and hg % 32 != 0:
                        rb_mm(hg - 1)
                    score_mm(3)
                    pv_mm(0)
                    score_mm(4)
                    pv_mm(1)
                    pv_mm(2)
                    pv_mm(3)
                    pv_mm(4)
                    if hg % 32 == 31:
                        rb_mm(hg)
                        gg = grp
                        t.wait_ge(vnr, 5 * 32 * (gg + 1))
                        for oc in range(8):
                            t.wait_ge(wdl, 16 * (8 * gg + oc + 1))
                            for tt in range(4):
                                k = 32 * gg + 4 * oc + tt
                                bank = k % 4
                                if k >= 4:
                                    t.wait_ge(dev, k - 3)
                                for hh in range(32):
                                    t.matmul(scb[bank][:, :],
                                             og[:, hh * 512 + tt * 128: hh * 512 + tt * 128 + 128],
                                             wdsl[:, hh * 512:(hh + 1) * 512],
                                             start=(hh == 0), stop=(hh == 31)).then_inc(mdd, 0 if hh < 31 else 1)

            @blk.sync
            def _(s2):
                s2.dma_start(out=onesr[0:1, :], in_=ones_h.ap()[0:1, :]).then_inc(ldc, 16)
                s2.dma_start(out=onesc[:, 0:1], in_=ones_c.ap()[:, 0:1]).then_inc(ldc, 16)
                s2.dma_start(out=mAB[:, :], in_=mAB_h.ap()[:, :]).then_inc(ldc, 16)
                s2.dma_start(out=bnd[:, 0:1], in_=bnd_in[:, 0:1]).then_inc(ldc, 16)
                for grp in range(4):
                    for kvi in range(8):
                        K = grp * 8 + kvi
                        if K >= 2:
                            s2.wait_ge(mpv, 5 * 4 * (K - 1))
                        s2.dma_start(out=vsb[K % 2].ap().rearrange("p (c o) -> p c o", o=128),
                                     in_=vv_in[grp * 512:grp * 512 + 640, kvi * 128:(kvi + 1) * 128]
                                     .rearrange("(c p) o -> p c o", p=128)).then_inc(kgl[K % 2], 16)
                    for oc in range(8):
                        G = grp * 8 + oc
                        if G >= 1:
                            s2.wait_ge(mdd, 4 * G)
                        if grp == 0:
                            s2.wait_ge(wdc, 16 * (oc + 1))
                        s2.dma_start(out=wdsl.ap().rearrange("p (c o) -> p c o", o=512),
                                     in_=wo_bf[:, oc * 512:(oc + 1) * 512]
                                     .rearrange("(c p) o -> p c o", p=128)).then_inc(wdl, 16)
                    for tt in range(4):
                        gt = grp * 4 + tt
                        if gt >= 1:
                            s2.wait_ge(cmb2, gt)
                        s2.dma_start(out=xt[0][:, :],
                                     in_=x2_in[gt * 128:(gt + 1) * 128, :]).then_inc(xld, 16)
                        s2.wait_ge(cmb2, gt + 1)
                        s2.dma_start(out=x3_out[gt * 128:(gt + 1) * 128, :],
                                     in_=tmpc[:, :]).then_inc(sx1, 16)
                s2.wait_ge(sx1, 16 * 16)
    return nc

BSTAGES = [[(0,1),(2,3),(4,5),(6,7)], [(0,2),(1,3),(4,6),(5,7)], [(1,2),(5,6)],
           [(0,4),(1,5),(2,6),(3,7)], [(2,4),(3,5)], [(1,2),(3,4),(5,6)]]

def build_l1_ffn():
    nc = new_nc()
    x3_in = nc.dram_tensor("x3", [T, D], F32, kind="ExternalInput").ap()
    wbt = nc.dram_tensor("wb1t", [D, D], F32, kind="ExternalInput").ap()
    bbr = nc.dram_tensor("bb1r", [1, D], F32, kind="ExternalInput").ap()
    wrt = nc.dram_tensor("wrt", [D, 8], F32, kind="ExternalInput").ap()
    brr = nc.dram_tensor("brr", [1, 8], F32, kind="ExternalInput").ap()
    lat = nc.dram_tensor("lat", [D, 128], F32, kind="ExternalInput").ap()
    lbt = nc.dram_tensor("lbt", [128, D], F32, kind="ExternalInput").ap()
    xo_out = nc.dram_tensor("xo", [T, D], F32, kind="ExternalOutput").ap()
    xn_hbm = nc.dram_tensor("xnh", [T, D], BF16, kind="Internal").ap()
    wb_bf = nc.dram_tensor("wb_bf", [D, D], BF16, kind="Internal").ap()
    ones_h = nc.inline_tensor(np.ones((1, 128), dtype=BF), name="ones_row")
    ident_h = nc.inline_tensor(np.eye(128, dtype=BF), name="ident128")
    kv_h = nc.inline_tensor(np.tile(np.arange(1, 9, dtype=np.float32), (128, 1)),
                            name="kvec8")

    with ExitStack() as ctx:
        e = ctx.enter_context
        xt = [e(nc.sbuf_tensor("xt0", [128, D], F32))]
        bns = e(nc.sbuf_tensor("bns", [128, 48], F32))
        mv = [e(nc.sbuf_tensor(f"mv{i}", [128, 2], F32)) for i in range(2)]
        sc = e(nc.sbuf_tensor("sc", [128, 4], F32))
        consts = e(nc.sbuf_tensor("consts", [128, 4], F32))
        xn_t = e(nc.sbuf_tensor("xnt", [128, D], BF16))
        xnf = e(nc.sbuf_tensor("xnf", [128, HC * 512], BF16))
        wsl = e(nc.sbuf_tensor("wsl", [128, HC * 512], BF16))
        wr_b = e(nc.sbuf_tensor("wr_b", [128, HC * 8], BF16))
        la_b = e(nc.sbuf_tensor("la_b", [128, HC * 128], BF16))
        lb_b = e(nc.sbuf_tensor("lb_b", [128, D], BF16))
        bb_b = e(nc.sbuf_tensor("bb_b", [1, D], BF16))
        br_b = e(nc.sbuf_tensor("br_b", [1, 8], BF16))
        onesr = e(nc.sbuf_tensor("onesr", [1, 128], BF16))
        ident = e(nc.sbuf_tensor("identb", [128, 128], BF16))
        kvec = e(nc.sbuf_tensor("kvec", [128, 8], F32))
        ones8 = e(nc.sbuf_tensor("ones8", [128, 8], F32))
        zt = e(nc.sbuf_tensor("zt", [128, 8], F32))
        zs = e(nc.sbuf_tensor("zs", [128, 8], F32))
        zc = e(nc.sbuf_tensor("zc", [128, 8], F32))
        ztm = e(nc.sbuf_tensor("ztm", [128, 8], F32))
        sup = e(nc.sbuf_tensor("sup", [128, 8], F32))
        scr = e(nc.sbuf_tensor("scr", [128, 4], F32))
        wsb = e(nc.sbuf_tensor("wsb", [128, 8], F32))
        hsb = [e(nc.sbuf_tensor(f"hsb{i}", [128, 128], BF16)) for i in range(2)]
        hw = [e(nc.sbuf_tensor(f"hw{i}", [128, 128], BF16)) for i in range(2)]
        hwf = e(nc.sbuf_tensor("hwf", [128, 4 * 128], BF16))
        ao = [e(nc.sbuf_tensor(f"ao{i}", [128, D], BF16)) for i in range(4)]
        tmpc = e(nc.sbuf_tensor("tmpc", [128, D], F32))
        xr = e(nc.sbuf_tensor("xr", [128, D], F32))
        ps = [e(nc.psum_tensor(f"ps{i}", [128, 512], F32)) for i in range(2)]
        pr = e(nc.psum_tensor("pr", [128, 8], F32))
        ph = [e(nc.psum_tensor(f"ph{i}", [128, 128], F32)) for i in range(2)]
        pth = [e(nc.psum_tensor(f"pth{i}", [128, 128], BF16)) for i in range(2)]

        ldc = e(nc.semaphore("ldc"))
        ldx = e(nc.semaphore("ldx"))
        gc = e(nc.semaphore("gc"))
        wcc = e(nc.semaphore("wcc"))
        wbl = e(nc.semaphore("wbl"))
        xfl = e(nc.semaphore("xfl"))
        vst = e(nc.semaphore("vst"))
        sst = e(nc.semaphore("sst"))
        mrr = e(nc.semaphore("mrr"))
        vsp = e(nc.semaphore("vsp"))
        mhh = e(nc.semaphore("mhh"))
        ahe = e(nc.semaphore("ahe"))
        vhw = e(nc.semaphore("vhw"))
        mht = e(nc.semaphore("mht"))
        mdd = e(nc.semaphore("mdd"))
        mev = e(nc.semaphore("mev"))
        vst2 = e(nc.semaphore("vst2"))
        sst2 = e(nc.semaphore("sst2"))
        cmb = e(nc.semaphore("cmb"))
        cmb2 = e(nc.semaphore("cmb2"))
        xld = e(nc.semaphore("xld"))
        sxn = [e(nc.semaphore(f"sxn{i}")) for i in range(2)]
        sxo = e(nc.semaphore("sxo"))

        SPN = 24

        with nc.Block() as blk:
            @blk.gpsimd
            def _(g):
                g.dma_start(out=bb_b[0:1, :], in_=bbr[0:1, :]).then_inc(gc, 16)
                g.dma_start(out=br_b[0:1, :], in_=brr[0:1, :]).then_inc(gc, 16)
                g.dma_start(out=wr_b.ap().rearrange("p (c r) -> p c r", r=8),
                            in_=wrt.rearrange("(c p) r -> p c r", p=128)).then_inc(gc, 16)
                g.dma_start(out=la_b.ap().rearrange("p (c r) -> p c r", r=128),
                            in_=lat.rearrange("(c p) r -> p c r", p=128)).then_inc(gc, 16)
                g.dma_start(out=lb_b[:, :], in_=lbt[:, :]).then_inc(gc, 16)
                for oc in range(8):
                    if oc >= 1:
                        g.wait_ge(wcc, 16 * oc)
                    g.dma_start(out=wb_bf[:, oc * 512:(oc + 1) * 512],
                                in_=wbt[:, oc * 512:(oc + 1) * 512]).then_inc(wcc, 16)
                for gt in range(16):
                    g.wait_ge(cmb, gt + 1)
                    g.wait_ge(xld, 16 * (gt + 1))
                    if gt >= 1:
                        g.wait_ge(sxo, 16 * gt)
                    g.tensor_tensor(tmpc[:, :], xr[:, :], tmpc[:, :], ALU.add).then_inc(cmb2, 1)

            @blk.vector
            def _(v):
                v.memset(consts[:, 0:1], 1e-5)
                v.memset(consts[:, 1:2], -0.5)
                v.memset(consts[:, 2:3], -1.0)
                v.memset(consts[:, 3:4], 1.0)
                v.memset(ones8[:, :], 1.0).then_inc(vst, 1)
                for i in range(NTT):
                    v.wait_ge(ldx, 16 * (i + 1))
                    if i >= 1:
                        v.wait_ge(vst, 1 + 2 * i)
                    if i >= 2:
                        v.wait_ge(sst, 5 * (i - 2) + 3)
                    for c in range(8):
                        ii = v.bn_stats(bns[:, 6 * c:6 * c + 6], xt[0][:, 512 * c:512 * (c + 1)])
                    ii.then_inc(vst, 1)
                    v.wait_ge(vst, 1 + 2 * i + 1)
                    v.bn_aggr(mv[i % 2][:, :], bns[:, :]).then_inc(vst, 1)
                v.wait_ge(ldc, 48)
                nsp = 0
                for G in range(4):
                    for tt in range(4):
                        gt = G * 4 + tt
                        v.wait_ge(mrr, gt + 1)
                        v.tensor_copy(zt[:, :], pr[:, :]).then_inc(vsp, 1); nsp += 1
                        v.wait_ge(vsp, nsp)
                        v.tensor_copy(zs[:, :], zt[:, :]).then_inc(vsp, 1); nsp += 1
                        for stg in BSTAGES:
                            v.wait_ge(vsp, nsp)
                            zv = zs.ap()
                            ii = None
                            for (a, b2) in stg:
                                v.tensor_tensor(ztm[:, a:a + 1], zv[:, a:a + 1], zv[:, b2:b2 + 1], ALU.max)
                                ii = v.tensor_tensor(ztm[:, b2:b2 + 1], zv[:, a:a + 1], zv[:, b2:b2 + 1], ALU.min)
                            ii.then_inc(vsp, 1); nsp += 1
                            v.wait_ge(vsp, nsp)
                            for (a, b2) in stg:
                                v.tensor_copy(zs[:, a:a + 1], ztm[:, a:a + 1])
                                ii = v.tensor_copy(zs[:, b2:b2 + 1], ztm[:, b2:b2 + 1])
                            ii.then_inc(vsp, 1); nsp += 1
                        v.wait_ge(vsp, nsp)
                        v.tensor_tensor_scan(zc[:, :], zs[:, :], ones8[:, :],
                                             0.0, ALU.add, ALU.mult).then_inc(vsp, 1); nsp += 1
                        v.tensor_tensor(ztm[:, :], zs[:, :], kvec[:, :], ALU.mult).then_inc(vsp, 1); nsp += 1
                        v.wait_ge(vsp, nsp)
                        v.tensor_scalar(ztm[:, :], ztm[:, :], 1.0, None, ALU.add).then_inc(vsp, 1); nsp += 1
                        v.wait_ge(vsp, nsp)
                        v.tensor_tensor(sup[:, :], ztm[:, :], zc[:, :], ALU.is_gt).then_inc(vsp, 1); nsp += 1
                        v.wait_ge(vsp, nsp)
                        v.reduce_sum(scr[:, 0:1], sup[:, :], axis=AX.X).then_inc(vsp, 1); nsp += 1
                        v.tensor_tensor(ztm[:, :], zs[:, :], sup[:, :], ALU.mult).then_inc(vsp, 1); nsp += 1
                        v.wait_ge(vsp, nsp)
                        v.reduce_sum(scr[:, 1:2], ztm[:, :], axis=AX.X).then_inc(vsp, 1); nsp += 1
                        v.wait_ge(vsp, nsp)
                        v.reciprocal(scr[:, 2:3], scr[:, 0:1]).then_inc(vsp, 1); nsp += 1
                        v.wait_ge(vsp, nsp)
                        v.tensor_scalar(scr[:, 3:4], scr[:, 1:2], 1.0, scr[:, 2:3],
                                        ALU.subtract, ALU.mult).then_inc(vsp, 1); nsp += 1
                        v.wait_ge(vsp, nsp)
                        v.tensor_scalar(wsb[:, :], zt[:, :], scr[:, 3:4], 0.0,
                                        ALU.subtract, ALU.max).then_inc(vsp, 1); nsp += 1
                        v.wait_ge(ahe, gt + 1)
                        v.wait_ge(vsp, nsp)
                        for e8 in range(8):
                            ii = v.tensor_scalar(hw[gt % 2][:, e8 * 16:(e8 + 1) * 16],
                                                 hsb[gt % 2][:, e8 * 16:(e8 + 1) * 16],
                                                 wsb[:, e8:e8 + 1], None, ALU.mult)
                        ii.then_inc(vhw, 1)
                        v.wait_ge(mht, gt + 1)
                        v.tensor_copy(hwf[:, tt * 128:(tt + 1) * 128], pth[gt % 2][:, :]).then_inc(vhw, 1)
                    for tt in range(4):
                        gt = G * 4 + tt
                        v.wait_ge(mev, 32 * G + 28 + tt + 1)
                        if gt >= 2:
                            v.wait_ge(sst2, 5 * (gt - 2) + 3)
                        if gt >= 1:
                            v.wait_ge(vst2, 2 * gt)
                        for c in range(8):
                            ii = v.bn_stats(bns[:, 6 * c:6 * c + 6], ao[tt][:, 512 * c:512 * (c + 1)])
                        ii.then_inc(vst2, 1)
                        v.wait_ge(vst2, 2 * gt + 1)
                        v.bn_aggr(mv[gt % 2][:, :], bns[:, :]).then_inc(vst2, 1)
                        v.wait_ge(sst2, 5 * gt + 4)
                        if gt >= 1:
                            v.wait_ge(sxo, 16 * gt)
                        v.tensor_scalar(tmpc[:, :], ao[tt][:, :], mv[gt % 2][:, 0:1],
                                        sc[:, 1:2], ALU.subtract, ALU.mult).then_inc(cmb, 1)

            @blk.scalar
            def _(s):
                s.wait_ge(vst, 1)
                for i in range(NTT):
                    s.wait_ge(vst, 1 + 2 * (i + 1))
                    if i >= 1:
                        jj, kk = divmod(i - 1, 8)
                        s.wait_ge(sxn[jj], 16 * (kk + 1))
                    s.activation(sc[:, 0:1], mv[i % 2][:, 1:2], AF.Ln, bias=consts[:, 0:1]).then_inc(sst, 1)
                    s.wait_ge(sst, 5 * i + 1)
                    s.activation(sc[:, 1:2], sc[:, 0:1], AF.Exp, scale=consts[:, 1:2]).then_inc(sst, 1)
                    s.wait_ge(sst, 5 * i + 2)
                    s.activation(sc[:, 2:3], mv[i % 2][:, 0:1], AF.Identity, scale=sc[:, 1:2]).then_inc(sst, 1)
                    s.wait_ge(sst, 5 * i + 3)
                    s.activation(sc[:, 3:4], sc[:, 2:3], AF.Identity, scale=consts[:, 2:3]).then_inc(sst, 1)
                    s.wait_ge(sst, 5 * i + 4)
                    s.activation(xn_t[:, :], xt[0][:, :], AF.Identity,
                                 scale=sc[:, 1:2], bias=sc[:, 3:4]).then_inc(sst, 1)
                for G in range(4):
                    s.wait_ge(sxn[G // 2], 16 * ((4 * G + 3) % 8 + 1))
                    if G >= 1:
                        s.wait_ge(mdd, 32 * G)
                    for c in range(HC):
                        s.dma_start_transpose(xnf[:, c * 512:(c + 1) * 512],
                                              xn_hbm[G * 512:(G + 1) * 512,
                                                     c * 128:(c + 1) * 128]).then_inc(xfl, 16)
                    for tt in range(4):
                        gt = G * 4 + tt
                        s.wait_ge(mhh, gt + 1)
                        if gt >= 2:
                            s.wait_ge(vhw, 2 * (gt - 2) + 1)
                        s.activation(hsb[gt % 2][:, :], ph[gt % 2][:, :], AF.Identity).then_inc(ahe, 1)
                    for oc in range(8):
                        for tt in range(4):
                            U = 32 * G + 4 * oc + tt
                            s.wait_ge(mdd, U + 1)
                            if G >= 1 and oc == 0:
                                s.wait_ge(cmb, 4 * (G - 1) + tt + 1)
                            s.activation(ao[tt][:, oc * 512:(oc + 1) * 512],
                                         ps[U % 2][:, :], AF.Identity).then_inc(mev, 1)
                    for tt in range(4):
                        gt = G * 4 + tt
                        s.wait_ge(vst2, 2 * (gt + 1))
                        s.activation(sc[:, 0:1], mv[gt % 2][:, 1:2], AF.Ln, bias=consts[:, 0:1]).then_inc(sst2, 1)
                        s.wait_ge(sst2, 5 * gt + 1)
                        s.activation(sc[:, 1:2], sc[:, 0:1], AF.Exp, scale=consts[:, 1:2]).then_inc(sst2, 1)
                        s.wait_ge(sst2, 5 * gt + 2)
                        s.activation(sc[:, 2:3], sc[:, 1:2], AF.Identity).then_inc(sst2, 1)
                        s.wait_ge(sst2, 5 * gt + 3)
                        s.activation(sc[:, 2:3], sc[:, 2:3], AF.Identity).then_inc(sst2, 1)
                        s.wait_ge(sst2, 5 * gt + 4)
                        s.activation(sc[:, 2:3], sc[:, 2:3], AF.Identity).then_inc(sst2, 1)

            @blk.tensor
            def _(t):
                t.wait_ge(ldc, 48)
                t.wait_ge(gc, 80)
                for G in range(4):
                    t.wait_ge(xfl, 16 * HC * (G + 1))
                    for tt in range(4):
                        gt = G * 4 + tt
                        if gt >= 1:
                            t.wait_ge(vsp, SPN * gt - (SPN - 1))
                        t.matmul(pr[:, :], onesr[0:1, :], br_b[0:1, :], start=True, stop=False)
                        for c in range(HC):
                            t.matmul(pr[:, :], xnf[:, c * 512 + tt * 128: c * 512 + tt * 128 + 128],
                                     wr_b[:, c * 8:(c + 1) * 8],
                                     start=False, stop=(c == HC - 1)).then_inc(mrr, 0 if c < HC - 1 else 1)
                        if gt >= 2:
                            t.wait_ge(ahe, gt - 1)
                        for c in range(HC):
                            t.matmul(ph[gt % 2][:, :], xnf[:, c * 512 + tt * 128: c * 512 + tt * 128 + 128],
                                     la_b[:, c * 128:(c + 1) * 128],
                                     start=(c == 0), stop=(c == HC - 1)).then_inc(mhh, 0 if c < HC - 1 else 1)
                        t.wait_ge(vhw, 2 * gt + 1)
                        if gt >= 2:
                            t.wait_ge(vhw, 2 * (gt - 2) + 2)
                        t.transpose(pth[gt % 2][:, :], hw[gt % 2][:, :], ident[:, :]).then_inc(mht, 1)
                    t.wait_ge(vhw, 2 * (4 * G + 4))
                    for oc in range(8):
                        t.wait_ge(wbl, 16 * (8 * G + oc + 1))
                        for tt in range(4):
                            U = 32 * G + 4 * oc + tt
                            bank = U % 2
                            if U >= 2:
                                t.wait_ge(mev, U - 1)
                            for c in range(HC):
                                t.matmul(ps[bank][:, :],
                                         xnf[:, c * 512 + tt * 128: c * 512 + tt * 128 + 128],
                                         wsl[:, c * 512:(c + 1) * 512],
                                         start=(c == 0), stop=False)
                            t.matmul(ps[bank][:, :], hwf[:, tt * 128:(tt + 1) * 128],
                                     lb_b[:, oc * 512:(oc + 1) * 512], start=False, stop=False)
                            t.matmul(ps[bank][:, :], onesr[0:1, :],
                                     bb_b[0:1, oc * 512:(oc + 1) * 512],
                                     start=False, stop=True).then_inc(mdd, 1)

            @blk.sync
            def _(s2):
                s2.dma_start(out=onesr[0:1, :], in_=ones_h.ap()[0:1, :]).then_inc(ldc, 16)
                s2.dma_start(out=ident[:, :], in_=ident_h.ap()[:, :])\
                  .then_inc(ldc, 16)
                s2.dma_start(out=kvec[:, :], in_=kv_h.ap()[:, :]).then_inc(ldc, 16)
                for i in range(NTT):
                    if i >= 1:
                        s2.wait_ge(sst, 5 * i)
                    s2.dma_start(out=xt[0][:, :],
                                 in_=x3_in[i * 128:(i + 1) * 128, :]).then_inc(ldx, 16)
                    if i >= 1:
                        s2.dma_start(out=xn_hbm[(i - 1) * 128:i * 128, :],
                                     in_=xn_t[:, :]).then_inc(sxn[(i - 1) // 8], 16)
                s2.wait_ge(sst, 5 * NTT)
                s2.dma_start(out=xn_hbm[(NTT - 1) * 128:NTT * 128, :],
                             in_=xn_t[:, :]).then_inc(sxn[1], 16)
                for G in range(4):
                    for oc in range(8):
                        G8 = G * 8 + oc
                        if G8 >= 1:
                            s2.wait_ge(mdd, 4 * G8)
                        if G == 0:
                            s2.wait_ge(wcc, 16 * (oc + 1))
                        s2.dma_start(out=wsl.ap().rearrange("p (c o) -> p c o", o=512),
                                     in_=wb_bf[:, oc * 512:(oc + 1) * 512]
                                     .rearrange("(c p) o -> p c o", p=128)).then_inc(wbl, 16)
                    for tt in range(4):
                        gt = G * 4 + tt
                        if gt >= 1:
                            s2.wait_ge(cmb2, gt)
                        s2.dma_start(out=xr[:, :],
                                     in_=x3_in[gt * 128:(gt + 1) * 128, :]).then_inc(xld, 16)
                        s2.wait_ge(cmb2, gt + 1)
                        s2.dma_start(out=xo_out[gt * 128:(gt + 1) * 128, :],
                                     in_=tmpc[:, :]).then_inc(sxo, 16)
                s2.wait_ge(sxo, 16 * 16)
    return nc


def irfft_basis(n=4096):
    t = np.arange(n)
    M = np.empty((n, n), dtype=np.float64)
    M[:, 0] = 1.0 / np.sqrt(n)
    f = np.arange(1, n // 2)
    ang = 2.0 * np.pi * np.outer(t, f) / n
    M[:, 1:n - 1:2] = np.sqrt(2.0 / n) * np.cos(ang)
    M[:, 2:n - 1:2] = np.sqrt(2.0 / n) * np.sin(ang)
    M[:, n - 1] = (-1.0) ** t / np.sqrt(n)
    return M

def l0_host_gather(h, k_all, v_all):
    kg = np.zeros((NS_TOT, 8, 128, 128), dtype=BF)
    vg = np.zeros((NS_TOT, 8, 128, 128), dtype=BF)
    biasc = np.full((128, NS_TOT), -1e30, dtype=np.float32)
    vcol = np.zeros((128, 16), dtype=np.float32)
    for g in range(4):
        plan = L0_PLANS[(h, g)]
        for s, sl in enumerate(plan):
            gs = NS_OFF[g] + s
            j = sl["j"]
            if not sl.get("dead"):
                kt = k_all[j * 128:(j + 1) * 128, :]
                kg[gs] = np.ascontiguousarray(
                    kt.reshape(128, 8, 128).transpose(1, 2, 0))
                vg[gs] = np.ascontiguousarray(
                    v_all[j * 128:(j + 1) * 128, :].reshape(128, 8, 128).transpose(1, 0, 2))
            biasc[0:64, gs] = 0.0 if sl["w0"] else -1e30
            biasc[64:128, gs] = 0.0 if sl["w1"] else -1e30
            if 4 <= s < 8:
                vcol[0:64, g * 4 + s - 4] = sl["v0"]
                vcol[64:128, g * 4 + s - 4] = sl["v1"]
    em = np.zeros((128, 16 * 512), dtype=BF)
    tkh = (np.arange(128)[:, None] >= 64).astype(np.int64)
    tqb = np.arange(512)[None, :] // 64
    for gi in range(16):
        cmk = (tqb < 2 * (gi % 4) + tkh).astype(np.float32)
        em[:, gi * 512:(gi + 1) * 512] = np.maximum(cmk, vcol[:, gi:gi + 1]).astype(BF)
    return kg, vg, np.ascontiguousarray(biasc), em

def _run(nc, in_maps):
    res = run_bass_kernel_spmd(nc, in_maps, core_ids=list(range(NCORES)))
    return res.results

def kernel(**inputs):
    x_in = inputs["L0_attn_hidden_states"]
    C = np.ascontiguousarray

    wt = C(inputs["L0_attn_w_qkv"].T)
    brow = C(inputs["L0_attn_b_qkv"][None, :])
    maps = []
    for c in range(NCORES):
        b, h = divmod(c, 2)
        pos = h * T + np.arange(T)
        cosv, sinv = rope_tm_tables(1e6, pos)
        maps.append(dict(x=C(x_in[b, h * T:(h + 1) * T]), wt=wt, brow=brow,
                         cos=cosv, sin=sinv))
    r1 = _run(build_l0_qkv(), maps)

    wdt = C(inputs["L0_attn_w_dense"].T)
    bdr = C(inputs["L0_attn_b_dense"][None, :])
    maps = []
    for c in range(NCORES):
        b, h = divmod(c, 2)
        k_all = np.concatenate([r1[2 * b]["kr"], r1[2 * b + 1]["kr"]], 0)
        v_all = np.concatenate([r1[2 * b]["vr"], r1[2 * b + 1]["vr"]], 0)
        kg, vg, biasc, em = l0_host_gather(h, k_all, v_all)
        maps.append(dict(qf=r1[c]["qr"], kg=kg, vg=vg, biasc=biasc, emaski=em,
                         x=C(x_in[b, h * T:(h + 1) * T]), wdt=wdt, bdr=bdr))
    r2 = _run(build_l0_attn(), maps)

    MT = C(irfft_basis().T).astype(np.float32)
    wbt = C(inputs["L0_ffn_W_base"].T)
    bbr = C(inputs["L0_ffn_b_base"][None, :])
    encc = C(np.concatenate([inputs["L0_ffn_enc0"].T, inputs["L0_ffn_enc1"].T], 1))
    c0p = C(inputs["L0_ffn_coefs0"].transpose(1, 0, 2).reshape(D, 96))
    m0p = C(inputs["L0_ffn_mask0"].transpose(1, 0, 2).reshape(D, 96)).view(np.uint8)
    c1p = C(inputs["L0_ffn_coefs1"].transpose(1, 0, 2).reshape(D, 24))
    m1p = C(inputs["L0_ffn_mask1"].transpose(1, 0, 2).reshape(D, 24)).view(np.uint8)
    maps = [dict(x1=r2[c]["x1"], wbt=wbt, bbr=bbr, encc=encc,
                 c0p=c0p, m0p=m0p, c1p=c1p, m1p=m1p, mtt=MT)
            for c in range(NCORES)]
    r3 = _run(build_l0_ffn(), maps)

    wcat = C(np.concatenate([inputs["L1_attn_Wq"], inputs["L1_attn_Wk"],
                             inputs["L1_attn_Wv"]], 1))
    maps = []
    for c in range(NCORES):
        b, h = divmod(c, 2)
        pos = h * T + np.arange(T)
        cosv, sinv = rope_tm_tables(1e4, pos)
        maps.append(dict(x2=r3[c]["x2"], wcat=wcat, cos=cosv, sin=sinv))
    r4 = _run(build_l1_qkv(), maps)

    maps = []
    for c in range(NCORES):
        b, p = divmod(c, 2)
        qs_f = np.concatenate([r4[2 * b]["qs"], r4[2 * b + 1]["qs"]], 0)
        ks_f = np.concatenate([r4[2 * b]["ks"], r4[2 * b + 1]["ks"]], 0)
        gs_f = np.concatenate([r4[2 * b]["gs"], r4[2 * b + 1]["gs"]], 0)
        vv_f = np.concatenate([r4[2 * b]["vv"], r4[2 * b + 1]["vv"]], 0)
        maps.append(dict(qs4=C(qs_f[:, p * 2048:(p + 1) * 2048]),
                         ks4=C(ks_f[:, p * 512:(p + 1) * 512]),
                         gs4=C(gs_f[:, p * 512:(p + 1) * 512]),
                         vv4=C(vv_f[:, p * 512:(p + 1) * 512])))
    r5 = _run(build_l1_gla(), maps)

    wot = inputs["L1_attn_Wo"]
    maps = []
    for c in range(NCORES):
        b, h = divmod(c, 2)
        sk_f = np.concatenate([r4[2 * b]["sk"], r4[2 * b + 1]["sk"]], 0)
        vv_f = np.concatenate([r4[2 * b]["vv"], r4[2 * b + 1]["vv"]], 0)
        ol_f = np.concatenate([r5[2 * b]["ol"], r5[2 * b + 1]["ol"]], 0)
        ske = np.zeros((2176, 1024), dtype=BF)
        vve = np.zeros((2176, 1024), dtype=BF)
        lo = h * T
        ske[128:, :] = sk_f[lo:lo + T]
        vve[128:, :] = vv_f[lo:lo + T]
        bnd = np.full((128, 1), -1e30, dtype=np.float32)
        if h == 1:
            ske[64:128, :] = sk_f[T - 64:T]
            vve[64:128, :] = vv_f[T - 64:T]
            bnd[64:128] = 0.0
        maps.append(dict(sq=r4[c]["sq"], ske=ske, vve=vve,
                         oli=C(ol_f[:, :, lo:lo + T]), bndc=bnd,
                         x2=r3[c]["x2"], wot=wot))
    r6 = _run(build_l1_swa(), maps)

    wb1t = C(inputs["L1_ffn_W_base"].T)
    bb1r = C(inputs["L1_ffn_b_base"][None, :])
    wrt = C(inputs["L1_ffn_W_router"].T)
    brr = C(inputs["L1_ffn_b_router"][None, :])
    lat = C(inputs["L1_ffn_lora_A"].transpose(1, 0, 2).reshape(D, 128))
    lbt = C(inputs["L1_ffn_lora_B"].reshape(128, D))
    maps = [dict(x3=r6[c]["x3"], wb1t=wb1t, bb1r=bb1r, wrt=wrt, brr=brr,
                 lat=lat, lbt=lbt) for c in range(NCORES)]
    r7 = _run(build_l1_ffn(), maps)

    out = np.empty((B, S, D), dtype=np.float32)
    for c in range(NCORES):
        b, h = divmod(c, 2)
        out[b, h * T:(h + 1) * T] = r7[c]["xo"]
    return out
